# version 05
# baseline (speedup 1.0000x reference)
import numpy as np
import ml_dtypes
from contextlib import ExitStack
import concourse.bass as bass
import concourse.mybir as mybir
from concourse.bass_utils import run_bass_kernel_spmd

F32 = mybir.dt.float32
BF16 = mybir.dt.bfloat16
U8 = mybir.dt.uint8
AF = mybir.ActivationFunctionType
ALU = mybir.AluOpType
AX = mybir.AxisListType
BF = ml_dtypes.bfloat16

B, T, D = 2, 8192, 1024
NCORE, GRP, TOK, NSLOT, NBLK = 8, 4, 2048, 16, 64


def tiles_of(c):
    out = []
    for m in range(8):
        out += [8 * m + c, 8 * m + 7 - c]
    return out


def tok_idx_of(c):
    t = np.asarray(tiles_of(c))
    return (t[:, None] * 128 + np.arange(128)[None, :]).reshape(-1)


class Cx:
    def __init__(self, name):
        self.nc = bass.Bass("TRN2", target_bir_lowering=False)
        self.es = ExitStack()
        self.sems = {}
        self.cnt = {}
        self.name = name

    def sem(self, name):
        if name not in self.sems:
            self.sems[name] = self.es.enter_context(self.nc.semaphore(name))
            self.cnt[name] = 0
        return self.sems[name]

    def inc(self, instr, name, n=1):
        instr.then_inc(self.sem(name), n)
        self.cnt[name] += n
        return self.cnt[name]

    def w(self, eng, name, val=None):
        eng.wait_ge(self.sem(name), self.cnt[name] if val is None else val)

    def sb(self, name, shape, dt):
        return self.es.enter_context(self.nc.sbuf_tensor(name, shape, dt))

    def ps(self, name, shape, dt=F32):
        return self.es.enter_context(self.nc.psum_tensor(name, shape, dt))

    def din(self, name, shape, dt):
        return self.nc.dram_tensor(name, shape, dt, kind="ExternalInput").ap()

    def dout(self, name, shape, dt):
        return self.nc.dram_tensor(name, shape, dt, kind="ExternalOutput").ap()

    def block(self):
        return self.es.enter_context(self.nc.Block())

    def close(self):
        self.es.close()


def mm_acc(cx, t, tile_ap, pairs, pe):
    n = len(pairs)
    for i, (lw, rr) in enumerate(pairs):
        ins = t.matmul(tile_ap, lw, rr, start=(i == 0), stop=(i == n - 1))
        if i == n - 1:
            cx.inc(ins, pe, 1)


def fproj(cx, block, name, jobs, pst, wait_tensor=None, evac_eng="vector"):
    pe, dv = name + "p", name + "v"

    @block.tensor
    def _(t):
        if wait_tensor:
            wait_tensor(t)
        for idx, (pairs, width, _) in enumerate(jobs):
            if idx >= 2:
                cx.w(t, dv, idx - 1)
            mm_acc(cx, t, pst[idx % 2][:, 0:width], pairs, pe)

    body = getattr(block, evac_eng)

    def _evac(eng):
        for idx, (_, width, evac_fn) in enumerate(jobs):
            cx.w(eng, pe, idx + 1)
            last = evac_fn(eng, pst[idx % 2][:, 0:width])
            cx.inc(last, dv, 1)
    body(_evac)
    return len(jobs)


def wchunked(sb_ap, ic, colw, off, width):
    return sb_ap[:, ic * colw + off: ic * colw + off + width]


def cp_to_gen(cx, dst, off, w):
    def f(eng, tile):
        return eng.tensor_copy(dst[:, off: off + w], tile)
    return f


def rope_views(t_ap, nh, par):
    return t_ap[:, 0:nh * 64].rearrange("p (h d two) -> p h d two",
                                        h=nh, two=2)[:, :, :, par]


def emit_rope(v, out_t, in_t, cos_ap, sin_ap, nh, rA, rB):
    xe = rope_views(in_t, nh, 0)
    xo = rope_views(in_t, nh, 1)
    cosv = cos_ap.rearrange("p (h d) -> p h d", d=32)
    sinv = sin_ap.rearrange("p (h d) -> p h d", d=32)
    rAv = rA[:, 0:nh * 32].rearrange("p (h d) -> p h d", d=32)
    rBv = rB[:, 0:nh * 32].rearrange("p (h d) -> p h d", d=32)
    v.tensor_mul(rAv, xe, cosv)
    v.tensor_mul(rBv, xo, sinv)
    v.tensor_tensor(rope_views(out_t, nh, 0), rAv, rBv, ALU.subtract)
    v.tensor_mul(rAv, xe, sinv)
    v.tensor_mul(rBv, xo, cosv)
    v.tensor_tensor(rope_views(out_t, nh, 1), rAv, rBv, ALU.add)


def build_A(upto="full"):
    cx = Cx("A")
    x0 = cx.din("x0", [TOK, D], F32)
    wdqT = cx.din("wdqT", [1024, 512], F32)
    wuqT = cx.din("wuqT", [512, 1024], F32)
    wdkvT = cx.din("wdkvT", [1024, 512], F32)
    wdq = cx.din("wdq", [512, 1024], F32)
    wuk = cx.din("wuk", [1024, 512], F32)
    wuv = cx.din("wuv", [1024, 512], F32)
    woT = cx.din("woT", [1024, 1024], F32)
    ident = cx.din("ident", [128, 128], BF16)
    epsA = cx.din("epsA", [128, 1], F32)
    o_xn0f = cx.dout("xn0f", [1024, TOK], BF16)
    o_q0f = cx.dout("q0f", [1024, TOK], BF16)
    o_K0f = cx.dout("K0f", [1024, TOK], BF16)
    o_V0t = cx.dout("V0t", [TOK, 1024], BF16)

    xn0f = cx.sb("xn0f_s", [128, 8 * TOK], BF16)
    idsb = cx.sb("idsb", [128, 128], BF16)
    eps5 = cx.sb("eps5", [128, 1], F32)
    wuqT_sb = cx.sb("wuqT_sb", [128, 4 * 1024], BF16)
    wdq_sb = cx.sb("wdq_sb", [128, 4 * 1024], BF16)
    wuk_sb = cx.sb("wuk_sb", [128, 8 * 512], BF16)
    wuv_sb = cx.sb("wuv_sb", [128, 8 * 512], BF16)
    woT_sb = cx.sb("woT_sb", [128, 8 * 1024], BF16)
    wdqT_sb = cx.sb("wdqT_sb", [128, 8 * 512], BF16)
    wdkvT_sb = cx.sb("wdkvT_sb", [128, 8 * 512], BF16)
    T1 = cx.sb("T1", [128, 8 * 1024], BF16)
    MT = cx.sb("MT", [128, 4 * 1024], BF16)
    Nn = cx.sb("Nn", [128, 4 * 1024], BF16)
    qdf = cx.sb("qdf", [128, 4 * TOK], BF16)
    ckvf = cx.sb("ckvf", [128, 4 * TOK], BF16)
    stgQ = cx.sb("stgQ", [128, 2 * 512], BF16)
    stgK = cx.sb("stgK", [128, 2 * 512], BF16)
    xt = cx.sb("xt", [128, 2 * 1024], F32)
    sqf = cx.sb("sqf", [128, 1024], F32)
    lnS1 = cx.sb("lnS1", [128, 64], F32)
    lnS2 = cx.sb("lnS2", [128, 64], F32)
    lnM = cx.sb("lnM", [128, 64], F32)
    lnV = cx.sb("lnV", [128, 64], F32)
    lnR = cx.sb("lnR", [128, 64], F32)
    lnSh = cx.sb("lnSh", [128, 64], F32)
    xn_tm = cx.sb("xn_tm", [128, 1024], BF16)
    stgV = cx.sb("stgV", [128, 2 * 1024], BF16)

    psA = cx.ps("psA", [128, 512])
    psB = cx.ps("psB", [128, 512])
    ptrA = cx.ps("ptrA", [128, 1024], BF16)
    ptrB = cx.ps("ptrB", [128, 1024], BF16)
    ptrs = (ptrA, ptrB)

    block = cx.block()

    @block.gpsimd
    def _(g):
        for dst, src, w in ((wuqT_sb, wuqT, 1024), (wdq_sb, wdq, 1024),
                            (wuk_sb, wuk, 512), (wuv_sb, wuv, 512),
                            (woT_sb, woT, 1024), (wdqT_sb, wdqT, 512),
                            (wdkvT_sb, wdkvT, 512)):
            cx.inc(g.dma_start(out=dst[:].rearrange("p (i c) -> p i c", c=w),
                               in_=src.rearrange("(i p) c -> p i c", p=128)),
                   "ldw", 16)

    @block.sync
    def _(s):
        cx.inc(s.dma_start(out=idsb[:], in_=ident), "ldi", 16)
        cx.inc(s.dma_start(out=eps5[:], in_=epsA), "ini", 16)


    for j in range(NSLOT):
        @block.sync
        def _(s, j=j):
            if j >= 2:
                cx.w(s, "lnd", 2 * (j - 1))
            cx.inc(s.dma_start(out=xt[:, (j % 2) * 1024:(j % 2) * 1024 + 1024],
                               in_=x0[j * 128:(j + 1) * 128, :]), "ldx", 16)

        @block.vector
        def _(v, j=j):
            cx.w(v, "ldx", (j + 1) * 16)
            xv = xt[:, (j % 2) * 1024:(j % 2) * 1024 + 1024]
            cx.inc(v.reduce_sum(lnS1[:, 4 * j:4 * j + 1], xv, AX.X), "lnd", 1)
            v.tensor_mul(sqf[:], xv, xv)
            cx.w(v, "lnd", cx.cnt["lnd"])
            cx.inc(v.reduce_sum(lnS2[:, 4 * j:4 * j + 1], sqf[:], AX.X), "lnd", 1)

    @block.vector
    def _(v):
        cx.w(v, "lnd", 2 * NSLOT)
        s1v = lnS1[:].rearrange("p (j f) -> p j f", f=4)[:, :, 0]
        s2v = lnS2[:].rearrange("p (j f) -> p j f", f=4)[:, :, 0]
        mv = lnM[:].rearrange("p (j f) -> p j f", f=4)[:, :, 0]
        vv = lnV[:].rearrange("p (j f) -> p j f", f=4)[:, :, 0]
        sv = lnSh[:].rearrange("p (j f) -> p j f", f=4)[:, :, 0]
        cx.inc(v.tensor_scalar_mul(mv, s1v, 1.0 / 1024), "cmb", 1)
        cx.w(v, "cmb", cx.cnt["cmb"])
        cx.inc(v.tensor_scalar_mul(vv, s2v, 1.0 / 1024), "cmb", 1)
        cx.w(v, "cmb", cx.cnt["cmb"])
        cx.inc(v.tensor_mul(sv, mv, mv), "cmb", 1)
        cx.w(v, "cmb", cx.cnt["cmb"])
        cx.inc(v.tensor_sub(vv, vv, sv), "cmb", 1)
        cx.w(v, "cmb", cx.cnt["cmb"])
        cx.inc(v.tensor_copy(sv, vv), "cmb", 1)

    @block.scalar
    def _(s):
        cx.w(s, "cmb", cx.cnt["cmb"])
        cx.w(s, "ini", 16)
        vv = lnV[:].rearrange("p (j f) -> p j f", f=4)[:, :, 0]
        rv_ = lnR[:].rearrange("p (j f) -> p j f", f=4)[:, :, 0]
        r2v = lnR[:].rearrange("p (j f) -> p j f", f=4)[:, :, 1]
        sv = lnSh[:].rearrange("p (j f) -> p j f", f=4)[:, :, 1]
        cx.inc(s.activation(out=r2v, in_=vv, func=AF.Ln, bias=eps5[:, 0:1]),
               "lna", 1)
        cx.w(s, "lna", cx.cnt["lna"])
        cx.inc(s.activation(out=rv_, in_=r2v, func=AF.Exp, scale=-0.5), "lna", 1)
        cx.w(s, "lna", cx.cnt["lna"])
        cx.inc(s.activation(out=sv, in_=rv_, func=AF.Copy), "lna", 1)

    for j in range(NSLOT):
        @block.sync
        def _(s, j=j):
            if j < 2:
                cx.w(s, "lnd", 2 * NSLOT)
            else:
                cx.w(s, "lne", 8 * (j - 1))
            cx.inc(s.dma_start(out=xt[:, (j % 2) * 1024:(j % 2) * 1024 + 1024],
                               in_=x0[j * 128:(j + 1) * 128, :]), "ldy", 16)

        @block.vector
        def _(v, j=j):
            cx.w(v, "ldy", (j + 1) * 16)
            if j == 0:
                cx.w(v, "lna", cx.cnt["lna"])
            if j >= 1:
                cx.w(v, "lnt", 8 * j)
            xv = xt[:, (j % 2) * 1024:(j % 2) * 1024 + 1024]
            cx.inc(v.tensor_scalar(xn_tm[:], xv, lnM[:, 4 * j:4 * j + 1],
                                   lnR[:, 4 * j:4 * j + 1],
                                   ALU.subtract, ALU.mult), "lnm", 1)

        @block.tensor
        def _(t, j=j):
            if j == 0:
                cx.w(t, "ldi", 16)
            cx.w(t, "lnm", cx.cnt["lnm"])
            for ci in range(8):
                gi = j * 8 + ci
                if gi >= 2:
                    cx.w(t, "lne", gi - 1)
                cx.inc(t.transpose(ptrs[ci % 2][:, 0:128],
                                   xn_tm[:, ci * 128:(ci + 1) * 128], idsb[:]),
                       "lnt", 1)

        @block.vector
        def _(v, j=j):
            for ci in range(8):
                cx.w(v, "lnt", j * 8 + ci + 1)
                cx.inc(v.tensor_copy(
                    xn0f[:, ci * TOK + j * 128: ci * TOK + (j + 1) * 128],
                    ptrs[ci % 2][:, 0:128]), "lne", 1)

    def cp_to(dst, coloff, w):
        def f(eng, tile):
            return eng.tensor_copy(dst[:, coloff: coloff + w], tile)
        return f

    jobs = []
    for oc in range(8):
        for half in range(2):
            pairs = [(wchunked(wuqT_sb, ic, 1024, oc * 128, 128),
                      wchunked(wdq_sb, ic, 1024, half * 512, 512))
                     for ic in range(4)]
            jobs.append((pairs, 512, cp_to(T1, oc * 1024 + half * 512, 512)))
    fproj(cx, block, "t1", jobs, (psA, psB),
          wait_tensor=lambda t: cx.w(t, "ldw"))

    nT1 = cx.cnt["t1p"]
    jobs = []
    for oc in range(4):
        for half in range(2):
            pairs = [(wchunked(wuk_sb, ic, 512, oc * 128, 128),
                      wchunked(T1, ic, 1024, half * 512, 512))
                     for ic in range(8)]
            jobs.append((pairs, 512, cp_to(MT, oc * 1024 + half * 512, 512)))
    fproj(cx, block, "mt", jobs, (psA, psB),
          wait_tensor=lambda t: cx.w(t, "t1v", nT1))

    jobs = []
    for oc in range(4):
        for half in range(2):
            pairs = [(wchunked(wuv_sb, ic, 512, oc * 128, 128),
                      wchunked(woT_sb, ic, 1024, half * 512, 512))
                     for ic in range(8)]
            jobs.append((pairs, 512, cp_to(Nn, oc * 1024 + half * 512, 512)))
    nMT0 = cx.cnt["mtv"]
    fproj(cx, block, "nn", jobs, (psA, psB),
          wait_tensor=lambda t: cx.w(t, "mtv", nMT0))

    if upto == "chains":
        @block.sync
        def _(s):
            cx.w(s, "lne", NSLOT * 8)
            cx.inc(s.dma_start(out=o_xn0f.rearrange("(i p) t -> p i t", p=128),
                               in_=xn0f[:].rearrange("p (i t) -> p i t", t=TOK)),
                   "sto", 16)
            cx.w(s, "nnv", cx.cnt["nnv"])
            cx.w(s, "sto")
        cx.close()
        return cx

    jobs = []
    for oc in range(4):
        for tc in range(4):
            pairs = [(wchunked(wdqT_sb, ic, 512, oc * 128, 128),
                      wchunked(xn0f, ic, TOK, tc * 512, 512))
                     for ic in range(8)]
            jobs.append((pairs, 512, cp_to(qdf, oc * TOK + tc * 512, 512)))
    nNN0 = cx.cnt["nnv"]
    fproj(cx, block, "qd", jobs, (psA, psB),
          wait_tensor=lambda t: (cx.w(t, "lne", NSLOT * 8), cx.w(t, "nnv", nNN0)))

    nQD = cx.cnt["qdv"]
    jobs = []
    for idx0, (oc, tc) in enumerate([(oc, tc) for oc in range(8) for tc in range(4)]):
        pairs = [(wchunked(wuqT_sb, ic, 1024, oc * 128, 128),
                  wchunked(qdf, ic, TOK, tc * 512, 512))
                 for ic in range(4)]
        def evQ(eng, tile, idx0=idx0):
            if idx0 >= 2:
                cx.w(eng, "stq", 16 * (idx0 - 1))
            return eng.tensor_copy(
                stgQ[:, (idx0 % 2) * 512:(idx0 % 2) * 512 + 512], tile)
        jobs.append((pairs, 512, evQ))
    fproj(cx, block, "q0", jobs, (psA, psB),
          wait_tensor=lambda t: cx.w(t, "qdv", nQD))

    @block.sync
    def _(s):
        for idx0, (oc, tc) in enumerate([(oc, tc) for oc in range(8) for tc in range(4)]):
            cx.w(s, "q0v", idx0 + 1)
            cx.inc(s.dma_start(
                out=o_q0f[oc * 128:(oc + 1) * 128, tc * 512:(tc + 1) * 512],
                in_=stgQ[:, (idx0 % 2) * 512:(idx0 % 2) * 512 + 512]), "stq", 16)

    jobs = []
    for oc in range(4):
        for tc in range(4):
            pairs = [(wchunked(wdkvT_sb, ic, 512, oc * 128, 128),
                      wchunked(xn0f, ic, TOK, tc * 512, 512))
                     for ic in range(8)]
            jobs.append((pairs, 512, cp_to(ckvf, oc * TOK + tc * 512, 512)))
    nQ00 = cx.cnt["q0v"]
    fproj(cx, block, "ck", jobs, (psA, psB),
          wait_tensor=lambda t: cx.w(t, "q0v", nQ00))

    nCK = cx.cnt["ckv"]
    nMT = cx.cnt["mtv"]
    jobs = []
    for idx0, (oc, tc) in enumerate([(oc, tc) for oc in range(8) for tc in range(4)]):
        pairs = [(wchunked(MT, ic, 1024, oc * 128, 128),
                  wchunked(ckvf, ic, TOK, tc * 512, 512))
                 for ic in range(4)]
        def evK(eng, tile, idx0=idx0):
            if idx0 >= 2:
                cx.w(eng, "stk", 16 * (idx0 - 1))
            return eng.tensor_copy(
                stgK[:, (idx0 % 2) * 512:(idx0 % 2) * 512 + 512], tile)
        jobs.append((pairs, 512, evK))
    fproj(cx, block, "k0", jobs, (psA, psB),
          wait_tensor=lambda t: (cx.w(t, "ckv", nCK), cx.w(t, "mtv", nMT)))

    @block.sync
    def _(s):
        for idx0, (oc, tc) in enumerate([(oc, tc) for oc in range(8) for tc in range(4)]):
            cx.w(s, "k0v", idx0 + 1)
            cx.inc(s.dma_start(
                out=o_K0f[oc * 128:(oc + 1) * 128, tc * 512:(tc + 1) * 512],
                in_=stgK[:, (idx0 % 2) * 512:(idx0 % 2) * 512 + 512]), "stk", 16)

    jobs = []
    for j in range(NSLOT):
        for oc in range(2):
            pairs = [(ckvf[:, lc * TOK + j * 128: lc * TOK + (j + 1) * 128],
                      wchunked(Nn, lc, 1024, oc * 512, 512))
                     for lc in range(4)]
            def evV(eng, tile, j=j, oc=oc):
                if j >= 2:
                    cx.w(eng, "stv", 16 * (j - 1))
                return eng.tensor_copy(
                    stgV[:, (j % 2) * 1024 + oc * 512:
                         (j % 2) * 1024 + oc * 512 + 512], tile)
            jobs.append((pairs, 512, evV))
    nK00 = cx.cnt["k0v"]
    fproj(cx, block, "v0", jobs, (psA, psB),
          wait_tensor=lambda t: cx.w(t, "k0v", nK00))

    @block.sync
    def _(s):
        for j in range(NSLOT):
            cx.w(s, "v0v", (j + 1) * 2)
            cx.inc(s.dma_start(out=o_V0t[j * 128:(j + 1) * 128, :],
                               in_=stgV[:, (j % 2) * 1024:(j % 2) * 1024 + 1024]),
                   "stv", 16)

    @block.sync
    def _(s):
        cx.w(s, "lne", NSLOT * 8)
        cx.inc(s.dma_start(out=o_xn0f.rearrange("(i p) t -> p i t", p=128),
                           in_=xn0f[:].rearrange("p (i t) -> p i t", t=TOK)),
               "sto", 16)
        cx.w(s, "sto")
        cx.w(s, "stv")
        cx.w(s, "stq")
        cx.w(s, "stk")

    cx.close()
    return cx



def emit_attention(cx, block, tag, *, Ksb, Vsb, qsb, msksb, idf32,
                   sb4, yb, ysb, ytok, rlt, kcol_fn, vcol_fn, y_store_fn,
                   wait_pass_fn, exp_scale=0.125, NH=4):
    S_, E_, M_, P_, T_, D_, Y_ = (tag + x for x in ("s", "e", "m", "p", "t", "d", "y"))
    PT = cx.PT

    for m in range(8):
        nkb = 8 * m + 8
        q0 = 256 * m
        sbase = cx.cnt.get(S_, 0)
        ebase = cx.cnt.get(E_, 0)
        pbase = cx.cnt.get(P_, 0)
        mbase = cx.cnt.get(M_, 0)
        ybase = cx.cnt.get(Y_, 0)
        tbase = cx.cnt.get(T_, 0)
        dbase = cx.cnt.get(D_, 0)
        stybase = cx.cnt.get("sty", 0)

        def iexp(k, hl):
            return ebase + k * 2 + hl // 2 + 1

        def imsk(k, hl):
            return mbase + (k - (nkb - 8)) * NH + hl + 1

        @block.tensor
        def _(t, m=m, nkb=nkb, q0=q0):
            if m == 0:
                wait_pass_fn(t)
            else:
                cx.w(t, Y_, ybase)
                cx.w(t, D_, dbase)
            for k in range(nkb):
                par = k % 2
                for hl in range(NH):
                    i = k * NH + hl
                    j = k * 2 + hl // 2
                    if hl % 2 == 0 and j >= 4:
                        cx.w(t, E_, ebase + j - 4 + 1)
                    ins = t.matmul(sb4[(ebase + j) % 4][:, (hl % 2) * 256:
                                                        (hl % 2) * 256 + 256],
                                   Ksb[0:64, kcol_fn(k, hl): kcol_fn(k, hl) + 128],
                                   qsb[0:64, hl * TOK + q0: hl * TOK + q0 + 256],
                                   start=(hl % 2 == 0), stop=(hl % 2 == 1))
                    cx.inc(ins, S_, 1)
                boundary = k >= nkb - 8
                upper = k >= nkb - 4
                for hl in range(NH):
                    bank = yb[hl // 2]
                    cb = (hl % 2) * 256
                    if boundary:
                        cx.w(t, M_, imsk(k, hl))
                    else:
                        cx.w(t, E_, iexp(k, hl))
                    first = (k == 0 and (hl % 2) == 0)
                    last = (k == nkb - 1 and (hl % 2) == 1)
                    if upper:
                        ins = t.matmul(bank[0:65, cb + 128: cb + 256],
                                       Vsb[:, vcol_fn(k, hl): vcol_fn(k, hl) + 65],
                                       PT[:, par * 1024 + hl * 256 + 128:
                                          par * 1024 + hl * 256 + 256],
                                       start=first, stop=last)
                    else:
                        ins = t.matmul(bank[0:65, cb: cb + 256],
                                       Vsb[:, vcol_fn(k, hl): vcol_fn(k, hl) + 65],
                                       PT[:, par * 1024 + hl * 256:
                                          par * 1024 + hl * 256 + 256],
                                       start=first, stop=last)
                    cx.inc(ins, P_, 1)

        @block.scalar
        def _(s, m=m, nkb=nkb):
            for k in range(nkb):
                par = k % 2
                if k >= 2:
                    cx.w(s, P_, pbase + (k - 2) * NH + NH)
                elif pbase > 0 and k < 2:
                    cx.w(s, P_, pbase)
                for hp in range(2):
                    j = k * 2 + hp
                    cx.w(s, S_, sbase + k * NH + hp * 2 + 2)
                    ins = s.activation(
                        out=PT[:, par * 1024 + hp * 512: par * 1024 + hp * 512 + 512],
                        in_=sb4[(ebase + j) % 4][:, 0:512], func=AF.Exp,
                        scale=exp_scale)
                    cx.inc(ins, E_, 1)

        @block.vector
        def _(v, m=m, nkb=nkb):
            for k in range(nkb - 8, nkb):
                par = k % 2
                upper = k >= nkb - 4
                slot = 2 * m + (1 if upper else 0)
                r = k - (nkb - 8) if not upper else k - (nkb - 4)
                mc = (slot * 4 + r) * 128
                for hl in range(NH):
                    cx.w(v, E_, iexp(k, hl))
                    off = par * 1024 + hl * 256 + (128 if upper else 0)
                    ins = v.tensor_mul(PT[:, off: off + 128],
                                       PT[:, off: off + 128],
                                       msksb[:, mc: mc + 128])
                    cx.inc(ins, M_, 1)

        @block.vector
        def _(v, m=m, nkb=nkb):
            cx.w(v, P_, pbase + nkb * NH)
            for i in range(2):
                cx.inc(v.tensor_copy(ysb[0:65, i * 512:(i + 1) * 512],
                                     yb[i][0:65, 0:512]), Y_, 1)

        @block.tensor
        def _(t, m=m):
            cx.w(t, Y_, ybase + 2)
            for qs in range(2):
                for hl in range(NH):
                    i = qs * NH + hl
                    if tbase + i >= 2:
                        cx.w(t, D_, tbase + i - 2 + 1)
                    ins = t.transpose(sb4[i % 2][0:128, 0:65],
                                      ysb[0:65, hl * 256 + qs * 128:
                                          hl * 256 + qs * 128 + 128],
                                      idf32[0:65, 0:65])
                    cx.inc(ins, T_, 1)

        @block.vector
        def _(v, m=m):
            for qs in range(2):
                if qs == 0 and (m > 0 or stybase > 0):
                    cx.w(v, "sty", stybase)
                if qs == 1:
                    cx.w(v, "sty", stybase + 16)
                for hl in range(NH):
                    i = qs * NH + hl
                    cx.w(v, T_, tbase + i + 1)
                    cx.inc(v.reciprocal(rlt[:, 0:1], sb4[i % 2][:, 64:65]), "rle", 1)
                    cx.w(v, "rle", cx.cnt["rle"])
                    ins = v.tensor_scalar(ytok[:, hl * 64: hl * 64 + 64],
                                          sb4[i % 2][:, 0:64], rlt[:, 0:1], None,
                                          ALU.mult)
                    cx.inc(ins, D_, 1)

        @block.sync
        def _(s, m=m):
            for qs in range(2):
                cx.w(s, D_, dbase + qs * NH + NH)
                cx.inc(y_store_fn(s, m, qs, ytok), "sty", 16)


def build_B(upto="full"):
    cx = Cx("B")
    nc = cx.nc
    x0 = cx.din("x0", [TOK, D], F32)
    qh = [cx.din(f"qh{p}", [64, 4 * TOK], BF16) for p in range(4)]
    Kh = [cx.din(f"Kh{p}", [64, NBLK * 4 * 128], BF16) for p in range(4)]
    Vh = [cx.din(f"Vh{p}", [128, NBLK * 260], BF16) for p in range(4)]
    mskt = cx.din("mskt", [128, NSLOT * 4 * 128], BF16)
    xn0fi = cx.din("xn0fi", [128, 8 * TOK], BF16)
    k0a_l = cx.din("k0a_l", [32, 128, 1024], U8)
    k0b_l = cx.din("k0b_l", [32, 128, 1024], U8)
    k1_l = cx.din("k1_l", [32, 128, 4096], U8)
    k2_l = cx.din("k2_l", [32, 128, 1024], U8)
    s012 = cx.din("s012", [1, 3], F32)
    epsB = cx.din("epsB", [128, 2], F32)
    n1w = cx.din("n1w", [1, 1024], F32)
    g_wq = cx.din("g_wq", [1024, 1024], F32)
    g_kvd = cx.din("g_kvd", [1024, 256], F32)
    g_ku = cx.din("g_ku", [256, 256], F32)
    g_vu = cx.din("g_vu", [256, 256], F32)
    cg512 = cx.din("cg512", [TOK, 512], F32)
    sg512 = cx.din("sg512", [TOK, 512], F32)
    cg128 = cx.din("cg128", [TOK, 128], F32)
    sg128 = cx.din("sg128", [TOK, 128], F32)
    identb = cx.din("identb", [128, 128], BF16)
    identf = cx.din("identf", [128, 128], F32)
    onesb = cx.din("onesb", [128, 1], BF16)
    onesf = cx.din("onesf", [1, 128], F32)

    o_x1 = cx.dout("x1", [TOK, D], F32)
    o_xn1 = cx.dout("xn1", [TOK, D], F32)
    o_xn1b = cx.dout("xn1b", [TOK, D], BF16)
    o_q1r = cx.dout("q1r", [TOK, 1024], BF16)
    o_k1r = cx.dout("k1r", [TOK, 256], BF16)
    o_v1 = cx.dout("v1", [TOK, 256], BF16)
    y_scr = cx.nc.dram_tensor("y_scr", [TOK, D], F32, kind="Internal").ap()

    idb = cx.sb("idb", [128, 128], BF16)
    idf = cx.sb("idf", [128, 128], F32)
    ones1b = cx.sb("ones1b", [128, 1], BF16)
    onesrf = cx.sb("onesrf", [1, 128], F32)
    scal = cx.sb("scal", [128, 16], F32)
    n1wb = cx.sb("n1wb", [128, 1024], F32)
    s012s = cx.sb("s012s", [1, 3], F32)
    n1ws = cx.sb("n1ws", [1, 1024], F32)

    block = cx.block()

    @block.sync
    def _(s):
        cx.inc(s.dma_start(out=idb[:], in_=identb), "ldc", 16)
        cx.inc(s.dma_start(out=idf[:], in_=identf), "ldc", 16)
        cx.inc(s.dma_start(out=ones1b[:], in_=onesb), "ldc", 16)
        cx.inc(s.dma_start(out=onesrf[:], in_=onesf), "ldc", 16)
        cx.inc(s.dma_start(out=s012s[:], in_=s012), "ldc", 16)
        cx.inc(s.dma_start(out=scal[:, 5:7], in_=epsB), "ldc", 16)
        cx.inc(s.dma_start(out=n1ws[:], in_=n1w), "ldc", 16)

    with cx.nc.psum_tensor("pset", [128, 512], F32) as pset:
        @block.tensor
        def _(t):
            cx.w(t, "ldc")
            cx.inc(t.matmul(pset[:, 0:3], onesrf[0:1, :], s012s[0:1, :],
                            start=True, stop=True), "bps", 1)

        @block.vector
        def _(v):
            cx.w(v, "bps", 1)
            v.tensor_scalar_mul(scal[:, 0:1], pset[:, 0:1], 2.0)
            v.tensor_scalar_mul(scal[:, 1:2], pset[:, 1:2], 4.0)
            v.tensor_scalar_mul(scal[:, 2:3], pset[:, 1:2], 2.0)
            v.tensor_scalar_mul(scal[:, 3:4], pset[:, 2:3], 4.0)
            v.tensor_scalar_mul(scal[:, 4:5], pset[:, 2:3], 2.0)
            cx.inc(v.tensor_copy(scal[:, 8:9], scal[:, 4:5]), "bse", 1)

        for hf in range(2):
            @block.tensor
            def _(t, hf=hf):
                cx.w(t, "bse", 1 + hf)
                cx.inc(t.matmul(pset[:, 0:512], onesrf[0:1, :],
                                n1ws[0:1, hf * 512:(hf + 1) * 512],
                                start=True, stop=True), "bps", 1)

            @block.vector
            def _(v, hf=hf):
                cx.w(v, "bps", 2 + hf)
                cx.inc(v.tensor_copy(n1wb[:, hf * 512:(hf + 1) * 512],
                                     pset[:, 0:512]), "bse", 1)

    if upto != "ffn":
      with (
          cx.nc.sbuf_tensor("Ksb", [64, NBLK * 4 * 128], BF16) as Ksb,
          cx.nc.sbuf_tensor("Vsb", [128, NBLK * 260], BF16) as Vsb,
          cx.nc.sbuf_tensor("qsb", [64, 4 * TOK], BF16) as qsb,
          cx.nc.sbuf_tensor("msksb", [128, NSLOT * 4 * 128], BF16) as msksb,
          cx.nc.sbuf_tensor("PTb", [128, 2 * 1024], BF16) as PTb,
          cx.nc.sbuf_tensor("ysb", [128, 1024], F32) as ysb,
          cx.nc.sbuf_tensor("ytok", [128, 256], F32) as ytok,
          cx.nc.sbuf_tensor("rlt", [128, 2], F32) as rlt,
          cx.nc.psum_tensor("sb40", [128, 512], F32) as sb40,
          cx.nc.psum_tensor("sb41", [128, 512], F32) as sb41,
          cx.nc.psum_tensor("sb42", [128, 512], F32) as sb42,
          cx.nc.psum_tensor("sb43", [128, 512], F32) as sb43,
          cx.nc.psum_tensor("yb0", [128, 512], F32) as yb0,
          cx.nc.psum_tensor("yb1", [128, 512], F32) as yb1,
      ):
          cx.PT = PTb
          sb4 = (sb40, sb41, sb42, sb43)
          yb = (yb0, yb1)

          @block.sync
          def _(s):
              cx.inc(s.dma_start(out=msksb[:], in_=mskt), "ldm", 16)

          for p in range(4):
              @block.sync
              def _(s, p=p):
                  if p >= 1:
                      cx.w(s, "aLp", cx.cnt["aLp"])
                      cx.w(s, "aLd", cx.cnt["aLd"])
                  cx.inc(s.dma_start(out=Ksb[:], in_=Kh[p]), "ldK", 16)
                  cx.inc(s.dma_start(out=Vsb[:], in_=Vh[p]), "ldK", 16)
                  cx.inc(s.dma_start(out=qsb[:], in_=qh[p]), "ldK", 16)

              nldK = cx.cnt["ldK"]

              def wp(t, p=p, nldK=nldK):
                  cx.w(t, "ldK", nldK)
                  cx.w(t, "ldm", 16)

              def yst(s_eng, m, qs, yt, p=p):
                  return s_eng.dma_start(
                      out=y_scr[256 * m + 128 * qs: 256 * m + 128 * qs + 128,
                                p * 256:(p + 1) * 256],
                      in_=yt[:, 0:256])

              emit_attention(cx, block, "aL", Ksb=Ksb, Vsb=Vsb, qsb=qsb,
                             msksb=msksb, idf32=idf,
                             sb4=sb4, yb=yb, ysb=ysb, ytok=ytok, rlt=rlt,
                             kcol_fn=lambda kb, hl: (kb * 4 + hl) * 128,
                             vcol_fn=lambda kb, hl: kb * 260 + hl * 65,
                             y_store_fn=yst, wait_pass_fn=wp)

          @block.sync
          def _(s):
              cx.w(s, "sty", cx.cnt["sty"])
              cx.w(s, "aLd", cx.cnt["aLd"])
              cx.w(s, "aLm", cx.cnt["aLm"])
              cx.w(s, "aLe", cx.cnt["aLe"])



          @block.vector
          def _(v):
              cx.w(v, "aLd", cx.cnt["aLd"])
              cx.w(v, "sty", cx.cnt["sty"])
              cx.inc(v.memset(rlt[:, 0:1], 0.0), "ph2", 1)


    xng1f = cx.sb("xng1f", [128, 8 * TOK], BF16)
    es2 = ExitStack()
    xn0sb = es2.enter_context(cx.nc.sbuf_tensor("xn0sb", [128, 8 * TOK], BF16))
    negx = es2.enter_context(cx.nc.sbuf_tensor("negx", [128, 4096], BF16))
    u1f = es2.enter_context(cx.nc.sbuf_tensor("u1f", [128, 32 * 512], BF16))
    u2f = es2.enter_context(cx.nc.sbuf_tensor("u2f", [128, 32 * 512], BF16))
    kwa = es2.enter_context(cx.nc.sbuf_tensor("kwa", [128, 2 * 1024], BF16))
    kwb = es2.enter_context(cx.nc.sbuf_tensor("kwb", [128, 2 * 1024], BF16))
    kw1 = es2.enter_context(cx.nc.sbuf_tensor("kw1", [128, 2 * 4096], BF16))
    kw2 = es2.enter_context(cx.nc.sbuf_tensor("kw2", [128, 2 * 1024], BF16))
    onesrb = es2.enter_context(cx.nc.sbuf_tensor("onesrb", [1, 128], BF16))
    r1row = es2.enter_context(cx.nc.sbuf_tensor("r1row", [1, 512], F32))
    r1bs = es2.enter_context(cx.nc.sbuf_tensor("r1bs", [128, 512], F32))
    r2s = es2.enter_context(cx.nc.sbuf_tensor("r2s", [128, 4], F32))
    tmp2 = es2.enter_context(cx.nc.sbuf_tensor("tmp2", [128, 2 * 512], F32))
    f0t = es2.enter_context(cx.nc.sbuf_tensor("f0t", [128, 1024], F32))
    yt = es2.enter_context(cx.nc.sbuf_tensor("yt", [128, 1024], F32))
    x1t = es2.enter_context(cx.nc.sbuf_tensor("x1t", [128, 1024], F32))
    xn1t = es2.enter_context(cx.nc.sbuf_tensor("xn1t", [128, 1024], F32))
    lbS1 = es2.enter_context(cx.nc.sbuf_tensor("lbS1", [128, 64], F32))
    lbS2 = es2.enter_context(cx.nc.sbuf_tensor("lbS2", [128, 64], F32))
    lbM = es2.enter_context(cx.nc.sbuf_tensor("lbM", [128, 64], F32))
    lbV = es2.enter_context(cx.nc.sbuf_tensor("lbV", [128, 64], F32))
    lbR = es2.enter_context(cx.nc.sbuf_tensor("lbR", [128, 64], F32))
    lbSh = es2.enter_context(cx.nc.sbuf_tensor("lbSh", [128, 64], F32))
    lbS2r = es2.enter_context(cx.nc.sbuf_tensor("lbS2r", [128, 64], F32))
    lbR2 = es2.enter_context(cx.nc.sbuf_tensor("lbR2", [128, 64], F32))
    xngt = es2.enter_context(cx.nc.sbuf_tensor("xngt", [128, 1024], F32))
    pp0 = es2.enter_context(cx.nc.psum_tensor("pp0", [128, 512], F32))
    pp1 = es2.enter_context(cx.nc.psum_tensor("pp1", [128, 512], F32))
    prr = es2.enter_context(cx.nc.psum_tensor("prr", [128, 512], F32))
    pq0 = es2.enter_context(cx.nc.psum_tensor("pq0", [128, 512], F32))
    pq1 = es2.enter_context(cx.nc.psum_tensor("pq1", [128, 512], F32))
    pq2 = es2.enter_context(cx.nc.psum_tensor("pq2", [128, 512], F32))
    pq3 = es2.enter_context(cx.nc.psum_tensor("pq3", [128, 512], F32))
    if True:
        pp = (pp0, pp1)
        pq = (pq0, pq1, pq2, pq3)

        @block.sync
        def _(s):
            cx.inc(s.dma_start(out=xn0sb[:].rearrange("p (i t) -> p i t", t=TOK),
                               in_=xn0fi.rearrange("p (i t) -> p i t", t=TOK)),
                   "lxn", 16)

        @block.vector
        def _(v):
            cx.inc(v.memset(onesrb[:], 1.0), "bse", 1)

        for tt in range(4):
            tb = tt * 512
            @block.vector
            def _(v, tt=tt, tb=tb):
                cx.w(v, "lxn", 16)
                if tt > 0:
                    cx.w(v, "f0p", cx.cnt.get("f0p", 0))
                if tt > 0:
                    cx.w(v, "f0p", cx.cnt.get("f0p", 0))
                cx.inc(v.tensor_scalar_mul(
                    negx[:].rearrange("p (i t) -> p i t", t=512),
                    xn0sb[:].rearrange("p (i t) -> p i t", t=TOK)[:, :, tb:tb + 512],
                    -1.0), "ffd", 1)

            f0p_base = cx.cnt.get("f0p", 0)
            f0a_base = cx.cnt.get("f0a", 0)
            lka_base = cx.cnt.get("lka", 0)
            @block.gpsimd
            def _(g, tt=tt):
                if tt == 0 and "ph2" in cx.cnt:
                    cx.w(g, "ph2", 1)
                for oc in range(32):
                    if oc >= 2:
                        cx.w(g, "f0p", f0p_base + oc - 1)
                    elif tt > 0:
                        cx.w(g, "f0p", f0p_base)
                    cx.inc(g.dma_start(out=kwa[:, (oc % 2) * 1024:(oc % 2) * 1024 + 1024],
                                       in_=k0a_l[oc]), "lka", 16)
                    cx.inc(g.dma_start(out=kwb[:, (oc % 2) * 1024:(oc % 2) * 1024 + 1024],
                                       in_=k0b_l[oc]), "lka", 16)

            @block.tensor
            def _(t, tt=tt, tb=tb):
                if tt == 0 and upto != "ffn":
                    cx.w(t, "aLd", cx.cnt["aLd"])
                    cx.w(t, "aLy", cx.cnt["aLy"])
                cx.w(t, "ffd", cx.cnt["ffd"])
                if cx.cnt.get("f2d", 0) > 0:
                    cx.w(t, "f2d", cx.cnt["f2d"])
                for oc in range(32):
                    cx.w(t, "lka", lka_base + (oc + 1) * 32)
                    if oc >= 2:
                        cx.w(t, "f0a", f0a_base + oc - 1)
                    po = (oc % 2) * 1024
                    pairs = ([(kwa[:, po + ic * 128: po + ic * 128 + 128],
                               xn0sb[:, ic * TOK + tb: ic * TOK + tb + 512])
                              for ic in range(8)] +
                             [(kwb[:, po + ic * 128: po + ic * 128 + 128],
                               negx[:, ic * 512: ic * 512 + 512])
                              for ic in range(8)])
                    mm_acc(cx, t, pp[oc % 2][:, 0:512], pairs, "f0p")

            @block.scalar
            def _(s, tt=tt):
                for oc in range(32):
                    cx.w(s, "f0p", f0p_base + oc + 1)
                    cx.inc(s.activation(out=u1f[:, oc * 512:(oc + 1) * 512],
                                        in_=pp[oc % 2][:, 0:512], func=AF.Relu,
                                        scale=scal[:, 0:1]), "f0a", 1)

            r1p_base = cx.cnt.get("r1p", 0)
            @block.tensor
            def _(t, tt=tt):
                if "r2e" in cx.cnt:
                    cx.w(t, "r2e", cx.cnt["r2e"])
                if cx.cnt.get("f2d", 0) > 0:
                    cx.w(t, "f2d", cx.cnt["f2d"])
                for oc in range(32):
                    cx.w(t, "f0a", f0a_base + oc + 1)
                    ins = t.matmul(prr[0:1, 0:512], ones1b[:, 0:1],
                                   u1f[:, oc * 512:(oc + 1) * 512],
                                   start=(oc == 0), stop=(oc == 31))
                    if oc == 31:
                        cx.inc(ins, "r1p", 1)

            @block.vector
            def _(v, tt=tt):
                cx.w(v, "r1p", r1p_base + 1)
                cx.inc(v.tensor_copy(r1row[0:1, 0:512], prr[0:1, 0:512]), "r1e", 1)

            @block.tensor
            def _(t, tt=tt):
                cx.w(t, "r1e", cx.cnt["r1e"])
                cx.w(t, "bse", cx.cnt["bse"])
                if cx.cnt.get("f2d", 0) > 0:
                    cx.w(t, "f2d", cx.cnt["f2d"])
                cx.inc(t.matmul(pq0[:, 0:512], onesrf[0:1, :], r1row[0:1, :],
                                start=True, stop=True), "r1p", 1)

            @block.vector
            def _(v, tt=tt):
                cx.w(v, "r1p", r1p_base + 2)
                cx.inc(v.tensor_scalar(r1bs[:], pq0[:, 0:512], scal[:, 2:3], None,
                                       ALU.mult), "r1e", 1)

            f1p_base = cx.cnt.get("f1p", 0)
            f1a_base = cx.cnt.get("f1a", 0)
            f1d_base = cx.cnt.get("f1d", 0)
            lk1_base = cx.cnt.get("lk1", 0)
            @block.gpsimd
            def _(g, tt=tt):
                for oc in range(32):
                    if oc >= 2:
                        cx.w(g, "f1p", f1p_base + oc - 1)
                    elif tt > 0:
                        cx.w(g, "f1p", f1p_base)
                    cx.inc(g.dma_start(out=kw1[:, (oc % 2) * 4096:(oc % 2) * 4096 + 4096],
                                       in_=k1_l[oc]), "lk1", 16)

            @block.tensor
            def _(t, tt=tt):
                cx.w(t, "f0a", f0a_base + 32)
                for oc in range(32):
                    cx.w(t, "lk1", lk1_base + (oc + 1) * 16)
                    if oc >= 2:
                        cx.w(t, "f1d", f1d_base + oc - 1)
                    po = (oc % 2) * 4096
                    pairs = [(kw1[:, po + ic * 128: po + ic * 128 + 128],
                              u1f[:, ic * 512: ic * 512 + 512])
                             for ic in range(32)]
                    mm_acc(cx, t, pp[oc % 2][:, 0:512], pairs, "f1p")

            @block.vector
            def _(v, tt=tt):
                for oc in range(32):
                    cx.w(v, "f1p", f1p_base + oc + 1)
                    if oc == 0:
                        cx.w(v, "r1e", cx.cnt["r1e"])
                    if oc >= 2:
                        cx.w(v, "f1a", f1a_base + oc - 1)
                    cx.inc(v.scalar_tensor_tensor(
                        tmp2[:, (oc % 2) * 512:(oc % 2) * 512 + 512],
                        pp[oc % 2][:, 0:512], scal[:, 1:2], r1bs[:],
                        ALU.mult, ALU.subtract), "f1d", 1)

            @block.scalar
            def _(s, tt=tt):
                for oc in range(32):
                    cx.w(s, "f1d", f1d_base + oc + 1)
                    cx.inc(s.activation(out=u2f[:, oc * 512:(oc + 1) * 512],
                                        in_=tmp2[:, (oc % 2) * 512:(oc % 2) * 512 + 512],
                                        func=AF.Relu), "f1a", 1)

            r2p_base = cx.cnt.get("r2p", 0)
            @block.tensor
            def _(t, tt=tt):
                first = True
                for ts in range(4):
                    for ic in range(32):
                        cx.w(t, "f1a", f1a_base + ic + 1)
                        ins = t.matmul(prr[0:128, ts:ts + 1],
                                       u2f[:, ic * 512 + ts * 128: ic * 512 + ts * 128 + 128],
                                       ones1b[:, 0:1],
                                       start=first, stop=(ts == 3 and ic == 31))
                        first = False
                        if ts == 3 and ic == 31:
                            cx.inc(ins, "r2p", 1)

            @block.vector
            def _(v, tt=tt):
                cx.w(v, "r2p", r2p_base + 1)
                cx.inc(v.tensor_scalar(r2s[:], prr[:, 0:4], scal[:, 4:5], None,
                                       ALU.mult), "r2e", 1)

            lk2_base = cx.cnt.get("lk2", 0)
            f2p_base = cx.cnt.get("f2p", 0)
            for R in range(2):
                @block.gpsimd
                def _(g, tt=tt, R=R):
                    for ic in range(32):
                        gi = R * 32 + ic
                        if gi >= 2:
                            cx.w(g, "f2p", f2p_base + (gi - 1) * 4)
                        elif tt > 0 or R > 0:
                            cx.w(g, "f2p", f2p_base)
                        cx.inc(g.dma_start(
                            out=kw2[:, (gi % 2) * 1024:(gi % 2) * 1024 + 1024],
                            in_=k2_l[ic]), "lk2", 16)

                @block.tensor
                def _(t, tt=tt, R=R):
                    if cx.cnt.get("f2d", 0) > 0:
                        cx.w(t, "f2d", cx.cnt["f2d"])
                    cx.w(t, "f1a", f1a_base + 32)
                    for ic in range(32):
                        gi = R * 32 + ic
                        cx.w(t, "lk2", lk2_base + (gi + 1) * 16)
                        for tsl in range(2):
                            ts = R * 2 + tsl
                            for half in range(2):
                                ins = t.matmul(
                                    pq[tsl * 2 + half][:, 0:512],
                                    u2f[:, ic * 512 + ts * 128: ic * 512 + ts * 128 + 128],
                                    kw2[:, (gi % 2) * 1024 + half * 512:
                                        (gi % 2) * 1024 + half * 512 + 512],
                                    start=(ic == 0), stop=(ic == 31))
                                cx.inc(ins, "f2p", 1)

                for tsl in range(2):
                    ts = R * 2 + tsl

                    @block.sync
                    def _(s, tt=tt, ts=ts):
                        if cx.cnt.get("x1d", 0) > 0:
                            cx.w(s, "x1d", cx.cnt["x1d"])
                        r0 = tt * 512 + ts * 128
                        cx.inc(s.dma_start(out=x1t[:], in_=x0[r0:r0 + 128, :]), "lx0", 16)
                        cx.inc(s.dma_start(out=yt[:], in_=y_scr[r0:r0 + 128, :]), "lx0", 16)

                    @block.vector
                    def _(v, tt=tt, R=R, tsl=tsl, ts=ts):
                        cx.w(v, "f2p", f2p_base + R * 128 + 128)
                        cx.w(v, "r2e", cx.cnt["r2e"])
                        cx.w(v, "lx0", cx.cnt["lx0"])
                        for half in range(2):
                            v.tensor_scalar(f0t[:, half * 512:(half + 1) * 512],
                                            pq[tsl * 2 + half][:, 0:512],
                                            scal[:, 3:4], r2s[:, ts:ts + 1],
                                            ALU.mult, ALU.subtract)
                        if upto != "ffn":
                            v.tensor_add(x1t[:], x1t[:], yt[:])
                        cx.inc(v.tensor_add(x1t[:], x1t[:], f0t[:]), "f2d", 1)
                        cx.w(v, "f2d", cx.cnt["f2d"])
                        cx.inc(v.tensor_copy(f0t[:, 0:128], x1t[:, 0:128]), "f2s", 1)

                    @block.sync
                    def _(s, tt=tt, ts=ts):
                        r0 = tt * 512 + ts * 128
                        cx.w(s, "f2s", cx.cnt["f2s"])
                        cx.inc(s.dma_start(out=o_x1[r0:r0 + 128, :], in_=x1t[:]),
                               "x1d", 16)

        for j in range(NSLOT):
            @block.sync
            def _(s, j=j):
                if j < 2:
                    cx.w(s, "x1d", cx.cnt["x1d"])
                else:
                    cx.w(s, "b1d", 2 * (j - 1))
                cx.inc(s.dma_start(
                    out=(x1t if j % 2 == 0 else yt)[:],
                    in_=o_x1[j * 128:(j + 1) * 128, :]), "lb1", 16)

            @block.vector
            def _(v, j=j):
                cx.w(v, "lb1", (j + 1) * 16)
                xv = (x1t if j % 2 == 0 else yt)[:]
                cx.inc(v.reduce_sum(lbS1[:, 4 * j:4 * j + 1], xv, AX.X), "b1d", 1)
                v.tensor_mul(tmp2[:, 0:512], xv[:, 0:512], xv[:, 0:512])
                v.tensor_mul(tmp2[:, 512:1024], xv[:, 512:1024], xv[:, 512:1024])
                cx.w(v, "b1d", cx.cnt["b1d"])
                cx.inc(v.reduce_sum(lbS2[:, 4 * j:4 * j + 1], tmp2[:], AX.X), "b1d", 1)

        @block.vector
        def _(v):
            cx.w(v, "b1d", 2 * NSLOT)
            s1v = lbS1[:].rearrange("p (j f) -> p j f", f=4)[:, :, 0]
            s2v = lbS2[:].rearrange("p (j f) -> p j f", f=4)[:, :, 0]
            mv = lbM[:].rearrange("p (j f) -> p j f", f=4)[:, :, 0]
            vv = lbV[:].rearrange("p (j f) -> p j f", f=4)[:, :, 0]
            sv = lbSh[:].rearrange("p (j f) -> p j f", f=4)[:, :, 0]
            cx.inc(v.tensor_scalar_mul(mv, s1v, 1.0 / 1024), "bcm", 1)
            cx.w(v, "bcm", cx.cnt["bcm"])
            cx.inc(v.tensor_scalar_mul(vv, s2v, 1.0 / 1024), "bcm", 1)
            cx.w(v, "bcm", cx.cnt["bcm"])
            cx.inc(v.tensor_mul(sv, mv, mv), "bcm", 1)
            cx.w(v, "bcm", cx.cnt["bcm"])
            cx.inc(v.tensor_sub(vv, vv, sv), "bcm", 1)
            cx.w(v, "bcm", cx.cnt["bcm"])
            cx.inc(v.tensor_copy(sv, vv), "bcm", 1)

        @block.scalar
        def _(s):
            cx.w(s, "bcm", cx.cnt["bcm"])
            vv = lbV[:].rearrange("p (j f) -> p j f", f=4)[:, :, 0]
            rv_ = lbR[:].rearrange("p (j f) -> p j f", f=4)[:, :, 0]
            r2v = lbR[:].rearrange("p (j f) -> p j f", f=4)[:, :, 1]
            sv = lbSh[:].rearrange("p (j f) -> p j f", f=4)[:, :, 1]
            cx.inc(s.activation(out=r2v, in_=vv, func=AF.Ln, bias=scal[:, 5:6]),
                   "bca", 1)
            cx.w(s, "bca", cx.cnt["bca"])
            cx.inc(s.activation(out=rv_, in_=r2v, func=AF.Exp, scale=-0.5), "bca", 1)
            cx.w(s, "bca", cx.cnt["bca"])
            cx.inc(s.activation(out=sv, in_=rv_, func=AF.Copy), "bca", 1)

        for j in range(NSLOT):
            @block.sync
            def _(s, j=j):
                if j < 2:
                    cx.w(s, "b1d", 2 * NSLOT)
                else:
                    cx.w(s, "b2d", 2 * (j - 1))
                cx.inc(s.dma_start(
                    out=(x1t if j % 2 == 0 else yt)[:],
                    in_=o_x1[j * 128:(j + 1) * 128, :]), "lb2", 16)

            @block.vector
            def _(v, j=j):
                cx.w(v, "lb2", (j + 1) * 16)
                if j == 0:
                    cx.w(v, "bca", cx.cnt["bca"])
                if j >= 1:
                    cx.w(v, "x2d", cx.cnt["x2d"])
                xv = (x1t if j % 2 == 0 else yt)[:]
                cx.inc(v.tensor_scalar(xn1t[:], xv, lbM[:, 4 * j:4 * j + 1],
                                       lbR[:, 4 * j:4 * j + 1],
                                       ALU.subtract, ALU.mult), "b2d", 1)
                v.tensor_mul(tmp2[:, 0:512], xn1t[:, 0:512], xn1t[:, 0:512])
                v.tensor_mul(tmp2[:, 512:1024], xn1t[:, 512:1024], xn1t[:, 512:1024])
                cx.w(v, "b2d", cx.cnt["b2d"])
                cx.inc(v.reduce_sum(lbS2r[:, 4 * j:4 * j + 1], tmp2[:], AX.X),
                       "b2d", 1)

            @block.sync
            def _(s, j=j):
                cx.w(s, "b2d", cx.cnt["b2d"])
                cx.inc(s.dma_start(out=o_xn1[j * 128:(j + 1) * 128, :],
                                   in_=xn1t[:]), "x2d", 16)

            @block.gpsimd
            def _(g, j=j):
                cx.w(g, "b2d", cx.cnt["b2d"])
                cx.inc(g.dma_start(out=o_xn1b[j * 128:(j + 1) * 128, :],
                                   in_=xn1t[:]), "xbd", 16)

        @block.scalar
        def _(s):
            cx.w(s, "b2d", 2 * NSLOT)
            s2v = lbS2r[:].rearrange("p (j f) -> p j f", f=4)[:, :, 0]
            rv_ = lbR2[:].rearrange("p (j f) -> p j f", f=4)[:, :, 0]
            r2v = lbR2[:].rearrange("p (j f) -> p j f", f=4)[:, :, 1]
            sv = lbR2[:].rearrange("p (j f) -> p j f", f=4)[:, :, 2]
            cx.inc(s.activation(out=r2v, in_=s2v, func=AF.Ln, scale=1.0 / 1024,
                                bias=scal[:, 6:7]), "bca", 1)
            cx.w(s, "bca", cx.cnt["bca"])
            cx.inc(s.activation(out=rv_, in_=r2v, func=AF.Exp, scale=-0.5), "bca", 1)
            cx.w(s, "bca", cx.cnt["bca"])
            cx.inc(s.activation(out=sv, in_=rv_, func=AF.Copy), "bca", 1)

        for j in range(NSLOT):
            @block.sync
            def _(s, j=j):
                if j < 2:
                    cx.w(s, "x2d", cx.cnt["x2d"])
                else:
                    cx.w(s, "b3d", j - 1)
                cx.inc(s.dma_start(
                    out=(x1t if j % 2 == 0 else yt)[:],
                    in_=o_xn1[j * 128:(j + 1) * 128, :]), "lb3", 16)

            @block.vector
            def _(v, j=j):
                cx.w(v, "lb3", (j + 1) * 16)
                if j == 0:
                    cx.w(v, "bca", cx.cnt["bca"])
                if j >= 1:
                    cx.w(v, "l1t", 8 * j)
                xv = (x1t if j % 2 == 0 else yt)[:]
                v.tensor_scalar(xngt[:], xv, lbR2[:, 4 * j:4 * j + 1], None,
                                ALU.mult)
                cx.inc(v.tensor_mul(xngt[:], xngt[:], n1wb[:]), "b3d", 1)

            @block.tensor
            def _(t, j=j):
                cx.w(t, "b3d", cx.cnt["b3d"])
                for ci in range(8):
                    gi = j * 8 + ci
                    if gi >= 2:
                        cx.w(t, "l1e", gi - 1)
                    cx.inc(t.transpose(pp[ci % 2][0:128, 0:128],
                                       xngt[:, ci * 128:(ci + 1) * 128],
                                       idf[:]), "l1t", 1)

            @block.vector
            def _(v, j=j):
                for ci in range(8):
                    cx.w(v, "l1t", j * 8 + ci + 1)
                    cx.inc(v.tensor_copy(
                        xng1f[:, ci * TOK + j * 128: ci * TOK + (j + 1) * 128],
                        pp[ci % 2][:, 0:128]), "l1e", 1)

    es2.close()
    if True:
        es3 = ExitStack()
        gwq_sb = es3.enter_context(cx.nc.sbuf_tensor("gwq_sb", [128, 8 * 1024], BF16))
        gkvd_sb = es3.enter_context(cx.nc.sbuf_tensor("gkvd_sb", [128, 8 * 256], BF16))
        gku_sb = es3.enter_context(cx.nc.sbuf_tensor("gku_sb", [128, 2 * 256], BF16))
        gvu_sb = es3.enter_context(cx.nc.sbuf_tensor("gvu_sb", [128, 2 * 256], BF16))
        latf = es3.enter_context(cx.nc.sbuf_tensor("latf", [128, 2 * TOK], BF16))
        ropeb = es3.enter_context(cx.nc.sbuf_tensor("ropeb", [128, 2 * 512], F32))
        ropebk = es3.enter_context(cx.nc.sbuf_tensor("ropebk", [128, 2 * 128], F32))
        qtw = es3.enter_context(cx.nc.sbuf_tensor("qtw", [128, 1024], F32))
        qtn = es3.enter_context(cx.nc.sbuf_tensor("qtn", [128, 1024], F32))
        ktw = es3.enter_context(cx.nc.sbuf_tensor("ktw", [128, 256], F32))
        ktn = es3.enter_context(cx.nc.sbuf_tensor("ktn", [128, 256], F32))
        rA = es3.enter_context(cx.nc.sbuf_tensor("rA", [128, 512], F32))
        rB = es3.enter_context(cx.nc.sbuf_tensor("rB", [128, 512], F32))
        hsqf = es3.enter_context(cx.nc.sbuf_tensor("hsqf", [128, 1024], F32))
        hst = es3.enter_context(cx.nc.sbuf_tensor("hst", [128, 48], F32))
        qrb = es3.enter_context(cx.nc.sbuf_tensor("qrb", [128, 1024], BF16))
        krb = es3.enter_context(cx.nc.sbuf_tensor("krb", [128, 256], BF16))
        vtb = es3.enter_context(cx.nc.sbuf_tensor("vtb", [128, 256], BF16))
        if True:
            @block.gpsimd
            def _(g):
                cx.w(g, "l1e", cx.cnt["l1e"])
                cx.w(g, "x2d", cx.cnt["x2d"])
                cx.w(g, "b3d", cx.cnt["b3d"])
                cx.w(g, "f2p", cx.cnt["f2p"])
                cx.w(g, "f1a", cx.cnt["f1a"])
                cx.inc(g.dma_start(out=gwq_sb[:].rearrange("p (i c) -> p i c", c=1024),
                                   in_=g_wq.rearrange("(i p) c -> p i c", p=128)), "lgw", 16)
                cx.inc(g.dma_start(out=gkvd_sb[:].rearrange("p (i c) -> p i c", c=256),
                                   in_=g_kvd.rearrange("(i p) c -> p i c", p=128)), "lgw", 16)
                cx.inc(g.dma_start(out=gku_sb[:].rearrange("p (i c) -> p i c", c=256),
                                   in_=g_ku.rearrange("(i p) c -> p i c", p=128)), "lgw", 16)
                cx.inc(g.dma_start(out=gvu_sb[:].rearrange("p (i c) -> p i c", c=256),
                                   in_=g_vu.rearrange("(i p) c -> p i c", p=128)), "lgw", 16)

            nL1E = cx.cnt["l1e"]
            jobs = []
            for oc in range(2):
                for tc in range(4):
                    pairs = [(wchunked(gkvd_sb, ic, 256, oc * 128, 128),
                              wchunked(xng1f, ic, TOK, tc * 512, 512))
                             for ic in range(8)]
                    jobs.append((pairs, 512, cp_to_gen(cx, latf, oc * TOK + tc * 512, 512)))
            fproj(cx, block, "lat", jobs, (pq0, pq1),
                  wait_tensor=lambda t: (cx.w(t, "l1e", nL1E), cx.w(t, "lgw", 64)))

            for j in range(NSLOT):
                @block.sync
                def _(s, j=j):
                    if j >= 1:
                        cx.w(s, "rpd", cx.cnt["rpd"])
                    else:
                        cx.w(s, "lgw", 64)
                    cx.inc(s.dma_start(out=ropeb[:, 0:512],
                                       in_=cg512[j * 128:(j + 1) * 128, :]), "lrp", 16)
                    cx.inc(s.dma_start(out=ropeb[:, 512:1024],
                                       in_=sg512[j * 128:(j + 1) * 128, :]), "lrp", 16)
                    cx.inc(s.dma_start(out=ropebk[:, 0:128],
                                       in_=cg128[j * 128:(j + 1) * 128, :]), "lrp", 16)
                    cx.inc(s.dma_start(out=ropebk[:, 128:256],
                                       in_=sg128[j * 128:(j + 1) * 128, :]), "lrp", 16)

                @block.tensor
                def _(t, j=j):
                    if j == 0:
                        cx.w(t, "latv", cx.cnt["latv"])
                        cx.w(t, "lgw", 64)
                    else:
                        cx.w(t, "pre", cx.cnt["pre"])
                    for oc in range(2):
                        pairs = [(xng1f[:, ic * TOK + j * 128: ic * TOK + j * 128 + 128],
                                  wchunked(gwq_sb, ic, 1024, oc * 512, 512))
                                 for ic in range(8)]
                        mm_acc(cx, t, pq[oc][:, 0:512], pairs, "prp")
                    pairs = [(latf[:, lc * TOK + j * 128: lc * TOK + j * 128 + 128],
                              wchunked(gku_sb, lc, 256, 0, 256)) for lc in range(2)]
                    mm_acc(cx, t, pq2[:, 0:256], pairs, "prp")
                    pairs = [(latf[:, lc * TOK + j * 128: lc * TOK + j * 128 + 128],
                              wchunked(gvu_sb, lc, 256, 0, 256)) for lc in range(2)]
                    mm_acc(cx, t, pq3[:, 0:256], pairs, "prp")

                @block.vector
                def _(v, j=j):
                    cx.w(v, "prp", cx.cnt["prp"])
                    cx.w(v, "lrp", cx.cnt["lrp"])
                    if j >= 1:
                        cx.w(v, "stqkv", cx.cnt["stqkv"])
                    v.tensor_copy(qtw[:, 0:512], pq[0][:, 0:512])
                    v.tensor_copy(qtw[:, 512:1024], pq[1][:, 0:512])
                    v.tensor_copy(ktw[:], pq2[:, 0:256])
                    cx.inc(v.tensor_copy(vtb[:], pq3[:, 0:256]), "pre", 1)
                    emit_rope(v, qtn, qtw, ropeb[:, 0:512], ropeb[:, 512:1024],
                              16, rA, rB)
                    emit_rope(v, ktn, ktw, ropebk[:, 0:128], ropebk[:, 128:256],
                              4, rA, rB)
                    v.tensor_mul(hsqf[:], qtn[:], qtn[:])
                    cx.inc(v.tensor_reduce(
                        hst[:, 0:16], hsqf[:].rearrange("p (h d) -> p h d", d=64),
                        AX.X, ALU.add), "rpd", 1)
                    v.tensor_mul(hsqf[:, 0:256], ktn[:], ktn[:])
                    cx.inc(v.tensor_reduce(
                        hst[:, 16:20],
                        hsqf[:, 0:256].rearrange("p (h d) -> p h d", d=64),
                        AX.X, ALU.add), "rpd", 1)

                @block.scalar
                def _(s, j=j):
                    cx.w(s, "rpd", cx.cnt["rpd"])
                    s.activation(out=hst[:, 20:40], in_=hst[:, 0:20], func=AF.Ln,
                                 scale=1.0 / 64, bias=scal[:, 6:7])
                    cx.inc(s.activation(out=hst[:, 20:40], in_=hst[:, 20:40],
                                        func=AF.Exp, scale=-0.5), "rpa", 1)

                @block.vector
                def _(v, j=j):
                    cx.w(v, "rpa", cx.cnt["rpa"])
                    for h in range(16):
                        v.tensor_scalar(qrb[:, h * 64:(h + 1) * 64],
                                        qtn[:, h * 64:(h + 1) * 64],
                                        hst[:, 20 + h: 21 + h], None, ALU.mult)
                    for h in range(4):
                        v.tensor_scalar(krb[:, h * 64:(h + 1) * 64],
                                        ktn[:, h * 64:(h + 1) * 64],
                                        hst[:, 36 + h: 37 + h], None, ALU.mult)
                    cx.inc(v.tensor_copy(krb[:, 0:1], krb[:, 0:1]), "rpd", 1)

                @block.sync
                def _(s, j=j):
                    cx.w(s, "rpd", cx.cnt["rpd"])
                    cx.w(s, "pre", cx.cnt["pre"])
                    r0 = j * 128
                    cx.inc(s.dma_start(out=o_q1r[r0:r0 + 128, :], in_=qrb[:]),
                           "stqkv", 16)
                    cx.inc(s.dma_start(out=o_k1r[r0:r0 + 128, :], in_=krb[:]),
                           "stqkv", 16)
                    cx.inc(s.dma_start(out=o_v1[r0:r0 + 128, :], in_=vtb[:]),
                           "stqkv", 16)

            @block.sync
            def _(s):
                cx.w(s, "stqkv", cx.cnt["stqkv"])
                cx.w(s, "x1d", cx.cnt["x1d"])
                cx.w(s, "x2d", cx.cnt["x2d"])
                cx.w(s, "xbd", cx.cnt["xbd"])
                if "sty" in cx.cnt:
                    cx.w(s, "sty", cx.cnt["sty"])
        es3.close()

    cx.close()
    return cx




def build_C():
    cx = Cx("C")
    qh = [cx.din(f"qh{p}", [64, 4 * TOK], BF16) for p in range(4)]
    Kh = [cx.din(f"Kh{p}", [64, NBLK * 128], BF16) for p in range(4)]
    Vh = [cx.din(f"Vh{p}", [128, NBLK * 65], BF16) for p in range(4)]
    mskt = cx.din("mskt", [128, NSLOT * 4 * 128], BF16)
    xn1i = cx.din("xn1i", [TOK, D], F32)
    g_wo = cx.din("g_wo", [1024, 1024], F32)
    l_wq = cx.din("l_wq", [1024, 1024], F32)
    l_wk = cx.din("l_wk", [1024, 256], F32)
    l_wv = cx.din("l_wv", [1024, 256], F32)
    n2w = cx.din("n2w", [1, 1024], F32)
    cl512 = cx.din("cl512", [TOK, 512], F32)
    sl512 = cx.din("sl512", [TOK, 512], F32)
    cl128 = cx.din("cl128", [TOK, 128], F32)
    sl128 = cx.din("sl128", [TOK, 128], F32)
    identf = cx.din("identf", [128, 128], F32)
    identb = cx.din("identb", [128, 128], BF16)
    epsC = cx.din("epsC", [128, 2], F32)
    o_u = cx.dout("u", [TOK, D], F32)
    o_q2r = cx.dout("q2r", [TOK, 1024], BF16)
    o_k2r = cx.dout("k2r", [TOK, 256], BF16)
    o_v2 = cx.dout("v2", [TOK, 256], BF16)
    y_scr = cx.nc.dram_tensor("y_scr", [TOK, D], F32, kind="Internal").ap()

    idf = cx.sb("idf", [128, 128], F32)
    idb = cx.sb("idbC", [128, 128], BF16)
    eps6 = cx.sb("eps6", [128, 2], F32)
    n2wb = cx.sb("n2wb", [128, 1024], F32)
    n2ws = cx.sb("n2ws", [1, 1024], F32)
    onesrf = cx.sb("onesrf", [1, 128], F32)
    xng2f = cx.sb("xng2f", [128, 8 * TOK], BF16)

    block = cx.block()

    @block.sync
    def _(s):
        cx.inc(s.dma_start(out=idf[:], in_=identf), "ldc", 16)
        cx.inc(s.dma_start(out=idb[:], in_=identb), "ldc", 16)
        cx.inc(s.dma_start(out=eps6[:], in_=epsC), "ldc", 16)
        cx.inc(s.dma_start(out=n2ws[:], in_=n2w), "ldc", 16)

    @block.vector
    def _(v):
        cx.inc(v.memset(onesrf[:], 1.0), "ini", 1)

    with cx.nc.psum_tensor("psetC", [128, 512], F32) as pset:
        for hf in range(2):
            @block.tensor
            def _(t, hf=hf):
                if hf == 0:
                    cx.w(t, "ldc")
                    cx.w(t, "ini", 1)
                else:
                    cx.w(t, "bse", 1)
                cx.inc(t.matmul(pset[:, 0:512], onesrf[0:1, :],
                                n2ws[0:1, hf * 512:(hf + 1) * 512],
                                start=True, stop=True), "bps", 1)

            @block.vector
            def _(v, hf=hf):
                cx.w(v, "bps", 1 + hf)
                cx.inc(v.tensor_copy(n2wb[:, hf * 512:(hf + 1) * 512],
                                     pset[:, 0:512]), "bse", 1)

    with (
        cx.nc.sbuf_tensor("KsbC", [64, NBLK * 128], BF16) as Ksb,
        cx.nc.sbuf_tensor("VsbC", [128, NBLK * 65], BF16) as Vsb,
        cx.nc.sbuf_tensor("qsbC", [64, 4 * TOK], BF16) as qsb,
        cx.nc.sbuf_tensor("mskC", [128, NSLOT * 4 * 128], BF16) as msksb,
        cx.nc.sbuf_tensor("PTC", [128, 2 * 1024], BF16) as PTb,
        cx.nc.sbuf_tensor("ysbC", [128, 1024], F32) as ysb,
        cx.nc.sbuf_tensor("ytokC", [128, 256], F32) as ytok,
        cx.nc.sbuf_tensor("rltC", [128, 2], F32) as rlt,
        cx.nc.psum_tensor("sc0", [128, 512], F32) as sc0,
        cx.nc.psum_tensor("sc1", [128, 512], F32) as sc1,
        cx.nc.psum_tensor("sc2", [128, 512], F32) as sc2,
        cx.nc.psum_tensor("sc3", [128, 512], F32) as sc3,
        cx.nc.psum_tensor("yc0", [128, 512], F32) as yc0,
        cx.nc.psum_tensor("yc1", [128, 512], F32) as yc1,
    ):
        cx.PT = PTb
        sb4 = (sc0, sc1, sc2, sc3)
        yb = (yc0, yc1)

        @block.sync
        def _(s):
            cx.inc(s.dma_start(out=msksb[:], in_=mskt), "ldm", 16)

        for p in range(4):
            @block.sync
            def _(s, p=p):
                if p >= 1:
                    cx.w(s, "aLp", cx.cnt["aLp"])
                    cx.w(s, "aLd", cx.cnt["aLd"])
                cx.inc(s.dma_start(out=Ksb[:], in_=Kh[p]), "ldK", 16)
                cx.inc(s.dma_start(out=Vsb[:], in_=Vh[p]), "ldK", 16)
                cx.inc(s.dma_start(out=qsb[:], in_=qh[p]), "ldK", 16)

            nldK = cx.cnt["ldK"]

            def wp(t, p=p, nldK=nldK):
                cx.w(t, "ldK", nldK)
                cx.w(t, "ldm", 16)

            def yst(s_eng, m, qs, yt, p=p):
                return s_eng.dma_start(
                    out=y_scr[256 * m + 128 * qs: 256 * m + 128 * qs + 128,
                              p * 256:(p + 1) * 256],
                    in_=yt[:, 0:256])

            emit_attention(cx, block, "aL", Ksb=Ksb, Vsb=Vsb, qsb=qsb,
                           msksb=msksb, idf32=idf,
                           sb4=sb4, yb=yb, ysb=ysb, ytok=ytok, rlt=rlt,
                           kcol_fn=lambda kb, hl: kb * 128,
                           vcol_fn=lambda kb, hl: kb * 65,
                           y_store_fn=yst, wait_pass_fn=wp)

        @block.sync
        def _(s):
            cx.w(s, "sty", cx.cnt["sty"])
            cx.w(s, "aLd", cx.cnt["aLd"])
            cx.w(s, "aLm", cx.cnt["aLm"])
            cx.w(s, "aLe", cx.cnt["aLe"])

        @block.vector
        def _(v):
            cx.w(v, "aLd", cx.cnt["aLd"])
            cx.w(v, "sty", cx.cnt["sty"])
            cx.inc(v.memset(rlt[:, 0:1], 0.0), "ph2", 1)

    es2 = ExitStack()
    gwo_sb = es2.enter_context(cx.nc.sbuf_tensor("gwo_sb", [128, 8 * 1024], BF16))
    yf = es2.enter_context(cx.nc.sbuf_tensor("yf", [128, 8 * TOK], BF16))
    ytl = es2.enter_context(cx.nc.sbuf_tensor("ytl", [128, 2 * 1024], F32))
    ytb = es2.enter_context(cx.nc.sbuf_tensor("ytb", [128, 1024], BF16))
    ut = es2.enter_context(cx.nc.sbuf_tensor("ut", [128, 1024], F32))
    xnt = es2.enter_context(cx.nc.sbuf_tensor("xnt", [128, 1024], F32))
    sq2 = es2.enter_context(cx.nc.sbuf_tensor("sq2", [128, 1024], F32))
    lcS2 = es2.enter_context(cx.nc.sbuf_tensor("lcS2", [128, 64], F32))
    lcR = es2.enter_context(cx.nc.sbuf_tensor("lcR", [128, 64], F32))
    xngt2 = es2.enter_context(cx.nc.sbuf_tensor("xngt2", [128, 1024], F32))
    ppc0 = es2.enter_context(cx.nc.psum_tensor("ppc0", [128, 512], F32))
    ppc1 = es2.enter_context(cx.nc.psum_tensor("ppc1", [128, 512], F32))
    ptc0 = es2.enter_context(cx.nc.psum_tensor("ptc0", [128, 1024], BF16))
    ptc1 = es2.enter_context(cx.nc.psum_tensor("ptc1", [128, 1024], BF16))
    if True:
        ppc = (ppc0, ppc1)
        ptc = (ptc0, ptc1)

        @block.gpsimd
        def _(g):
            cx.w(g, "ph2", 1)
            cx.inc(g.dma_start(out=gwo_sb[:].rearrange("p (i c) -> p i c", c=1024),
                               in_=g_wo.rearrange("(i p) c -> p i c", p=128)),
                   "lgw", 16)

        for j in range(NSLOT):
            @block.gpsimd
            def _(g, j=j):
                if j == 0:
                    cx.w(g, "ph2", 1)
                if j >= 1:
                    cx.w(g, "ctt", 8 * j)
                cx.inc(g.dma_start(out=ytb[:],
                                   in_=y_scr[j * 128:(j + 1) * 128, :]), "lyt", 16)

            @block.tensor
            def _(t, j=j):
                cx.w(t, "lyt", (j + 1) * 16)
                for ci in range(8):
                    gi = j * 8 + ci
                    if gi >= 2:
                        cx.w(t, "cte", gi - 1)
                    cx.inc(t.transpose(ptc[ci % 2][:, 0:128],
                                       ytb[:, ci * 128:(ci + 1) * 128],
                                       idb[:]), "ctt", 1)

            @block.vector
            def _(v, j=j):
                for ci in range(8):
                    cx.w(v, "ctt", j * 8 + ci + 1)
                    cx.inc(v.tensor_copy(
                        yf[:, ci * TOK + j * 128: ci * TOK + (j + 1) * 128],
                        ptc[ci % 2][:, 0:128]), "cte", 1)

        for j in range(NSLOT):
            @block.sync
            def _(s, j=j):
                if j >= 1:
                    cx.w(s, "cud", cx.cnt["cud"])
                cx.inc(s.dma_start(out=xnt[:],
                                   in_=xn1i[j * 128:(j + 1) * 128, :]), "lxn", 16)

            @block.tensor
            def _(t, j=j):
                if j == 0:
                    cx.w(t, "cte", NSLOT * 8)
                    cx.w(t, "lgw", 16)
                else:
                    cx.w(t, "cue", cx.cnt["cue"])
                for oc in range(2):
                    pairs = [(yf[:, ic * TOK + j * 128: ic * TOK + j * 128 + 128],
                              wchunked(gwo_sb, ic, 1024, oc * 512, 512))
                             for ic in range(8)]
                    mm_acc(cx, t, ppc[oc][:, 0:512], pairs, "cup")

            @block.vector
            def _(v, j=j):
                cx.w(v, "cup", cx.cnt["cup"])
                cx.w(v, "lxn", cx.cnt["lxn"])
                v.tensor_add(ut[:, 0:512], xnt[:, 0:512], ppc[0][:, 0:512])
                cx.inc(v.tensor_add(ut[:, 512:1024], xnt[:, 512:1024],
                                    ppc[1][:, 0:512]), "cue", 1)
                v.tensor_mul(sq2[:, 0:512], ut[:, 0:512], ut[:, 0:512])
                v.tensor_mul(sq2[:, 512:1024], ut[:, 512:1024], ut[:, 512:1024])
                cx.w(v, "cue", cx.cnt["cue"])
                cx.inc(v.reduce_sum(lcS2[:, 4 * j:4 * j + 1], sq2[:], AX.X),
                       "cud", 1)

            @block.sync
            def _(s, j=j):
                cx.w(s, "cud", cx.cnt["cud"])
                cx.inc(s.dma_start(out=o_u[j * 128:(j + 1) * 128, :], in_=ut[:]),
                       "ust", 16)

        @block.scalar
        def _(s):
            cx.w(s, "cud", NSLOT)
            s2v = lcS2[:].rearrange("p (j f) -> p j f", f=4)[:, :, 0]
            rv_ = lcR[:].rearrange("p (j f) -> p j f", f=4)[:, :, 0]
            r2v = lcR[:].rearrange("p (j f) -> p j f", f=4)[:, :, 1]
            sv = lcR[:].rearrange("p (j f) -> p j f", f=4)[:, :, 2]
            cx.inc(s.activation(out=r2v, in_=s2v, func=AF.Ln, scale=1.0 / 1024,
                                bias=eps6[:, 1:2]), "cla", 1)
            cx.w(s, "cla", cx.cnt["cla"])
            cx.inc(s.activation(out=rv_, in_=r2v, func=AF.Exp, scale=-0.5), "cla", 1)
            cx.w(s, "cla", cx.cnt["cla"])
            cx.inc(s.activation(out=sv, in_=rv_, func=AF.Copy), "cla", 1)

        for j in range(NSLOT):
            @block.sync
            def _(s, j=j):
                if j < 2:
                    cx.w(s, "ust", cx.cnt["ust"])
                else:
                    cx.w(s, "cgd", j - 1)
                cx.inc(s.dma_start(out=(ut if j % 2 == 0 else xnt)[:],
                                   in_=o_u[j * 128:(j + 1) * 128, :]), "lu2", 16)

            @block.vector
            def _(v, j=j):
                cx.w(v, "lu2", (j + 1) * 16)
                if j == 0:
                    cx.w(v, "cla", cx.cnt["cla"])
                if j >= 1:
                    cx.w(v, "cgt", 8 * j)
                uv = (ut if j % 2 == 0 else xnt)[:]
                v.tensor_scalar(xngt2[:], uv, lcR[:, 4 * j:4 * j + 1], None,
                                ALU.mult)
                cx.inc(v.tensor_mul(xngt2[:], xngt2[:], n2wb[:]), "cgd", 1)

            @block.tensor
            def _(t, j=j):
                cx.w(t, "cgd", cx.cnt["cgd"])
                for ci in range(8):
                    gi = j * 8 + ci
                    if gi >= 2:
                        cx.w(t, "cge", gi - 1)
                    cx.inc(t.transpose(ppc[ci % 2][0:128, 0:128],
                                       xngt2[:, ci * 128:(ci + 1) * 128],
                                       idf[:]), "cgt", 1)

            @block.vector
            def _(v, j=j):
                for ci in range(8):
                    cx.w(v, "cgt", j * 8 + ci + 1)
                    cx.inc(v.tensor_copy(
                        xng2f[:, ci * TOK + j * 128: ci * TOK + (j + 1) * 128],
                        ppc[ci % 2][:, 0:128]), "cge", 1)
    es2.close()

    es3 = ExitStack()
    lwq_sb = es3.enter_context(cx.nc.sbuf_tensor("lwq_sb", [128, 8 * 1024], BF16))
    lwk_sb = es3.enter_context(cx.nc.sbuf_tensor("lwk_sb", [128, 8 * 256], BF16))
    lwv_sb = es3.enter_context(cx.nc.sbuf_tensor("lwv_sb", [128, 8 * 256], BF16))
    ropeb = es3.enter_context(cx.nc.sbuf_tensor("ropebC", [128, 2 * 512], F32))
    ropebk = es3.enter_context(cx.nc.sbuf_tensor("ropebkC", [128, 2 * 128], F32))
    qtw = es3.enter_context(cx.nc.sbuf_tensor("qtwC", [128, 1024], F32))
    qtn = es3.enter_context(cx.nc.sbuf_tensor("qtnC", [128, 1024], F32))
    ktw = es3.enter_context(cx.nc.sbuf_tensor("ktwC", [128, 256], F32))
    ktn = es3.enter_context(cx.nc.sbuf_tensor("ktnC", [128, 256], F32))
    rA = es3.enter_context(cx.nc.sbuf_tensor("rAC", [128, 512], F32))
    rB = es3.enter_context(cx.nc.sbuf_tensor("rBC", [128, 512], F32))
    hsqf = es3.enter_context(cx.nc.sbuf_tensor("hsqfC", [128, 1024], F32))
    hst = es3.enter_context(cx.nc.sbuf_tensor("hstC", [128, 48], F32))
    qrb = es3.enter_context(cx.nc.sbuf_tensor("qrbC", [128, 1024], BF16))
    krb = es3.enter_context(cx.nc.sbuf_tensor("krbC", [128, 256], BF16))
    vtb = es3.enter_context(cx.nc.sbuf_tensor("vtbC", [128, 256], BF16))
    pq0 = es3.enter_context(cx.nc.psum_tensor("pqc0", [128, 512], F32))
    pq1 = es3.enter_context(cx.nc.psum_tensor("pqc1", [128, 512], F32))
    pq2 = es3.enter_context(cx.nc.psum_tensor("pqc2", [128, 512], F32))
    pq3 = es3.enter_context(cx.nc.psum_tensor("pqc3", [128, 512], F32))
    if True:
        pqq = (pq0, pq1)

        @block.gpsimd
        def _(g):
            cx.w(g, "cge", cx.cnt["cge"])
            cx.w(g, "ust", cx.cnt["ust"])
            cx.inc(g.dma_start(out=lwq_sb[:].rearrange("p (i c) -> p i c", c=1024),
                               in_=l_wq.rearrange("(i p) c -> p i c", p=128)), "llw", 16)
            cx.inc(g.dma_start(out=lwk_sb[:].rearrange("p (i c) -> p i c", c=256),
                               in_=l_wk.rearrange("(i p) c -> p i c", p=128)), "llw", 16)
            cx.inc(g.dma_start(out=lwv_sb[:].rearrange("p (i c) -> p i c", c=256),
                               in_=l_wv.rearrange("(i p) c -> p i c", p=128)), "llw", 16)

        for j in range(NSLOT):
            @block.sync
            def _(s, j=j):
                if j >= 1:
                    cx.w(s, "rpd", cx.cnt["rpd"])
                cx.inc(s.dma_start(out=ropeb[:, 0:512],
                                   in_=cl512[j * 128:(j + 1) * 128, :]), "lrp", 16)
                cx.inc(s.dma_start(out=ropeb[:, 512:1024],
                                   in_=sl512[j * 128:(j + 1) * 128, :]), "lrp", 16)
                cx.inc(s.dma_start(out=ropebk[:, 0:128],
                                   in_=cl128[j * 128:(j + 1) * 128, :]), "lrp", 16)
                cx.inc(s.dma_start(out=ropebk[:, 128:256],
                                   in_=sl128[j * 128:(j + 1) * 128, :]), "lrp", 16)

            @block.tensor
            def _(t, j=j):
                if j == 0:
                    cx.w(t, "llw", 48)
                else:
                    cx.w(t, "pre", cx.cnt["pre"])
                for oc in range(2):
                    pairs = [(xng2f[:, ic * TOK + j * 128: ic * TOK + j * 128 + 128],
                              wchunked(lwq_sb, ic, 1024, oc * 512, 512))
                             for ic in range(8)]
                    mm_acc(cx, t, pqq[oc][:, 0:512], pairs, "prp")
                pairs = [(xng2f[:, ic * TOK + j * 128: ic * TOK + j * 128 + 128],
                          wchunked(lwk_sb, ic, 256, 0, 256)) for ic in range(8)]
                mm_acc(cx, t, pq2[:, 0:256], pairs, "prp")
                pairs = [(xng2f[:, ic * TOK + j * 128: ic * TOK + j * 128 + 128],
                          wchunked(lwv_sb, ic, 256, 0, 256)) for ic in range(8)]
                mm_acc(cx, t, pq3[:, 0:256], pairs, "prp")

            @block.vector
            def _(v, j=j):
                cx.w(v, "prp", cx.cnt["prp"])
                cx.w(v, "lrp", cx.cnt["lrp"])
                if j >= 1:
                    cx.w(v, "stqkv", cx.cnt["stqkv"])
                v.tensor_copy(qtw[:, 0:512], pqq[0][:, 0:512])
                v.tensor_copy(qtw[:, 512:1024], pqq[1][:, 0:512])
                v.tensor_copy(ktw[:], pq2[:, 0:256])
                cx.inc(v.tensor_copy(vtb[:], pq3[:, 0:256]), "pre", 1)
                emit_rope(v, qtn, qtw, ropeb[:, 0:512], ropeb[:, 512:1024],
                          16, rA, rB)
                emit_rope(v, ktn, ktw, ropebk[:, 0:128], ropebk[:, 128:256],
                          4, rA, rB)
                v.tensor_mul(hsqf[:], qtn[:], qtn[:])
                cx.inc(v.tensor_reduce(
                    hst[:, 0:16], hsqf[:].rearrange("p (h d) -> p h d", d=64),
                    AX.X, ALU.add), "rpd", 1)
                v.tensor_mul(hsqf[:, 0:256], ktn[:], ktn[:])
                cx.inc(v.tensor_reduce(
                    hst[:, 16:20],
                    hsqf[:, 0:256].rearrange("p (h d) -> p h d", d=64),
                    AX.X, ALU.add), "rpd", 1)

            @block.scalar
            def _(s, j=j):
                cx.w(s, "rpd", cx.cnt["rpd"])
                s.activation(out=hst[:, 20:40], in_=hst[:, 0:20], func=AF.Ln,
                             scale=1.0 / 64, bias=eps6[:, 1:2])
                cx.inc(s.activation(out=hst[:, 20:40], in_=hst[:, 20:40],
                                    func=AF.Exp, scale=-0.5), "rpa", 1)

            @block.vector
            def _(v, j=j):
                cx.w(v, "rpa", cx.cnt["rpa"])
                for h in range(16):
                    v.tensor_scalar(qrb[:, h * 64:(h + 1) * 64],
                                    qtn[:, h * 64:(h + 1) * 64],
                                    hst[:, 20 + h: 21 + h], None, ALU.mult)
                for h in range(4):
                    v.tensor_scalar(krb[:, h * 64:(h + 1) * 64],
                                    ktn[:, h * 64:(h + 1) * 64],
                                    hst[:, 36 + h: 37 + h], None, ALU.mult)
                cx.inc(v.tensor_copy(krb[:, 0:1], krb[:, 0:1]), "rpd", 1)

            @block.sync
            def _(s, j=j):
                cx.w(s, "rpd", cx.cnt["rpd"])
                cx.w(s, "pre", cx.cnt["pre"])
                r0 = j * 128
                cx.inc(s.dma_start(out=o_q2r[r0:r0 + 128, :], in_=qrb[:]), "stqkv", 16)
                cx.inc(s.dma_start(out=o_k2r[r0:r0 + 128, :], in_=krb[:]), "stqkv", 16)
                cx.inc(s.dma_start(out=o_v2[r0:r0 + 128, :], in_=vtb[:]), "stqkv", 16)

        @block.sync
        def _(s):
            cx.w(s, "stqkv", cx.cnt["stqkv"])
            cx.w(s, "ust", cx.cnt["ust"])
    es3.close()

    cx.close()
    return cx




def build_D(upto="full"):
    nslA = NSLOT
    phase = "all"
    if upto.startswith("attn:"):
        ps = upto.split(":")
        nslA = int(ps[1])
        phase = ps[2] if len(ps) > 2 else "all"
        upto = "attn"
    do_exp = phase == "all" or "e" in phase
    do_msk = phase == "all" or "m" in phase
    do_pv = phase == "all" or "p" in phase
    do_fin = phase == "all"
    cx = Cx("D")
    x1i = cx.din("x1i", [TOK, D], F32)
    ui = cx.din("ui", [TOK, D], F32)
    xn1fi = cx.din("xn1fi", [128, 8 * TOK], BF16)
    qh = [cx.din(f"qh{p}", [64, 4 * TOK], BF16) for p in range(4)]
    Kw = [cx.din(f"Kw{p}", [64, NSLOT * 3 * 128], BF16) for p in range(4)]
    Vw = [cx.din(f"Vw{p}", [128, NSLOT * 3 * 65], BF16) for p in range(4)]
    msklt = cx.din("msklt", [128, NSLOT * 3 * 128], BF16)
    l_wo = cx.din("l_wo", [1024, 1024], F32)
    gate_wT = cx.din("gate_wT", [128, 8 * 8], BF16)
    gate_b = cx.din("gate_b", [1, 8], F32)
    w1 = cx.din("w1", [8, 1024, 1024], F32)
    w2 = cx.din("w2", [8, 1024, 1024], F32)
    w3 = cx.din("w3", [8, 1024, 1024], F32)
    b1i = cx.din("b1i", [8, 128, 8], F32)
    b2i = cx.din("b2i", [8, 128, 8], F32)
    b3i = cx.din("b3i", [8, 1, 1024], F32)
    iota8 = cx.din("iota8", [128, 8], F32)
    cst = cx.din("cst", [128, 4], F32)
    identf = cx.din("identf", [128, 128], F32)
    identb = cx.din("identb", [128, 128], BF16)
    o_x2 = cx.dout("x2", [TOK, D], F32)
    yl_scr = cx.nc.dram_tensor("yl_scr", [TOK, D], F32, kind="Internal").ap()

    idf = cx.sb("idfD", [128, 128], F32)
    idb = cx.sb("idbD", [128, 128], BF16)
    io8 = cx.sb("io8", [128, 8], F32)
    cstb = cx.sb("cstb", [128, 4], F32)
    gwT = cx.sb("gwT", [128, 8 * 8], BF16)
    gbb = cx.sb("gbb", [128, 8], F32)
    gbr = cx.sb("gbr", [1, 8], F32)
    onesrf = cx.sb("onesrfD", [1, 128], F32)
    cwt = cx.sb("cwt", [128, 2 * 4 * 8], F32)
    block = cx.block()

    @block.sync
    def _(s):
        cx.inc(s.dma_start(out=idf[:], in_=identf), "ldc", 16)
        cx.inc(s.dma_start(out=idb[:], in_=identb), "ldc", 16)
        cx.inc(s.dma_start(out=io8[:], in_=iota8), "ldc", 16)
        cx.inc(s.dma_start(out=cstb[:], in_=cst), "ldc", 16)
        cx.inc(s.dma_start(out=gwT[:], in_=gate_wT), "ldc", 16)
        cx.inc(s.dma_start(out=gbr[:], in_=gate_b), "ldc", 16)

    @block.vector
    def _(v):
        cx.inc(v.memset(onesrf[:], 1.0), "ini", 1)

    with cx.nc.psum_tensor("psetD", [128, 512], F32) as pset:
        @block.tensor
        def _(t):
            cx.w(t, "ldc")
            cx.w(t, "ini", 1)
            cx.inc(t.matmul(pset[:, 0:8], onesrf[0:1, :], gbr[0:1, :],
                            start=True, stop=True), "bps", 1)

        @block.vector
        def _(v):
            cx.w(v, "bps", 1)
            cx.inc(v.tensor_copy(gbb[:], pset[:, 0:8]), "bse", 1)

    with (
        cx.nc.sbuf_tensor("KsbD", [64, 4 * NSLOT * 3 * 128], BF16) as Ksb,
        cx.nc.sbuf_tensor("VsbD", [128, 4 * NSLOT * 3 * 65], BF16) as Vsb,
        cx.nc.sbuf_tensor("qsbD", [64, 4 * 4 * TOK], BF16) as qsb,
        cx.nc.sbuf_tensor("mskDD", [128, NSLOT * 3 * 128], BF16) as msksb,
        cx.nc.sbuf_tensor("PTD", [128, 2 * 512], BF16) as PTb,
        cx.nc.sbuf_tensor("ysbDD", [128, 2048], F32) as ysb,
        cx.nc.sbuf_tensor("ytokDD", [128, 1024], F32) as ytok,
        cx.nc.sbuf_tensor("rltDD", [128, 2], F32) as rlt,
        cx.nc.psum_tensor("sdd0", [128, 512], F32) as sd0,
        cx.nc.psum_tensor("sdd1", [128, 512], F32) as sd1,
        cx.nc.psum_tensor("sdd2", [128, 512], F32) as sd2,
        cx.nc.psum_tensor("sdd3", [128, 512], F32) as sd3,
        cx.nc.psum_tensor("ydd0", [128, 512], F32) as yd0,
        cx.nc.psum_tensor("ydd1", [128, 512], F32) as yd1,
        cx.nc.psum_tensor("ydd2", [128, 512], F32) as yd2,
        cx.nc.psum_tensor("ydd3", [128, 512], F32) as yd3,
    ):
        sb4 = (sd0, sd1, sd2, sd3)
        ybk = (yd0, yd1, yd2, yd3)

        @block.sync
        def _(s):
            cx.inc(s.dma_start(out=msksb[:], in_=msklt), "ldm", 16)
            for p in range(4):
                cx.inc(s.dma_start(
                    out=Ksb[0:64, p * NSLOT * 384: (p + 1) * NSLOT * 384],
                    in_=Kw[p]), "ldK", 16)
                cx.inc(s.dma_start(
                    out=Vsb[:, p * NSLOT * 195: (p + 1) * NSLOT * 195],
                    in_=Vw[p]), "ldK", 16)
                cx.inc(s.dma_start(out=qsb[0:64, p * 4 * TOK: (p + 1) * 4 * TOK],
                                   in_=qh[p]), "ldK", 16)

        for j in range(nslA):
            sS_base = cx.cnt.get("dS", 0)
            dE_base = cx.cnt.get("dE", 0)
            dM_base = cx.cnt.get("dM", 0)
            dP_base = cx.cnt.get("dP", 0)
            dT_base = cx.cnt.get("dT", 0)
            dD_base = cx.cnt.get("dD", 0)
            dY_base = cx.cnt.get("dY", 0)
            q0 = j * 128

            @block.tensor
            def _(t, j=j, q0=q0, sS_base=sS_base, dE_base=dE_base,
                  dM_base=dM_base, dD_base=dD_base, dY_base=dY_base):
                if j == 0:
                    cx.w(t, "ldK", cx.cnt["ldK"])
                    cx.w(t, "ldm", 16)
                    cx.w(t, "bse", 1)
                elif do_fin:
                    cx.w(t, "dY", dY_base)
                    cx.w(t, "dD", dD_base)
                for p in range(4):
                    for kb in range(3):
                        for hl in range(4):
                            i = (p * 3 + kb) * 4 + hl
                            if sS_base + i >= 4 and do_exp:
                                cx.w(t, "dE", sS_base + i - 4 + 1)
                            ins = t.matmul(
                                sb4[(sS_base + i) % 4][:, 0:128],
                                Ksb[0:64, p * NSLOT * 384 + (j * 3 + kb) * 128:
                                    p * NSLOT * 384 + (j * 3 + kb) * 128 + 128],
                                qsb[0:64, p * 4 * TOK + hl * TOK + q0:
                                    p * 4 * TOK + hl * TOK + q0 + 128],
                                start=True, stop=True)
                            cx.inc(ins, "dS", 1)
                        for hl in (range(4) if do_pv else []):
                            i = (p * 3 + kb) * 4 + hl
                            cx.w(t, "dM", dM_base + i + 1)
                            first = (kb == 0 and hl == 0)
                            last = (kb == 2 and hl == 3)
                            ins = t.matmul(
                                ybk[p][0:65, hl * 128: hl * 128 + 128],
                                Vsb[:, p * NSLOT * 195 + (j * 3 + kb) * 65:
                                    p * NSLOT * 195 + (j * 3 + kb) * 65 + 65],
                                PTb[:, (i % 2) * 512 + hl * 128:
                                    (i % 2) * 512 + hl * 128 + 128],
                                start=first, stop=last)
                            cx.inc(ins, "dP", 1)

            @block.scalar
            def _(s, j=j, sS_base=sS_base, dP_base=dP_base):
                for p in (range(4) if do_exp else []):
                    for kb in range(3):
                        for hl in range(4):
                            i = (p * 3 + kb) * 4 + hl
                            if (dP_base > 0 or i >= 4) and do_pv:
                                cx.w(s, "dP", dP_base + max(i - 3, 0))
                            cx.w(s, "dS", sS_base + i + 1)
                            ins = s.activation(
                                out=PTb[:, (i % 2) * 512 + hl * 128:
                                        (i % 2) * 512 + hl * 128 + 128],
                                in_=sb4[(sS_base + i) % 4][:, 0:128],
                                func=AF.Exp, scale=0.125)
                            cx.inc(ins, "dE", 1)

            @block.vector
            def _(v, j=j, dE_base=dE_base):
                for p in (range(4) if do_msk else []):
                    for kb in range(3):
                        for hl in range(4):
                            i = (p * 3 + kb) * 4 + hl
                            cx.w(v, "dE", dE_base + i + 1)
                            off = (i % 2) * 512 + hl * 128
                            cx.inc(v.tensor_mul(
                                PTb[:, off: off + 128], PTb[:, off: off + 128],
                                msksb[:, (j * 3 + kb) * 128: (j * 3 + kb) * 128 + 128]),
                                "dM", 1)

            @block.vector
            def _(v, j=j, dP_base=dP_base):
                if not do_fin:
                    return
                cx.w(v, "dP", dP_base + 48)
                for p in range(4):
                    cx.inc(v.tensor_copy(ysb[0:65, p * 512:(p + 1) * 512],
                                         ybk[p][0:65, 0:512]), "dY", 1)

            @block.tensor
            def _(t, j=j, dY_base=dY_base, dT_base=dT_base, dD_base=dD_base):
                if not do_fin:
                    return
                cx.w(t, "dY", dY_base + 4)
                for h in range(16):
                    if dT_base + h >= 2:
                        cx.w(t, "dD", dT_base + h - 2 + 1)
                    cx.inc(t.transpose(sb4[h % 2][0:128, 0:65],
                                       ysb[0:65, h * 128: h * 128 + 128],
                                       idf[0:65, 0:65]), "dT", 1)

            @block.vector
            def _(v, j=j, dT_base=dT_base):
                if not do_fin:
                    return
                if j >= 1:
                    cx.w(v, "dst", 16 * j)
                for h in range(16):
                    cx.w(v, "dT", dT_base + h + 1)
                    cx.inc(v.reciprocal(rlt[:, 0:1], sb4[h % 2][:, 64:65]), "dre", 1)
                    cx.w(v, "dre", cx.cnt["dre"])
                    cx.inc(v.tensor_scalar(ytok[:, h * 64: h * 64 + 64],
                                           sb4[h % 2][:, 0:64], rlt[:, 0:1], None,
                                           ALU.mult), "dD", 1)

            @block.sync
            def _(s, j=j, dD_base=dD_base):
                if not do_fin:
                    return
                cx.w(s, "dD", dD_base + 16)
                cx.inc(s.dma_start(out=yl_scr[j * 128:(j + 1) * 128, :],
                                   in_=ytok[:]), "dst", 16)
                if upto == "attn":
                    cx.inc(s.dma_start(out=o_x2[j * 128:(j + 1) * 128, :],
                                       in_=ytok[:]), "dst", 16)

        @block.sync
        def _(s):
            cx.w(s, "dst", cx.cnt.get("dst", 0))

        @block.vector
        def _(v):
            cx.w(v, "dD", cx.cnt.get("dD", 0))
            cx.w(v, "dst", cx.cnt.get("dst", 0))
            cx.inc(v.memset(rlt[:, 0:1], 0.0), "ph2", 1)

    if upto == "attn":
        @block.sync
        def _(s):
            cx.w(s, "dst", cx.cnt.get("dst", 0))
            cx.w(s, "ldK", cx.cnt.get("ldK", 0))
            cx.w(s, "ldm", cx.cnt.get("ldm", 0))
            cx.w(s, "ldc", cx.cnt.get("ldc", 0))
        cx.close()
        return cx

    es2 = ExitStack()
    xn1sb = es2.enter_context(cx.nc.sbuf_tensor("xn1sb", [128, 8 * TOK], BF16))
    lwo_sb = es2.enter_context(cx.nc.sbuf_tensor("lwo_sb", [128, 8 * 1024], BF16))
    ylf = es2.enter_context(cx.nc.sbuf_tensor("ylf", [128, 8 * TOK], BF16))
    ytb = es2.enter_context(cx.nc.sbuf_tensor("ytbD", [128, 1024], BF16))
    mw0 = es2.enter_context(cx.nc.sbuf_tensor("mwD0", [128, 8 * 1024], BF16))
    mw1 = es2.enter_context(cx.nc.sbuf_tensor("mwD1", [128, 8 * 1024], BF16))
    h1f = es2.enter_context(cx.nc.sbuf_tensor("h1fD", [128, 2 * 8 * 512], BF16))
    h2f = es2.enter_context(cx.nc.sbuf_tensor("h2fD", [128, 2 * 8 * 512], BF16))
    facc = es2.enter_context(cx.nc.sbuf_tensor("faccD", [128, 2 * 4 * 1024], F32))
    b1b = es2.enter_context(cx.nc.sbuf_tensor("b1bD", [128, 8], F32))
    b2b = es2.enter_context(cx.nc.sbuf_tensor("b2bD", [128, 8], F32))
    b3r = es2.enter_context(cx.nc.sbuf_tensor("b3rD", [1, 1024], F32))
    b3b = es2.enter_context(cx.nc.sbuf_tensor("b3bD", [128, 1024], F32))
    gpro = es2.enter_context(cx.nc.sbuf_tensor("gproD", [128, 8], F32))
    gtmp = es2.enter_context(cx.nc.sbuf_tensor("gtmpD", [128, 8], F32))
    gst = es2.enter_context(cx.nc.sbuf_tensor("gstD", [128, 16], F32))
    t8 = es2.enter_context(cx.nc.sbuf_tensor("t8D", [128, 8], F32))
    I1 = es2.enter_context(cx.nc.sbuf_tensor("I1D", [128, 8], F32))
    I2 = es2.enter_context(cx.nc.sbuf_tensor("I2D", [128, 8], F32))
    pex = es2.enter_context(cx.nc.sbuf_tensor("pexD", [128, 8], F32))
    xt1 = es2.enter_context(cx.nc.sbuf_tensor("xt1D", [128, 1024], F32))
    xt2 = es2.enter_context(cx.nc.sbuf_tensor("xt2D", [128, 1024], F32))
    ppd0 = es2.enter_context(cx.nc.psum_tensor("ppd0", [128, 512], F32))
    ppd1 = es2.enter_context(cx.nc.psum_tensor("ppd1", [128, 512], F32))
    ptd0 = es2.enter_context(cx.nc.psum_tensor("ptd0", [128, 1024], BF16))
    ptd1 = es2.enter_context(cx.nc.psum_tensor("ptd1", [128, 1024], BF16))
    pg = es2.enter_context(cx.nc.psum_tensor("pgD", [128, 512], F32))
    if True:
        ppd = (ppd0, ppd1)
        ptd = (ptd0, ptd1)
        mws = (mw0, mw1)
        stage_done = []
        stage_no = [0]

        @block.gpsimd
        def _(g):
            cx.w(g, "ph2", 1)
            cx.inc(g.dma_start(out=lwo_sb[:].rearrange("p (i c) -> p i c", c=1024),
                               in_=l_wo.rearrange("(i p) c -> p i c", p=128)),
                   "lwo", 16)

        @block.sync
        def _(s):
            cx.w(s, "dst", cx.cnt["dst"])
            cx.inc(s.dma_start(out=xn1sb[:].rearrange("p (i t) -> p i t", t=TOK),
                               in_=xn1fi.rearrange("p (i t) -> p i t", t=TOK)),
                   "lxn", 16)

        for j in range(NSLOT):
            @block.gpsimd
            def _(g, j=j):
                if j >= 1:
                    cx.w(g, "dtt", 8 * j)
                cx.inc(g.dma_start(out=ytb[:],
                                   in_=yl_scr[j * 128:(j + 1) * 128, :]), "lyt", 16)

            @block.tensor
            def _(t, j=j):
                cx.w(t, "lyt", (j + 1) * 16)
                for ci in range(8):
                    gi = j * 8 + ci
                    if gi >= 2:
                        cx.w(t, "dte", gi - 1)
                    cx.inc(t.transpose(ptd[ci % 2][:, 0:128],
                                       ytb[:, ci * 128:(ci + 1) * 128],
                                       idb[:]), "dtt", 1)

            @block.vector
            def _(v, j=j):
                for ci in range(8):
                    cx.w(v, "dtt", j * 8 + ci + 1)
                    cx.inc(v.tensor_copy(
                        ylf[:, ci * TOK + j * 128: ci * TOK + (j + 1) * 128],
                        ptd[ci % 2][:, 0:128]), "dte", 1)

        for tp in range(2):
            for tl in range(2):
                tb = (tp * 2 + tl) * 512

                @block.tensor
                def _(t, tb=tb):
                    cx.w(t, "lxn", 16)
                    cx.w(t, "bse", 1)
                    if cx.cnt.get("gte", 0) > 0:
                        cx.w(t, "gte", cx.cnt["gte"])
                    if cx.cnt.get("b3e", 0) > 0:
                        cx.w(t, "b3e", cx.cnt["b3e"])
                    for ts in range(4):
                        for ic in range(8):
                            ins = t.matmul(
                                pg[:, ts * 8: ts * 8 + 8],
                                xn1sb[:, ic * TOK + tb + ts * 128:
                                      ic * TOK + tb + ts * 128 + 128],
                                gwT[:, ic * 8:(ic + 1) * 8],
                                start=(ts == 0 and ic == 0),
                                stop=(ts == 3 and ic == 7))
                            if ts == 3 and ic == 7:
                                cx.inc(ins, "gtp", 1)

                for ts in range(4):
                    @block.vector
                    def _(v, ts=ts):
                        cx.w(v, "gtp", cx.cnt["gtp"])
                        cx.w(v, "bse", 1)
                        cx.inc(v.tensor_add(gtmp[:], pg[:, ts * 8: ts * 8 + 8],
                                            gbb[:]), "gtd", 1)

                    @block.scalar
                    def _(s, ts=ts):
                        cx.w(s, "gtd", cx.cnt["gtd"])
                        cx.inc(s.activation(out=gpro[:], in_=gtmp[:], func=AF.Exp,
                                            accum_out=gst[:, 0:1]), "gta", 1)
                        cx.inc(s.activation(out=gst[:, 4:5], in_=gst[:, 0:1],
                                            func=AF.Copy), "gta", 1)

                    @block.vector
                    def _(v, tl=tl, ts=ts):
                        cx.w(v, "gta", cx.cnt["gta"])
                        cx.inc(v.reciprocal(gst[:, 8:9], gst[:, 4:5]), "gtd", 1)
                        cx.w(v, "gtd", cx.cnt["gtd"])
                        cx.inc(v.tensor_scalar(gpro[:], gpro[:], gst[:, 8:9], None,
                                               ALU.mult), "gtd", 1)
                        cx.w(v, "gtd", cx.cnt["gtd"])
                        cx.inc(v.reduce_max(gst[:, 12:13], gpro[:], AX.X), "gtd", 1)
                        cx.w(v, "gtd", cx.cnt["gtd"])
                        v.tensor_scalar(t8[:], gpro[:], gst[:, 12:13], None,
                                        ALU.is_ge)
                        v.scalar_tensor_tensor(t8[:], t8[:], cstb[:, 0:1], io8[:],
                                               ALU.mult, ALU.add)
                        cx.inc(v.tensor_scalar(t8[:], t8[:], cstb[:, 1:2], None,
                                               ALU.add), "gtd", 1)
                        cx.w(v, "gtd", cx.cnt["gtd"])
                        cx.inc(v.tensor_reduce(gst[:, 13:14], t8[:], AX.X, ALU.min),
                               "gtd", 1)
                        cx.w(v, "gtd", cx.cnt["gtd"])
                        v.tensor_scalar(I1[:], io8[:], gst[:, 13:14], None,
                                        ALU.is_equal)
                        cx.inc(v.scalar_tensor_tensor(pex[:], I1[:], cstb[:, 2:3],
                                                      gpro[:], ALU.mult, ALU.add),
                               "gtd", 1)
                        cx.w(v, "gtd", cx.cnt["gtd"])
                        cx.inc(v.reduce_max(gst[:, 14:15], pex[:], AX.X), "gtd", 1)
                        cx.w(v, "gtd", cx.cnt["gtd"])
                        v.tensor_scalar(t8[:], pex[:], gst[:, 14:15], None,
                                        ALU.is_ge)
                        v.scalar_tensor_tensor(t8[:], t8[:], cstb[:, 0:1], io8[:],
                                               ALU.mult, ALU.add)
                        cx.inc(v.tensor_scalar(t8[:], t8[:], cstb[:, 1:2], None,
                                               ALU.add), "gtd", 1)
                        cx.w(v, "gtd", cx.cnt["gtd"])
                        cx.inc(v.tensor_reduce(gst[:, 15:16], t8[:], AX.X, ALU.min),
                               "gtd", 1)
                        cx.w(v, "gtd", cx.cnt["gtd"])
                        v.tensor_scalar(I2[:], io8[:], gst[:, 15:16], None,
                                        ALU.is_equal)
                        v.tensor_scalar(gtmp[:], I1[:], gpro[:, 0:1], None,
                                        ALU.mult)
                        cx.inc(v.scalar_tensor_tensor(
                            cwt[:, tl * 32 + ts * 8: tl * 32 + (ts + 1) * 8],
                            I2[:], gpro[:, 1:2], gtmp[:], ALU.mult, ALU.add),
                            "gte", 1)

            for e in (range(8) if upto == "full" else []):
                st1 = stage_no[0]
                stage_no[0] += 1

                @block.gpsimd
                def _(g, e=e, st=st1):
                    if st >= 2:
                        sn, sv = stage_done[st - 2]
                        cx.w(g, sn, sv)
                    cx.inc(g.dma_start(
                        out=mws[st % 2][:].rearrange("p (i c) -> p i c", c=1024),
                        in_=w1[e].rearrange("(i p) c -> p i c", p=128)), "lmw", 16)
                    cx.inc(g.dma_start(out=b1b[:], in_=b1i[e]), "lmw", 16)

                e1a_base = cx.cnt.get("e1a", 0)

                @block.tensor
                def _(t, tp=tp, e=e):
                    cx.w(t, "lmw", cx.cnt["lmw"])
                    if cx.cnt.get("f3d", 0) > 0:
                        cx.w(t, "f3d", cx.cnt["f3d"])
                    if cx.cnt.get("x2d", 0) > 0:
                        cx.w(t, "x2d", cx.cnt["x2d"])
                    for ji in range(16):
                        tl, oc = divmod(ji, 8)
                        tb = (tp * 2 + tl) * 512
                        if ji >= 2:
                            cx.w(t, "e1a", e1a_base + ji - 1)
                        elif cx.cnt.get("e1a", 0) > 0:
                            cx.w(t, "e1a", e1a_base)
                        pairs = [(wchunked(mws[st1 % 2], ic, 1024, oc * 128, 128),
                                  xn1sb[:, ic * TOK + tb: ic * TOK + tb + 512])
                                 for ic in range(8)]
                        mm_acc(cx, t, ppd[ji % 2][:, 0:512], pairs, "e1p")
                stage_done.append(("e1p", cx.cnt["e1p"]))

                @block.scalar
                def _(s, e=e):
                    for ji in range(16):
                        tl, oc = divmod(ji, 8)
                        cx.w(s, "e1p", cx.cnt["e1p"] - 15 + ji)
                        cx.inc(s.activation(
                            out=h1f[:, tl * 4096 + oc * 512:
                                    tl * 4096 + oc * 512 + 512],
                            in_=ppd[ji % 2][:, 0:512],
                            func=AF.Relu, bias=b1b[:, oc:oc + 1]), "e1a", 1)

                st2 = stage_no[0]
                stage_no[0] += 1

                @block.gpsimd
                def _(g, e=e, st=st2):
                    if st >= 2:
                        sn, sv = stage_done[st - 2]
                        cx.w(g, sn, sv)
                    cx.inc(g.dma_start(
                        out=mws[st % 2][:].rearrange("p (i c) -> p i c", c=1024),
                        in_=w2[e].rearrange("(i p) c -> p i c", p=128)), "lm2", 16)
                    cx.inc(g.dma_start(out=b2b[:], in_=b2i[e]), "lm2", 16)

                e2a_base = cx.cnt.get("e2a", 0)

                @block.tensor
                def _(t, e=e):
                    cx.w(t, "lm2", cx.cnt["lm2"])
                    cx.w(t, "e1a", cx.cnt["e1a"])
                    for ji in range(16):
                        tl, oc = divmod(ji, 8)
                        if ji >= 2:
                            cx.w(t, "e2a", e2a_base + ji - 1)
                        elif cx.cnt.get("e2a", 0) > 0:
                            cx.w(t, "e2a", e2a_base)
                        pairs = [(wchunked(mws[st2 % 2], ic, 1024, oc * 128, 128),
                                  h1f[:, tl * 4096 + ic * 512:
                                      tl * 4096 + ic * 512 + 512])
                                 for ic in range(8)]
                        mm_acc(cx, t, ppd[ji % 2][:, 0:512], pairs, "e2p")
                stage_done.append(("e2p", cx.cnt["e2p"]))

                @block.scalar
                def _(s, e=e):
                    for ji in range(16):
                        tl, oc = divmod(ji, 8)
                        cx.w(s, "e2p", cx.cnt["e2p"] - 15 + ji)
                        cx.inc(s.activation(
                            out=h2f[:, tl * 4096 + oc * 512:
                                    tl * 4096 + oc * 512 + 512],
                            in_=ppd[ji % 2][:, 0:512],
                            func=AF.Relu, bias=b2b[:, oc:oc + 1]), "e2a", 1)

                st3 = stage_no[0]
                stage_no[0] += 1

                @block.gpsimd
                def _(g, e=e, st=st3):
                    if st >= 2:
                        sn, sv = stage_done[st - 2]
                        cx.w(g, sn, sv)
                    cx.inc(g.dma_start(
                        out=mws[st % 2][:].rearrange("p (i c) -> p i c", c=1024),
                        in_=w3[e].rearrange("(i p) c -> p i c", p=128)), "lm3", 16)
                    cx.inc(g.dma_start(out=b3r[:], in_=b3i[e]), "lm3", 16)

                b3e_base = cx.cnt.get("b3e", 0)
                for hf in range(2):
                    @block.tensor
                    def _(t, e=e, hf=hf):
                        if hf == 0:
                            cx.w(t, "lm3", cx.cnt["lm3"])
                            cx.w(t, "gte", cx.cnt["gte"])
                            if b3e_base > 0:
                                cx.w(t, "b3e", b3e_base)
                        else:
                            cx.w(t, "b3e", cx.cnt["b3e"])
                        cx.inc(t.matmul(pg[:, 0:512], onesrf[0:1, :],
                                        b3r[0:1, hf * 512:(hf + 1) * 512],
                                        start=True, stop=True), "b3p", 1)

                    @block.vector
                    def _(v, e=e, hf=hf):
                        cx.w(v, "b3p", cx.cnt["b3p"])
                        cx.inc(v.tensor_copy(b3b[:, hf * 512:(hf + 1) * 512],
                                             pg[:, 0:512]), "b3e", 1)

                e3p_base = cx.cnt.get("e3p", 0)
                f3d_base = cx.cnt.get("f3d", 0)
                jobs3 = [(tl, ts, half) for tl in range(2)
                         for ts in range(4) for half in range(2)]

                @block.tensor
                def _(t, e=e):
                    cx.w(t, "e2a", cx.cnt["e2a"])
                    for ji, (tl, ts, half) in enumerate(jobs3):
                        if ji >= 2:
                            cx.w(t, "f3d", f3d_base + ji - 1)
                        pairs = [(h2f[:, tl * 4096 + ic * 512 + ts * 128:
                                      tl * 4096 + ic * 512 + ts * 128 + 128],
                                  wchunked(mws[st3 % 2], ic, 1024, half * 512, 512))
                                 for ic in range(8)]
                        mm_acc(cx, t, ppd[ji % 2][:, 0:512], pairs, "e3p")
                stage_done.append(("e3p", cx.cnt["e3p"]))

                @block.vector
                def _(v, e=e):
                    if e == 0 and cx.cnt.get("x2d", 0) > 0:
                        cx.w(v, "x2d", cx.cnt["x2d"])
                    cx.w(v, "b3e", cx.cnt["b3e"])
                    cx.w(v, "gte", cx.cnt["gte"])
                    for ji, (tl, ts, half) in enumerate(jobs3):
                        cx.w(v, "e3p", e3p_base + ji + 1)
                        fs = facc[:, tl * 4096 + ts * 1024 + half * 512:
                                  tl * 4096 + ts * 1024 + half * 512 + 512]
                        cwap = cwt[:, tl * 32 + ts * 8 + e:
                                   tl * 32 + ts * 8 + e + 1]
                        if e == 0:
                            v.tensor_scalar(fs, ppd[ji % 2][:, 0:512], cwap, None,
                                            ALU.mult)
                        else:
                            v.scalar_tensor_tensor(fs, ppd[ji % 2][:, 0:512], cwap,
                                                   fs, ALU.mult, ALU.add)
                        cx.inc(v.scalar_tensor_tensor(
                            fs, b3b[:, half * 512:(half + 1) * 512], cwap,
                            fs, ALU.mult, ALU.add), "f3d", 1)

            for fi in (range(8) if upto == "full" else []):
                tl, ts = divmod(fi, 4)
                tt = tp * 2 + tl
                tb = tt * 512

                @block.sync
                def _(s, tt=tt, ts=ts):
                    if cx.cnt.get("x2st", 0) > 0:
                        cx.w(s, "x2st", cx.cnt["x2st"])
                    r0 = tt * 512 + ts * 128
                    cx.inc(s.dma_start(out=xt1[:], in_=x1i[r0:r0 + 128, :]),
                           "lxx", 16)
                    cx.inc(s.dma_start(out=xt2[:], in_=ui[r0:r0 + 128, :]),
                           "lxx", 16)

                @block.tensor
                def _(t, fi=fi, tb=tb, ts=ts):
                    if fi == 0:
                        cx.w(t, "dte", NSLOT * 8)
                        cx.w(t, "lwo", 16)
                        cx.w(t, "f3d", cx.cnt["f3d"])
                    else:
                        cx.w(t, "x2d", cx.cnt["x2d"])
                    for half in range(2):
                        pairs = [(ylf[:, ic * TOK + tb + ts * 128:
                                      ic * TOK + tb + ts * 128 + 128],
                                  wchunked(lwo_sb, ic, 1024, half * 512, 512))
                                 for ic in range(8)]
                        mm_acc(cx, t, ppd[half][:, 0:512], pairs, "lop")

                @block.vector
                def _(v, tl=tl, ts=ts):
                    cx.w(v, "lop", cx.cnt["lop"])
                    cx.w(v, "lxx", cx.cnt["lxx"])
                    v.tensor_add(xt1[:], xt1[:], xt2[:])
                    v.tensor_add(xt1[:], xt1[:],
                                 facc[:, tl * 4096 + ts * 1024:
                                      tl * 4096 + ts * 1024 + 1024])
                    v.tensor_add(xt1[:, 0:512], xt1[:, 0:512], ppd[0][:, 0:512])
                    cx.inc(v.tensor_add(xt1[:, 512:1024], xt1[:, 512:1024],
                                        ppd[1][:, 0:512]), "x2d", 1)

                @block.sync
                def _(s, tt=tt, ts=ts):
                    cx.w(s, "x2d", cx.cnt["x2d"])
                    r0 = tt * 512 + ts * 128
                    cx.inc(s.dma_start(out=o_x2[r0:r0 + 128, :], in_=xt1[:]),
                           "x2st", 16)

        if upto == "gate":
            @block.sync
            def _(s):
                cx.w(s, "gte", cx.cnt["gte"])
                cx.inc(s.dma_start(out=o_x2[0:128, 0:64], in_=cwt[:]),
                       "x2st", 16)

        @block.sync
        def _(s):
            cx.w(s, "x2st", cx.cnt["x2st"])
    es2.close()

    cx.close()
    return cx


_NCA = None


def _get_A(upto="full"):
    global _NCA
    if _NCA is None or getattr(_NCA, "_upto", None) != upto:
        _NCA = build_A(upto)
        _NCA._upto = upto
    return _NCA


def _ident_bf16():
    return np.eye(128, dtype=BF)


def maps_A(inputs):
    x = np.asarray(inputs["L0_attn_x"])
    wdq = np.asarray(inputs["L0_attn_W_dq"])
    wuq = np.asarray(inputs["L0_attn_W_uq"])
    wdkv = np.asarray(inputs["L0_attn_W_dkv"])
    wuk = np.asarray(inputs["L0_attn_W_uk"])
    wuv = np.asarray(inputs["L0_attn_W_uv"])
    wo = np.asarray(inputs["L0_attn_W_o"])
    ident = _ident_bf16()
    in_maps = []
    for g in range(NCORE):
        b, c = g // GRP, g % GRP
        idx = tok_idx_of(c)
        in_maps.append({
            "x0": np.ascontiguousarray(x[b][idx]),
            "wdqT": np.ascontiguousarray(wdq.T),
            "wuqT": np.ascontiguousarray(wuq.T),
            "wdkvT": np.ascontiguousarray(wdkv.T),
            "wdq": wdq, "wuk": wuk, "wuv": wuv,
            "woT": np.ascontiguousarray(wo.T),
            "ident": ident,
            "epsA": np.full((128, 1), 1e-5, np.float32),
        })
    return in_maps


def run_A(inputs, upto="full"):
    cxA = _get_A(upto)
    in_maps = maps_A(inputs)
    res = run_bass_kernel_spmd(cxA.nc, in_maps, core_ids=list(range(NCORE)))
    return res.results




_NCB = None


def _get_B():
    global _NCB
    if _NCB is None:
        _NCB = build_B()
    return _NCB


def _rope_tables(pos, base, nh_rep):
    hd = 64
    inv_freq = (1.0 / (np.float32(base) **
                       (np.arange(0, hd, 2, dtype=np.float32) / np.float32(hd))))
    freqs = pos.astype(np.float32)[:, None] * inv_freq[None, :].astype(np.float32)
    cos = np.cos(freqs).astype(np.float32)
    sin = np.sin(freqs).astype(np.float32)
    return (np.ascontiguousarray(np.tile(cos, (1, nh_rep))),
            np.ascontiguousarray(np.tile(sin, (1, nh_rep))))


def _causal_masks(c):
    tl = tiles_of(c)
    out = np.zeros((NSLOT, 4, 128, 128), dtype=BF)
    kk = np.arange(128)[:, None]
    qq = np.arange(128)[None, :]
    for j in range(NSLOT):
        t = tl[j]
        m = j // 2
        kb0 = 8 * m + (0 if j % 2 == 0 else 4)
        for r in range(4):
            kb = kb0 + r
            out[j, r] = ((kb * 128 + kk) <= (t * 128 + qq)).astype(BF)
    return out.reshape(NSLOT * 4 * 128 // 128, -1) if False else \
        np.ascontiguousarray(out.transpose(2, 0, 1, 3).reshape(128, NSLOT * 4 * 128))


def _quarters_K(K_all):
    Kr = np.asarray(K_all).reshape(16, 64, 64, 128)
    out = []
    for p in range(4):
        a = Kr[4 * p:4 * p + 4]
        out.append(np.ascontiguousarray(
            a.transpose(1, 2, 0, 3).reshape(64, 64 * 4 * 128)))
    return out


def _quarters_V(V_all):
    Vr = np.asarray(V_all).reshape(64, 128, 16, 64)
    out = []
    for p in range(4):
        a = Vr[:, :, 4 * p:4 * p + 4, :]
        pad = np.zeros((64, 128, 4, 65), dtype=a.dtype)
        pad[:, :, :, 0:64] = a
        pad[:, :, :, 64] = 1.0
        out.append(np.ascontiguousarray(
            pad.transpose(1, 0, 2, 3).reshape(128, 64 * 260)))
    return out


def _quarters_q(q0f):
    qr = np.asarray(q0f).reshape(16, 64, TOK)
    return [np.ascontiguousarray(
        qr[4 * p:4 * p + 4].transpose(1, 0, 2).reshape(64, 4 * TOK))
        for p in range(4)]


def _fmajor_relayout(a):
    return np.ascontiguousarray(
        np.asarray(a).reshape(8, 128, TOK).transpose(1, 0, 2).reshape(128, 8 * TOK))


def maps_B(inputs, resA):
    x = np.asarray(inputs["L0_attn_x"])
    k0 = np.asarray(inputs["L0_ffn_k0"]).view(np.uint8)
    k1 = np.asarray(inputs["L0_ffn_k1"]).view(np.uint8)
    k2 = np.asarray(inputs["L0_ffn_k2"]).view(np.uint8)
    k0a = np.ascontiguousarray(
        k0[:1024].reshape(8, 128, 32, 128).transpose(2, 1, 0, 3).reshape(32, 128, 1024))
    k0b = np.ascontiguousarray(
        k0[1024:].reshape(8, 128, 32, 128).transpose(2, 1, 0, 3).reshape(32, 128, 1024))
    k1l = np.ascontiguousarray(
        k1.reshape(32, 128, 32, 128).transpose(2, 1, 0, 3).reshape(32, 128, 4096))
    k2l = np.ascontiguousarray(k2.reshape(32, 128, 1024))
    s012 = np.array([[inputs["L0_ffn_s0"], inputs["L0_ffn_s1"],
                      inputs["L0_ffn_s2"]]], dtype=np.float32)
    n1w = np.asarray(inputs["L1_attn_norm1_w"]).reshape(1, 1024).astype(np.float32)
    KV = {}
    for b in range(B):
        K_all = np.zeros((1024, T), dtype=BF)
        V_all = np.zeros((T, 1024), dtype=BF)
        for c in range(GRP):
            idx = tok_idx_of(c)
            g = b * GRP + c
            K_all[:, idx] = np.asarray(resA[g]["K0f"])
            V_all[idx, :] = np.asarray(resA[g]["V0t"])
        KV[b] = (_quarters_K(K_all), _quarters_V(V_all))
    ident = np.eye(128, dtype=BF)
    identf = np.eye(128, dtype=np.float32)
    onesb = np.ones((128, 1), dtype=BF)
    onesf = np.ones((1, 128), dtype=np.float32)
    in_maps = []
    for g in range(NCORE):
        b, c = g // GRP, g % GRP
        idx = tok_idx_of(c)
        qq = _quarters_q(resA[g]["q0f"])
        Kq, Vq = KV[b]
        cg512, sg512 = _rope_tables(idx, 1e6, 16)
        cg128, sg128 = _rope_tables(idx, 1e6, 4)
        m = {
            "x0": np.ascontiguousarray(x[b][idx]),
            "mskt": _causal_masks(c),
            "xn0fi": _fmajor_relayout(resA[g]["xn0f"]),
            "k0a_l": k0a, "k0b_l": k0b, "k1_l": k1l, "k2_l": k2l,
            "s012": s012, "n1w": n1w,
            "g_wq": np.asarray(inputs["L1_attn_g_wq"]),
            "g_kvd": np.asarray(inputs["L1_attn_g_kv_down"]),
            "g_ku": np.asarray(inputs["L1_attn_g_k_up"]),
            "g_vu": np.asarray(inputs["L1_attn_g_v_up"]),
            "cg512": cg512, "sg512": sg512, "cg128": cg128, "sg128": sg128,
            "identb": ident, "identf": identf, "onesb": onesb, "onesf": onesf,
            "epsB": np.concatenate([np.full((128, 1), 1e-5, np.float32),
                                    np.full((128, 1), 1e-6, np.float32)], 1),
        }
        for p in range(4):
            m[f"qh{p}"] = qq[p]
            m[f"Kh{p}"] = Kq[p]
            m[f"Vh{p}"] = Vq[p]
        in_maps.append(m)
    return in_maps


def run_B(inputs, resA):
    cxB = _get_B()
    in_maps = maps_B(inputs, resA)
    res = run_bass_kernel_spmd(cxB.nc, in_maps, core_ids=list(range(NCORE)))
    return res.results




_NCC = None


def _get_C():
    global _NCC
    if _NCC is None:
        _NCC = build_C()
    return _NCC


def _quarters_K_256(k_all):
    out = []
    for p in range(4):
        sub = np.asarray(k_all[:, 64 * p:64 * p + 64])
        a = sub.reshape(64, 128, 64).transpose(2, 0, 1)
        out.append(np.ascontiguousarray(a.reshape(64, NBLK * 128)))
    return out


def _quarters_V_256(v_all):
    out = []
    for p in range(4):
        sub = np.asarray(v_all[:, 64 * p:64 * p + 64]).reshape(64, 128, 64)
        pad = np.zeros((64, 128, 65), dtype=sub.dtype)
        pad[:, :, 0:64] = sub
        pad[:, :, 64] = 1.0
        out.append(np.ascontiguousarray(
            pad.transpose(1, 0, 2).reshape(128, NBLK * 65)))
    return out


def maps_C(inputs, resB):
    KV = {}
    for b in range(B):
        k_all = np.zeros((T, 256), dtype=BF)
        v_all = np.zeros((T, 256), dtype=BF)
        for c in range(GRP):
            idx = tok_idx_of(c)
            g = b * GRP + c
            k_all[idx] = np.asarray(resB[g]["k1r"]).astype(BF)
            v_all[idx] = np.asarray(resB[g]["v1"]).astype(BF)
        KV[b] = (_quarters_K_256(k_all), _quarters_V_256(v_all))
    identf = np.eye(128, dtype=np.float32)
    identb = np.eye(128, dtype=BF)
    in_maps = []
    for g in range(NCORE):
        b, c = g // GRP, g % GRP
        idx = tok_idx_of(c)
        qq = _quarters_q(np.asarray(resB[g]["q1r"]).astype(BF).T)
        Kq, Vq = KV[b]
        cl512, sl512 = _rope_tables(idx, 1e4, 16)
        cl128, sl128 = _rope_tables(idx, 1e4, 4)
        m = {
            "mskt": _causal_masks(c),
            "xn1i": np.asarray(resB[g]["xn1"]),
            "g_wo": np.asarray(inputs["L1_attn_g_wo"]),
            "l_wq": np.asarray(inputs["L1_attn_l_wq"]),
            "l_wk": np.asarray(inputs["L1_attn_l_wk"]),
            "l_wv": np.asarray(inputs["L1_attn_l_wv"]),
            "n2w": np.asarray(inputs["L1_attn_norm2_w"]).reshape(1, 1024).astype(np.float32),
            "cl512": cl512, "sl512": sl512, "cl128": cl128, "sl128": sl128,
            "identf": identf, "identb": identb,
            "epsC": np.concatenate([np.full((128, 1), 1e-5, np.float32),
                                    np.full((128, 1), 1e-6, np.float32)], 1),
        }
        for p in range(4):
            m[f"qh{p}"] = qq[p]
            m[f"Kh{p}"] = Kq[p]
            m[f"Vh{p}"] = Vq[p]
        in_maps.append(m)
    return in_maps


def run_C(inputs, resB):
    cxC = _get_C()
    res = run_bass_kernel_spmd(cxC.nc, maps_C(inputs, resB),
                               core_ids=list(range(NCORE)))
    return res.results


_NCD = None


def _get_D():
    global _NCD
    if _NCD is None:
        _NCD = build_D()
    return _NCD


def _window_KV(k_all, v_all, c):
    tl = tiles_of(c)
    Kw = [np.zeros((64, NSLOT * 3 * 128), dtype=BF) for _ in range(4)]
    Vw = [np.zeros((128, NSLOT * 3 * 65), dtype=BF) for _ in range(4)]
    for p in range(4):
        for j in range(NSLOT):
            t = tl[j]
            for kb in range(3):
                tk = t - 2 + kb
                col = j * 3 + kb
                Vw[p][:, col * 65 + 64] = 1.0
                if tk < 0:
                    continue
                blk = np.asarray(k_all[tk * 128:(tk + 1) * 128,
                                       64 * p:64 * p + 64])
                Kw[p][:, col * 128:(col + 1) * 128] = blk.T
                Vw[p][:, col * 65: col * 65 + 64] = \
                    v_all[tk * 128:(tk + 1) * 128, 64 * p:64 * p + 64]
    return Kw, Vw


def _local_masks(c):
    tl = tiles_of(c)
    out = np.zeros((128, NSLOT * 3 * 128), dtype=BF)
    kk = np.arange(128)[:, None]
    qq = np.arange(128)[None, :]
    for j in range(NSLOT):
        t = tl[j]
        for kb in range(3):
            tk = t - 2 + kb
            if tk < 0:
                continue
            diff = (t * 128 + qq) - (tk * 128 + kk)
            out[:, (j * 3 + kb) * 128:(j * 3 + kb + 1) * 128] = \
                ((diff >= 0) & (diff < 256)).astype(BF)
    return out


def maps_D(inputs, resB, resC):
    KV = {}
    for b in range(B):
        k_all = np.zeros((T, 256), dtype=BF)
        v_all = np.zeros((T, 256), dtype=BF)
        for c in range(GRP):
            idx = tok_idx_of(c)
            g = b * GRP + c
            k_all[idx] = np.asarray(resC[g]["k2r"]).astype(BF)
            v_all[idx] = np.asarray(resC[g]["v2"]).astype(BF)
        KV[b] = (k_all, v_all)
    gw = np.asarray(inputs["L1_ffn_gate_w"], np.float32)
    gate_wT = np.ascontiguousarray(
        gw.T.reshape(8, 128, 8).transpose(1, 0, 2).reshape(128, 64)).astype(BF)
    b1i = np.ascontiguousarray(np.asarray(inputs["L1_ffn_b1"], np.float32)
                               .reshape(8, 8, 128).transpose(0, 2, 1))
    b2i = np.ascontiguousarray(np.asarray(inputs["L1_ffn_b2"], np.float32)
                               .reshape(8, 8, 128).transpose(0, 2, 1))
    b3i = np.ascontiguousarray(
        np.asarray(inputs["L1_ffn_b3"], np.float32).reshape(8, 1, 1024))
    iota8 = np.tile(np.arange(8, dtype=np.float32)[None, :], (128, 1))
    cst = np.tile(np.array([[-65536.0, 65536.0, -2.0, 0.0]], np.float32),
                  (128, 1))
    identf = np.eye(128, dtype=np.float32)
    identb = np.eye(128, dtype=BF)
    w1 = np.asarray(inputs["L1_ffn_w1"], np.float32)
    w2 = np.asarray(inputs["L1_ffn_w2"], np.float32)
    w3 = np.asarray(inputs["L1_ffn_w3"], np.float32)
    in_maps = []
    for g in range(NCORE):
        b, c = g // GRP, g % GRP
        k_all, v_all = KV[b]
        Kw, Vw = _window_KV(k_all, v_all, c)
        qq4 = _quarters_q(np.asarray(resC[g]["q2r"]).astype(BF).T)
        m = {
            "x1i": np.asarray(resB[g]["x1"], np.float32),
            "ui": np.asarray(resC[g]["u"], np.float32),
            "xn1fi": _fmajor_relayout(np.asarray(resB[g]["xn1b"]).astype(BF).T),
            "msklt": _local_masks(c),
            "l_wo": np.asarray(inputs["L1_attn_l_wo"], np.float32),
            "gate_wT": gate_wT,
            "gate_b": np.asarray(inputs["L1_ffn_gate_b"],
                                 np.float32).reshape(1, 8),
            "w1": w1, "w2": w2, "w3": w3,
            "b1i": b1i, "b2i": b2i, "b3i": b3i,
            "iota8": iota8, "cst": cst,
            "identf": identf, "identb": identb,
        }
        for p in range(4):
            m[f"qh{p}"] = qq4[p]
            m[f"Kw{p}"] = Kw[p]
            m[f"Vw{p}"] = Vw[p]
        in_maps.append(m)
    return in_maps


def run_D(inputs, resB, resC):
    cxD = _get_D()
    res = run_bass_kernel_spmd(cxD.nc, maps_D(inputs, resB, resC),
                               core_ids=list(range(NCORE)))
    return res.results


def kernel(**inputs):
    resA = run_A(inputs)
    resB = run_B(inputs, resA)
    resC = run_C(inputs, resB)
    resD = run_D(inputs, resB, resC)
    out = np.zeros((B, T, D), np.float32)
    for g in range(NCORE):
        b, c = g // GRP, g % GRP
        out[b][tok_idx_of(c)] = np.asarray(resD[g]["x2"], np.float32)
    return out


# version 07 (final)
# speedup vs baseline: 1.0075x; 1.0075x over previous
import numpy as np
import ml_dtypes
from contextlib import ExitStack
import concourse.bass as bass
import concourse.mybir as mybir
from concourse.bass_utils import run_bass_kernel_spmd

F32 = mybir.dt.float32
BF16 = mybir.dt.bfloat16
U8 = mybir.dt.uint8
AF = mybir.ActivationFunctionType
ALU = mybir.AluOpType
AX = mybir.AxisListType
BF = ml_dtypes.bfloat16

B, T, D = 2, 8192, 1024
NCORE, GRP, TOK, NSLOT, NBLK = 8, 4, 2048, 16, 64


def tiles_of(c):
    out = []
    for m in range(8):
        out += [8 * m + c, 8 * m + 7 - c]
    return out


def tok_idx_of(c):
    t = np.asarray(tiles_of(c))
    return (t[:, None] * 128 + np.arange(128)[None, :]).reshape(-1)


class Cx:
    def __init__(self, name):
        self.nc = bass.Bass("TRN2", target_bir_lowering=False)
        self.es = ExitStack()
        self.sems = {}
        self.cnt = {}
        self.name = name

    def sem(self, name):
        if name not in self.sems:
            self.sems[name] = self.es.enter_context(self.nc.semaphore(name))
            self.cnt[name] = 0
        return self.sems[name]

    def inc(self, instr, name, n=1):
        instr.then_inc(self.sem(name), n)
        self.cnt[name] += n
        return self.cnt[name]

    def w(self, eng, name, val=None):
        eng.wait_ge(self.sem(name), self.cnt[name] if val is None else val)

    def sb(self, name, shape, dt):
        return self.es.enter_context(self.nc.sbuf_tensor(name, shape, dt))

    def ps(self, name, shape, dt=F32):
        return self.es.enter_context(self.nc.psum_tensor(name, shape, dt))

    def din(self, name, shape, dt):
        return self.nc.dram_tensor(name, shape, dt, kind="ExternalInput").ap()

    def dout(self, name, shape, dt):
        return self.nc.dram_tensor(name, shape, dt, kind="ExternalOutput").ap()

    def block(self):
        return self.es.enter_context(self.nc.Block())

    def close(self):
        self.es.close()


def mm_acc(cx, t, tile_ap, pairs, pe):
    n = len(pairs)
    for i, (lw, rr) in enumerate(pairs):
        ins = t.matmul(tile_ap, lw, rr, start=(i == 0), stop=(i == n - 1))
        if i == n - 1:
            cx.inc(ins, pe, 1)


def fproj(cx, block, name, jobs, pst, wait_tensor=None, evac_eng="vector"):
    pe, dv = name + "p", name + "v"

    @block.tensor
    def _(t):
        if wait_tensor:
            wait_tensor(t)
        for idx, (pairs, width, _) in enumerate(jobs):
            if idx >= 2:
                cx.w(t, dv, idx - 1)
            mm_acc(cx, t, pst[idx % 2][:, 0:width], pairs, pe)

    body = getattr(block, evac_eng)

    def _evac(eng):
        for idx, (_, width, evac_fn) in enumerate(jobs):
            cx.w(eng, pe, idx + 1)
            last = evac_fn(eng, pst[idx % 2][:, 0:width])
            cx.inc(last, dv, 1)
    body(_evac)
    return len(jobs)


def wchunked(sb_ap, ic, colw, off, width):
    return sb_ap[:, ic * colw + off: ic * colw + off + width]


def cp_to_gen(cx, dst, off, w):
    def f(eng, tile):
        return eng.tensor_copy(dst[:, off: off + w], tile)
    return f


def rope_views(t_ap, nh, par):
    return t_ap[:, 0:nh * 64].rearrange("p (h d two) -> p h d two",
                                        h=nh, two=2)[:, :, :, par]


def emit_rope(v, out_t, in_t, cos_ap, sin_ap, nh, rA, rB):
    xe = rope_views(in_t, nh, 0)
    xo = rope_views(in_t, nh, 1)
    cosv = cos_ap.rearrange("p (h d) -> p h d", d=32)
    sinv = sin_ap.rearrange("p (h d) -> p h d", d=32)
    rAv = rA[:, 0:nh * 32].rearrange("p (h d) -> p h d", d=32)
    rBv = rB[:, 0:nh * 32].rearrange("p (h d) -> p h d", d=32)
    v.tensor_mul(rAv, xe, cosv)
    v.tensor_mul(rBv, xo, sinv)
    v.tensor_tensor(rope_views(out_t, nh, 0), rAv, rBv, ALU.subtract)
    v.tensor_mul(rAv, xe, sinv)
    v.tensor_mul(rBv, xo, cosv)
    v.tensor_tensor(rope_views(out_t, nh, 1), rAv, rBv, ALU.add)


def build_A(upto="full"):
    cx = Cx("A")
    x0 = cx.din("x0", [TOK, D], F32)
    wdqT = cx.din("wdqT", [1024, 512], F32)
    wuqT = cx.din("wuqT", [512, 1024], F32)
    wdkvT = cx.din("wdkvT", [1024, 512], F32)
    wdq = cx.din("wdq", [512, 1024], F32)
    wuk = cx.din("wuk", [1024, 512], F32)
    wuv = cx.din("wuv", [1024, 512], F32)
    woT = cx.din("woT", [1024, 1024], F32)
    ident = cx.din("ident", [128, 128], BF16)
    epsA = cx.din("epsA", [128, 1], F32)
    o_xn0f = cx.dout("xn0f", [1024, TOK], BF16)
    o_q0f = cx.dout("q0f", [1024, TOK], BF16)
    o_K0f = cx.dout("K0f", [1024, TOK], BF16)
    o_V0t = cx.dout("V0t", [TOK, 1024], BF16)

    xn0f = cx.sb("xn0f_s", [128, 8 * TOK], BF16)
    idsb = cx.sb("idsb", [128, 128], BF16)
    eps5 = cx.sb("eps5", [128, 1], F32)
    wuqT_sb = cx.sb("wuqT_sb", [128, 4 * 1024], BF16)
    wdq_sb = cx.sb("wdq_sb", [128, 4 * 1024], BF16)
    wuk_sb = cx.sb("wuk_sb", [128, 8 * 512], BF16)
    wuv_sb = cx.sb("wuv_sb", [128, 8 * 512], BF16)
    woT_sb = cx.sb("woT_sb", [128, 8 * 1024], BF16)
    wdqT_sb = cx.sb("wdqT_sb", [128, 8 * 512], BF16)
    wdkvT_sb = cx.sb("wdkvT_sb", [128, 8 * 512], BF16)
    T1 = cx.sb("T1", [128, 8 * 1024], BF16)
    MT = cx.sb("MT", [128, 4 * 1024], BF16)
    Nn = cx.sb("Nn", [128, 4 * 1024], BF16)
    qdf = cx.sb("qdf", [128, 4 * TOK], BF16)
    ckvf = cx.sb("ckvf", [128, 4 * TOK], BF16)
    stgQ = cx.sb("stgQ", [128, 2 * 512], BF16)
    stgK = cx.sb("stgK", [128, 2 * 512], BF16)
    xt = cx.sb("xt", [128, 2 * 1024], F32)
    sqf = cx.sb("sqf", [128, 1024], F32)
    lnS1 = cx.sb("lnS1", [128, 64], F32)
    lnS2 = cx.sb("lnS2", [128, 64], F32)
    lnM = cx.sb("lnM", [128, 64], F32)
    lnV = cx.sb("lnV", [128, 64], F32)
    lnR = cx.sb("lnR", [128, 64], F32)
    lnSh = cx.sb("lnSh", [128, 64], F32)
    xn_tm = cx.sb("xn_tm", [128, 1024], BF16)
    stgV = cx.sb("stgV", [128, 2 * 1024], BF16)

    psA = cx.ps("psA", [128, 512])
    psB = cx.ps("psB", [128, 512])
    ptrA = cx.ps("ptrA", [128, 1024], BF16)
    ptrB = cx.ps("ptrB", [128, 1024], BF16)
    ptrs = (ptrA, ptrB)

    block = cx.block()

    @block.gpsimd
    def _(g):
        for dst, src, w in ((wuqT_sb, wuqT, 1024), (wdq_sb, wdq, 1024),
                            (wuk_sb, wuk, 512), (wuv_sb, wuv, 512),
                            (woT_sb, woT, 1024), (wdqT_sb, wdqT, 512),
                            (wdkvT_sb, wdkvT, 512)):
            cx.inc(g.dma_start(out=dst[:].rearrange("p (i c) -> p i c", c=w),
                               in_=src.rearrange("(i p) c -> p i c", p=128)),
                   "ldw", 16)

    @block.sync
    def _(s):
        cx.inc(s.dma_start(out=idsb[:], in_=ident), "ldi", 16)
        cx.inc(s.dma_start(out=eps5[:], in_=epsA), "ini", 16)


    for j in range(NSLOT):
        @block.sync
        def _(s, j=j):
            if j >= 2:
                cx.w(s, "lnd", 2 * (j - 1))
            cx.inc(s.dma_start(out=xt[:, (j % 2) * 1024:(j % 2) * 1024 + 1024],
                               in_=x0[j * 128:(j + 1) * 128, :]), "ldx", 16)

        @block.vector
        def _(v, j=j):
            cx.w(v, "ldx", (j + 1) * 16)
            xv = xt[:, (j % 2) * 1024:(j % 2) * 1024 + 1024]
            cx.inc(v.reduce_sum(lnS1[:, 4 * j:4 * j + 1], xv, AX.X), "lnd", 1)
            v.tensor_mul(sqf[:], xv, xv)
            cx.w(v, "lnd", cx.cnt["lnd"])
            cx.inc(v.reduce_sum(lnS2[:, 4 * j:4 * j + 1], sqf[:], AX.X), "lnd", 1)

    @block.vector
    def _(v):
        cx.w(v, "lnd", 2 * NSLOT)
        s1v = lnS1[:].rearrange("p (j f) -> p j f", f=4)[:, :, 0]
        s2v = lnS2[:].rearrange("p (j f) -> p j f", f=4)[:, :, 0]
        mv = lnM[:].rearrange("p (j f) -> p j f", f=4)[:, :, 0]
        vv = lnV[:].rearrange("p (j f) -> p j f", f=4)[:, :, 0]
        sv = lnSh[:].rearrange("p (j f) -> p j f", f=4)[:, :, 0]
        cx.inc(v.tensor_scalar_mul(mv, s1v, 1.0 / 1024), "cmb", 1)
        cx.w(v, "cmb", cx.cnt["cmb"])
        cx.inc(v.tensor_scalar_mul(vv, s2v, 1.0 / 1024), "cmb", 1)
        cx.w(v, "cmb", cx.cnt["cmb"])
        cx.inc(v.tensor_mul(sv, mv, mv), "cmb", 1)
        cx.w(v, "cmb", cx.cnt["cmb"])
        cx.inc(v.tensor_sub(vv, vv, sv), "cmb", 1)
        cx.w(v, "cmb", cx.cnt["cmb"])
        cx.inc(v.tensor_copy(sv, vv), "cmb", 1)

    @block.scalar
    def _(s):
        cx.w(s, "cmb", cx.cnt["cmb"])
        cx.w(s, "ini", 16)
        vv = lnV[:].rearrange("p (j f) -> p j f", f=4)[:, :, 0]
        rv_ = lnR[:].rearrange("p (j f) -> p j f", f=4)[:, :, 0]
        r2v = lnR[:].rearrange("p (j f) -> p j f", f=4)[:, :, 1]
        sv = lnSh[:].rearrange("p (j f) -> p j f", f=4)[:, :, 1]
        cx.inc(s.activation(out=r2v, in_=vv, func=AF.Ln, bias=eps5[:, 0:1]),
               "lna", 1)
        cx.w(s, "lna", cx.cnt["lna"])
        cx.inc(s.activation(out=rv_, in_=r2v, func=AF.Exp, scale=-0.5), "lna", 1)
        cx.w(s, "lna", cx.cnt["lna"])
        cx.inc(s.activation(out=sv, in_=rv_, func=AF.Copy), "lna", 1)

    for j in range(NSLOT):
        @block.sync
        def _(s, j=j):
            if j < 2:
                cx.w(s, "lnd", 2 * NSLOT)
            else:
                cx.w(s, "lne", 8 * (j - 1))
            cx.inc(s.dma_start(out=xt[:, (j % 2) * 1024:(j % 2) * 1024 + 1024],
                               in_=x0[j * 128:(j + 1) * 128, :]), "ldy", 16)

        @block.vector
        def _(v, j=j):
            cx.w(v, "ldy", (j + 1) * 16)
            if j == 0:
                cx.w(v, "lna", cx.cnt["lna"])
            if j >= 1:
                cx.w(v, "lnt", 8 * j)
            xv = xt[:, (j % 2) * 1024:(j % 2) * 1024 + 1024]
            cx.inc(v.tensor_scalar(xn_tm[:], xv, lnM[:, 4 * j:4 * j + 1],
                                   lnR[:, 4 * j:4 * j + 1],
                                   ALU.subtract, ALU.mult), "lnm", 1)

        @block.tensor
        def _(t, j=j):
            if j == 0:
                cx.w(t, "ldi", 16)
            cx.w(t, "lnm", cx.cnt["lnm"])
            for ci in range(8):
                gi = j * 8 + ci
                if gi >= 2:
                    cx.w(t, "lne", gi - 1)
                cx.inc(t.transpose(ptrs[ci % 2][:, 0:128],
                                   xn_tm[:, ci * 128:(ci + 1) * 128], idsb[:]),
                       "lnt", 1)

        @block.vector
        def _(v, j=j):
            for ci in range(8):
                cx.w(v, "lnt", j * 8 + ci + 1)
                cx.inc(v.tensor_copy(
                    xn0f[:, ci * TOK + j * 128: ci * TOK + (j + 1) * 128],
                    ptrs[ci % 2][:, 0:128]), "lne", 1)

    def cp_to(dst, coloff, w):
        def f(eng, tile):
            return eng.tensor_copy(dst[:, coloff: coloff + w], tile)
        return f

    jobs = []
    for oc in range(8):
        for half in range(2):
            pairs = [(wchunked(wuqT_sb, ic, 1024, oc * 128, 128),
                      wchunked(wdq_sb, ic, 1024, half * 512, 512))
                     for ic in range(4)]
            jobs.append((pairs, 512, cp_to(T1, oc * 1024 + half * 512, 512)))
    fproj(cx, block, "t1", jobs, (psA, psB),
          wait_tensor=lambda t: cx.w(t, "ldw"))

    nT1 = cx.cnt["t1p"]
    jobs = []
    for oc in range(4):
        for half in range(2):
            pairs = [(wchunked(wuk_sb, ic, 512, oc * 128, 128),
                      wchunked(T1, ic, 1024, half * 512, 512))
                     for ic in range(8)]
            jobs.append((pairs, 512, cp_to(MT, oc * 1024 + half * 512, 512)))
    fproj(cx, block, "mt", jobs, (psA, psB),
          wait_tensor=lambda t: cx.w(t, "t1v", nT1))

    jobs = []
    for oc in range(4):
        for half in range(2):
            pairs = [(wchunked(wuv_sb, ic, 512, oc * 128, 128),
                      wchunked(woT_sb, ic, 1024, half * 512, 512))
                     for ic in range(8)]
            jobs.append((pairs, 512, cp_to(Nn, oc * 1024 + half * 512, 512)))
    nMT0 = cx.cnt["mtv"]
    fproj(cx, block, "nn", jobs, (psA, psB),
          wait_tensor=lambda t: cx.w(t, "mtv", nMT0))

    if upto == "chains":
        @block.sync
        def _(s):
            cx.w(s, "lne", NSLOT * 8)
            cx.inc(s.dma_start(out=o_xn0f.rearrange("(i p) t -> p i t", p=128),
                               in_=xn0f[:].rearrange("p (i t) -> p i t", t=TOK)),
                   "sto", 16)
            cx.w(s, "nnv", cx.cnt["nnv"])
            cx.w(s, "sto")
        cx.close()
        return cx

    jobs = []
    for oc in range(4):
        for tc in range(4):
            pairs = [(wchunked(wdqT_sb, ic, 512, oc * 128, 128),
                      wchunked(xn0f, ic, TOK, tc * 512, 512))
                     for ic in range(8)]
            jobs.append((pairs, 512, cp_to(qdf, oc * TOK + tc * 512, 512)))
    nNN0 = cx.cnt["nnv"]
    fproj(cx, block, "qd", jobs, (psA, psB),
          wait_tensor=lambda t: (cx.w(t, "lne", NSLOT * 8), cx.w(t, "nnv", nNN0)))

    nQD = cx.cnt["qdv"]
    jobs = []
    for idx0, (oc, tc) in enumerate([(oc, tc) for oc in range(8) for tc in range(4)]):
        pairs = [(wchunked(wuqT_sb, ic, 1024, oc * 128, 128),
                  wchunked(qdf, ic, TOK, tc * 512, 512))
                 for ic in range(4)]
        def evQ(eng, tile, idx0=idx0):
            if idx0 >= 2:
                cx.w(eng, "stq", 16 * (idx0 - 1))
            return eng.tensor_copy(
                stgQ[:, (idx0 % 2) * 512:(idx0 % 2) * 512 + 512], tile)
        jobs.append((pairs, 512, evQ))
    fproj(cx, block, "q0", jobs, (psA, psB),
          wait_tensor=lambda t: cx.w(t, "qdv", nQD))

    @block.sync
    def _(s):
        for idx0, (oc, tc) in enumerate([(oc, tc) for oc in range(8) for tc in range(4)]):
            cx.w(s, "q0v", idx0 + 1)
            cx.inc(s.dma_start(
                out=o_q0f[oc * 128:(oc + 1) * 128, tc * 512:(tc + 1) * 512],
                in_=stgQ[:, (idx0 % 2) * 512:(idx0 % 2) * 512 + 512]), "stq", 16)

    jobs = []
    for oc in range(4):
        for tc in range(4):
            pairs = [(wchunked(wdkvT_sb, ic, 512, oc * 128, 128),
                      wchunked(xn0f, ic, TOK, tc * 512, 512))
                     for ic in range(8)]
            jobs.append((pairs, 512, cp_to(ckvf, oc * TOK + tc * 512, 512)))
    nQ00 = cx.cnt["q0v"]
    fproj(cx, block, "ck", jobs, (psA, psB),
          wait_tensor=lambda t: cx.w(t, "q0v", nQ00))

    nCK = cx.cnt["ckv"]
    nMT = cx.cnt["mtv"]
    jobs = []
    for idx0, (oc, tc) in enumerate([(oc, tc) for oc in range(8) for tc in range(4)]):
        pairs = [(wchunked(MT, ic, 1024, oc * 128, 128),
                  wchunked(ckvf, ic, TOK, tc * 512, 512))
                 for ic in range(4)]
        def evK(eng, tile, idx0=idx0):
            if idx0 >= 2:
                cx.w(eng, "stk", 16 * (idx0 - 1))
            return eng.tensor_copy(
                stgK[:, (idx0 % 2) * 512:(idx0 % 2) * 512 + 512], tile)
        jobs.append((pairs, 512, evK))
    fproj(cx, block, "k0", jobs, (psA, psB),
          wait_tensor=lambda t: (cx.w(t, "ckv", nCK), cx.w(t, "mtv", nMT)))

    @block.sync
    def _(s):
        for idx0, (oc, tc) in enumerate([(oc, tc) for oc in range(8) for tc in range(4)]):
            cx.w(s, "k0v", idx0 + 1)
            cx.inc(s.dma_start(
                out=o_K0f[oc * 128:(oc + 1) * 128, tc * 512:(tc + 1) * 512],
                in_=stgK[:, (idx0 % 2) * 512:(idx0 % 2) * 512 + 512]), "stk", 16)

    jobs = []
    for j in range(NSLOT):
        for oc in range(2):
            pairs = [(ckvf[:, lc * TOK + j * 128: lc * TOK + (j + 1) * 128],
                      wchunked(Nn, lc, 1024, oc * 512, 512))
                     for lc in range(4)]
            def evV(eng, tile, j=j, oc=oc):
                if j >= 2:
                    cx.w(eng, "stv", 16 * (j - 1))
                return eng.tensor_copy(
                    stgV[:, (j % 2) * 1024 + oc * 512:
                         (j % 2) * 1024 + oc * 512 + 512], tile)
            jobs.append((pairs, 512, evV))
    nK00 = cx.cnt["k0v"]
    fproj(cx, block, "v0", jobs, (psA, psB),
          wait_tensor=lambda t: cx.w(t, "k0v", nK00))

    @block.sync
    def _(s):
        for j in range(NSLOT):
            cx.w(s, "v0v", (j + 1) * 2)
            cx.inc(s.dma_start(out=o_V0t[j * 128:(j + 1) * 128, :],
                               in_=stgV[:, (j % 2) * 1024:(j % 2) * 1024 + 1024]),
                   "stv", 16)

    @block.sync
    def _(s):
        cx.w(s, "lne", NSLOT * 8)
        cx.inc(s.dma_start(out=o_xn0f.rearrange("(i p) t -> p i t", p=128),
                           in_=xn0f[:].rearrange("p (i t) -> p i t", t=TOK)),
               "sto", 16)
        cx.w(s, "sto")
        cx.w(s, "stv")
        cx.w(s, "stq")
        cx.w(s, "stk")

    cx.close()
    return cx



def emit_attention(cx, block, tag, *, Ksb, Vsb, qsb, msksb, idf32,
                   sb4, yb, ysb, ytok, rlt, kcol_fn, vcol_fn, y_store_fn,
                   wait_pass_fn, wait_m_fn=None, exp_scale=0.125, NH=4):
    S_, E_, M_, P_, T_, D_, Y_ = (tag + x for x in ("s", "e", "m", "p", "t", "d", "y"))
    PT = cx.PT

    for m in range(8):
        nkb = 8 * m + 8
        q0 = 256 * m
        sbase = cx.cnt.get(S_, 0)
        ebase = cx.cnt.get(E_, 0)
        pbase = cx.cnt.get(P_, 0)
        mbase = cx.cnt.get(M_, 0)
        ybase = cx.cnt.get(Y_, 0)
        tbase = cx.cnt.get(T_, 0)
        dbase = cx.cnt.get(D_, 0)
        stybase = cx.cnt.get("sty", 0)

        def iexp(k, hl):
            return ebase + k * 2 + hl // 2 + 1

        def imsk(k, hl):
            return mbase + (k - (nkb - 8)) * NH + hl + 1

        @block.tensor
        def _(t, m=m, nkb=nkb, q0=q0):
            if m == 0:
                wait_pass_fn(t)
            else:
                cx.w(t, Y_, ybase)
                cx.w(t, D_, dbase)
                if wait_m_fn is not None:
                    wait_m_fn(t, m)
            for k in range(nkb):
                par = k % 2
                for hl in range(NH):
                    i = k * NH + hl
                    j = k * 2 + hl // 2
                    if hl % 2 == 0 and j >= 4:
                        cx.w(t, E_, ebase + j - 4 + 1)
                    ins = t.matmul(sb4[(ebase + j) % 4][:, (hl % 2) * 256:
                                                        (hl % 2) * 256 + 256],
                                   Ksb[0:64, kcol_fn(k, hl): kcol_fn(k, hl) + 128],
                                   qsb[0:64, hl * TOK + q0: hl * TOK + q0 + 256],
                                   start=(hl % 2 == 0), stop=(hl % 2 == 1))
                    cx.inc(ins, S_, 1)
                boundary = k >= nkb - 8
                upper = k >= nkb - 4
                for hl in range(NH):
                    bank = yb[hl // 2]
                    cb = (hl % 2) * 256
                    if boundary:
                        cx.w(t, M_, imsk(k, hl))
                    else:
                        cx.w(t, E_, iexp(k, hl))
                    first = (k == 0 and (hl % 2) == 0)
                    last = (k == nkb - 1 and (hl % 2) == 1)
                    if upper:
                        ins = t.matmul(bank[0:65, cb + 128: cb + 256],
                                       Vsb[:, vcol_fn(k, hl): vcol_fn(k, hl) + 65],
                                       PT[:, par * 1024 + hl * 256 + 128:
                                          par * 1024 + hl * 256 + 256],
                                       start=first, stop=last)
                    else:
                        ins = t.matmul(bank[0:65, cb: cb + 256],
                                       Vsb[:, vcol_fn(k, hl): vcol_fn(k, hl) + 65],
                                       PT[:, par * 1024 + hl * 256:
                                          par * 1024 + hl * 256 + 256],
                                       start=first, stop=last)
                    cx.inc(ins, P_, 1)

        @block.scalar
        def _(s, m=m, nkb=nkb):
            for k in range(nkb):
                par = k % 2
                if k >= 2:
                    cx.w(s, P_, pbase + (k - 2) * NH + NH)
                elif pbase > 0 and k < 2:
                    cx.w(s, P_, pbase)
                for hp in range(2):
                    j = k * 2 + hp
                    cx.w(s, S_, sbase + k * NH + hp * 2 + 2)
                    ins = s.activation(
                        out=PT[:, par * 1024 + hp * 512: par * 1024 + hp * 512 + 512],
                        in_=sb4[(ebase + j) % 4][:, 0:512], func=AF.Exp,
                        scale=exp_scale)
                    cx.inc(ins, E_, 1)

        @block.vector
        def _(v, m=m, nkb=nkb):
            for k in range(nkb - 8, nkb):
                par = k % 2
                upper = k >= nkb - 4
                slot = 2 * m + (1 if upper else 0)
                r = k - (nkb - 8) if not upper else k - (nkb - 4)
                mc = (slot * 4 + r) * 128
                for hl in range(NH):
                    cx.w(v, E_, iexp(k, hl))
                    off = par * 1024 + hl * 256 + (128 if upper else 0)
                    ins = v.tensor_mul(PT[:, off: off + 128],
                                       PT[:, off: off + 128],
                                       msksb[:, mc: mc + 128])
                    cx.inc(ins, M_, 1)

        @block.vector
        def _(v, m=m, nkb=nkb):
            cx.w(v, P_, pbase + nkb * NH)
            for i in range(2):
                cx.inc(v.tensor_copy(ysb[0:65, i * 512:(i + 1) * 512],
                                     yb[i][0:65, 0:512]), Y_, 1)

        @block.tensor
        def _(t, m=m):
            cx.w(t, Y_, ybase + 2)
            for qs in range(2):
                for hl in range(NH):
                    i = qs * NH + hl
                    if tbase + i >= 2:
                        cx.w(t, D_, tbase + i - 2 + 1)
                    ins = t.transpose(sb4[i % 2][0:128, 0:65],
                                      ysb[0:65, hl * 256 + qs * 128:
                                          hl * 256 + qs * 128 + 128],
                                      idf32[0:65, 0:65])
                    cx.inc(ins, T_, 1)

        @block.vector
        def _(v, m=m):
            for qs in range(2):
                if qs == 0 and (m > 0 or stybase > 0):
                    cx.w(v, "sty", stybase)
                if qs == 1:
                    cx.w(v, "sty", stybase + 16)
                for hl in range(NH):
                    i = qs * NH + hl
                    cx.w(v, T_, tbase + i + 1)
                    cx.inc(v.reciprocal(rlt[:, 0:1], sb4[i % 2][:, 64:65]), "rle", 1)
                    cx.w(v, "rle", cx.cnt["rle"])
                    ins = v.tensor_scalar(ytok[:, hl * 64: hl * 64 + 64],
                                          sb4[i % 2][:, 0:64], rlt[:, 0:1], None,
                                          ALU.mult)
                    cx.inc(ins, D_, 1)

        @block.sync
        def _(s, m=m):
            for qs in range(2):
                cx.w(s, D_, dbase + qs * NH + NH)
                cx.inc(y_store_fn(s, m, qs, ytok), "sty", 16)


def build_B(upto="full"):
    cx = Cx("B")
    nc = cx.nc
    x0 = cx.din("x0", [TOK, D], F32)
    qh = [cx.din(f"qh{p}", [64, 4 * TOK], BF16) for p in range(4)]
    Kh = [cx.din(f"Kh{p}", [64, NBLK * 4 * 128], BF16) for p in range(4)]
    Vh = [cx.din(f"Vh{p}", [128, NBLK * 260], BF16) for p in range(4)]
    mskt = cx.din("mskt", [128, NSLOT * 4 * 128], BF16)
    xn0fi = cx.din("xn0fi", [128, 8 * TOK], BF16)
    k0a_l = cx.din("k0a_l", [32, 128, 1024], U8)
    k0b_l = cx.din("k0b_l", [32, 128, 1024], U8)
    k1_l = cx.din("k1_l", [32, 128, 4096], U8)
    k2_l = cx.din("k2_l", [32, 128, 1024], U8)
    s012 = cx.din("s012", [1, 3], F32)
    epsB = cx.din("epsB", [128, 2], F32)
    n1w = cx.din("n1w", [1, 1024], F32)
    g_wq = cx.din("g_wq", [1024, 1024], F32)
    g_kvd = cx.din("g_kvd", [1024, 256], F32)
    g_ku = cx.din("g_ku", [256, 256], F32)
    g_vu = cx.din("g_vu", [256, 256], F32)
    cg512 = cx.din("cg512", [TOK, 512], F32)
    sg512 = cx.din("sg512", [TOK, 512], F32)
    cg128 = cx.din("cg128", [TOK, 128], F32)
    sg128 = cx.din("sg128", [TOK, 128], F32)
    identb = cx.din("identb", [128, 128], BF16)
    identf = cx.din("identf", [128, 128], F32)
    onesb = cx.din("onesb", [128, 1], BF16)
    onesf = cx.din("onesf", [1, 128], F32)

    o_x1 = cx.dout("x1", [TOK, D], F32)
    o_xn1 = cx.dout("xn1", [TOK, D], F32)
    o_xn1b = cx.dout("xn1b", [TOK, D], BF16)
    o_q1r = cx.dout("q1r", [TOK, 1024], BF16)
    o_k1r = cx.dout("k1r", [TOK, 256], BF16)
    o_v1 = cx.dout("v1", [TOK, 256], BF16)
    y_scr = cx.nc.dram_tensor("y_scr", [TOK, D], F32, kind="Internal").ap()

    idb = cx.sb("idb", [128, 128], BF16)
    idf = cx.sb("idf", [128, 128], F32)
    ones1b = cx.sb("ones1b", [128, 1], BF16)
    onesrf = cx.sb("onesrf", [1, 128], F32)
    scal = cx.sb("scal", [128, 16], F32)
    n1wb = cx.sb("n1wb", [128, 1024], F32)
    s012s = cx.sb("s012s", [1, 3], F32)
    n1ws = cx.sb("n1ws", [1, 1024], F32)

    block = cx.block()

    @block.sync
    def _(s):
        cx.inc(s.dma_start(out=idb[:], in_=identb), "ldc", 16)
        cx.inc(s.dma_start(out=idf[:], in_=identf), "ldc", 16)
        cx.inc(s.dma_start(out=ones1b[:], in_=onesb), "ldc", 16)
        cx.inc(s.dma_start(out=onesrf[:], in_=onesf), "ldc", 16)
        cx.inc(s.dma_start(out=s012s[:], in_=s012), "ldc", 16)
        cx.inc(s.dma_start(out=scal[:, 5:7], in_=epsB), "ldc", 16)
        cx.inc(s.dma_start(out=n1ws[:], in_=n1w), "ldc", 16)

    with cx.nc.psum_tensor("pset", [128, 512], F32) as pset:
        @block.tensor
        def _(t):
            cx.w(t, "ldc")
            cx.inc(t.matmul(pset[:, 0:3], onesrf[0:1, :], s012s[0:1, :],
                            start=True, stop=True), "bps", 1)

        @block.vector
        def _(v):
            cx.w(v, "bps", 1)
            v.tensor_scalar_mul(scal[:, 0:1], pset[:, 0:1], 2.0)
            v.tensor_scalar_mul(scal[:, 1:2], pset[:, 1:2], 4.0)
            v.tensor_scalar_mul(scal[:, 2:3], pset[:, 1:2], 2.0)
            v.tensor_scalar_mul(scal[:, 3:4], pset[:, 2:3], 4.0)
            v.tensor_scalar_mul(scal[:, 4:5], pset[:, 2:3], 2.0)
            cx.inc(v.tensor_copy(scal[:, 8:9], scal[:, 4:5]), "bse", 1)

        for hf in range(2):
            @block.tensor
            def _(t, hf=hf):
                cx.w(t, "bse", 1 + hf)
                cx.inc(t.matmul(pset[:, 0:512], onesrf[0:1, :],
                                n1ws[0:1, hf * 512:(hf + 1) * 512],
                                start=True, stop=True), "bps", 1)

            @block.vector
            def _(v, hf=hf):
                cx.w(v, "bps", 2 + hf)
                cx.inc(v.tensor_copy(n1wb[:, hf * 512:(hf + 1) * 512],
                                     pset[:, 0:512]), "bse", 1)

    if upto != "ffn":
      with (
          cx.nc.sbuf_tensor("Ksb", [64, NBLK * 4 * 128], BF16) as Ksb,
          cx.nc.sbuf_tensor("Vsb", [128, NBLK * 260], BF16) as Vsb,
          cx.nc.sbuf_tensor("qsb", [64, 4 * TOK], BF16) as qsb,
          cx.nc.sbuf_tensor("msksb", [128, NSLOT * 4 * 128], BF16) as msksb,
          cx.nc.sbuf_tensor("PTb", [128, 2 * 1024], BF16) as PTb,
          cx.nc.sbuf_tensor("ysb", [128, 1024], F32) as ysb,
          cx.nc.sbuf_tensor("ytok", [128, 256], F32) as ytok,
          cx.nc.sbuf_tensor("rlt", [128, 2], F32) as rlt,
          cx.nc.psum_tensor("sb40", [128, 512], F32) as sb40,
          cx.nc.psum_tensor("sb41", [128, 512], F32) as sb41,
          cx.nc.psum_tensor("sb42", [128, 512], F32) as sb42,
          cx.nc.psum_tensor("sb43", [128, 512], F32) as sb43,
          cx.nc.psum_tensor("yb0", [128, 512], F32) as yb0,
          cx.nc.psum_tensor("yb1", [128, 512], F32) as yb1,
      ):
          cx.PT = PTb
          sb4 = (sb40, sb41, sb42, sb43)
          yb = (yb0, yb1)

          @block.sync
          def _(s):
              cx.inc(s.dma_start(out=msksb[:], in_=mskt), "ldm", 16)

          for p in range(4):
              ldK0 = cx.cnt.get("ldK", 0)

              @block.sync
              def _(s, p=p):
                  if p >= 1:
                      cx.w(s, "aLp", cx.cnt["aLp"])
                      cx.w(s, "aLd", cx.cnt["aLd"])
                  cx.inc(s.dma_start(out=qsb[:], in_=qh[p]), "ldK", 16)
                  for j in range(4):
                      cx.inc(s.dma_start(out=Ksb[0:64, 8192 * j:8192 * (j + 1)],
                                         in_=Kh[p][0:64, 8192 * j:8192 * (j + 1)]),
                             "ldK", 16)
                      cx.inc(s.dma_start(out=Vsb[:, 4160 * j:4160 * (j + 1)],
                                         in_=Vh[p][:, 4160 * j:4160 * (j + 1)]),
                             "ldK", 16)

              def wp(t, p=p, ldK0=ldK0):
                  cx.w(t, "ldK", ldK0 + 48)
                  cx.w(t, "ldm", 16)

              def wm(t, m, ldK0=ldK0):
                  need = (8 * m + 8 + 15) // 16
                  cx.w(t, "ldK", ldK0 + 16 + 32 * need)

              def yst(s_eng, m, qs, yt, p=p):
                  return s_eng.dma_start(
                      out=y_scr[256 * m + 128 * qs: 256 * m + 128 * qs + 128,
                                p * 256:(p + 1) * 256],
                      in_=yt[:, 0:256])

              emit_attention(cx, block, "aL", Ksb=Ksb, Vsb=Vsb, qsb=qsb,
                             msksb=msksb, idf32=idf,
                             sb4=sb4, yb=yb, ysb=ysb, ytok=ytok, rlt=rlt,
                             kcol_fn=lambda kb, hl: (kb * 4 + hl) * 128,
                             vcol_fn=lambda kb, hl: kb * 260 + hl * 65,
                             y_store_fn=yst, wait_pass_fn=wp, wait_m_fn=wm)

          @block.sync
          def _(s):
              cx.w(s, "sty", cx.cnt["sty"])
              cx.w(s, "aLd", cx.cnt["aLd"])
              cx.w(s, "aLm", cx.cnt["aLm"])
              cx.w(s, "aLe", cx.cnt["aLe"])



          @block.vector
          def _(v):
              cx.w(v, "aLd", cx.cnt["aLd"])
              cx.w(v, "sty", cx.cnt["sty"])
              cx.inc(v.memset(rlt[:, 0:1], 0.0), "ph2", 1)


    xng1f = cx.sb("xng1f", [128, 8 * TOK], BF16)
    es2 = ExitStack()
    xn0sb = es2.enter_context(cx.nc.sbuf_tensor("xn0sb", [128, 8 * TOK], BF16))
    negx = es2.enter_context(cx.nc.sbuf_tensor("negx", [128, 4096], BF16))
    u1f = es2.enter_context(cx.nc.sbuf_tensor("u1f", [128, 32 * 512], BF16))
    u2f = es2.enter_context(cx.nc.sbuf_tensor("u2f", [128, 32 * 512], BF16))
    kwa = es2.enter_context(cx.nc.sbuf_tensor("kwa", [128, 2 * 1024], BF16))
    kwb = es2.enter_context(cx.nc.sbuf_tensor("kwb", [128, 2 * 1024], BF16))
    kw1 = es2.enter_context(cx.nc.sbuf_tensor("kw1", [128, 2 * 4096], BF16))
    kw2 = es2.enter_context(cx.nc.sbuf_tensor("kw2", [128, 2 * 1024], BF16))
    onesrb = es2.enter_context(cx.nc.sbuf_tensor("onesrb", [1, 128], BF16))
    r1row = es2.enter_context(cx.nc.sbuf_tensor("r1row", [1, 512], F32))
    r1bs = es2.enter_context(cx.nc.sbuf_tensor("r1bs", [128, 512], F32))
    r2s = es2.enter_context(cx.nc.sbuf_tensor("r2s", [128, 4], F32))
    tmp2 = es2.enter_context(cx.nc.sbuf_tensor("tmp2", [128, 2 * 512], F32))
    f0t = es2.enter_context(cx.nc.sbuf_tensor("f0t", [128, 1024], F32))
    yt = es2.enter_context(cx.nc.sbuf_tensor("yt", [128, 1024], F32))
    x1t = es2.enter_context(cx.nc.sbuf_tensor("x1t", [128, 1024], F32))
    xn1t = es2.enter_context(cx.nc.sbuf_tensor("xn1t", [128, 1024], F32))
    lbS1 = es2.enter_context(cx.nc.sbuf_tensor("lbS1", [128, 64], F32))
    lbS2 = es2.enter_context(cx.nc.sbuf_tensor("lbS2", [128, 64], F32))
    lbM = es2.enter_context(cx.nc.sbuf_tensor("lbM", [128, 64], F32))
    lbV = es2.enter_context(cx.nc.sbuf_tensor("lbV", [128, 64], F32))
    lbR = es2.enter_context(cx.nc.sbuf_tensor("lbR", [128, 64], F32))
    lbSh = es2.enter_context(cx.nc.sbuf_tensor("lbSh", [128, 64], F32))
    lbS2r = es2.enter_context(cx.nc.sbuf_tensor("lbS2r", [128, 64], F32))
    lbR2 = es2.enter_context(cx.nc.sbuf_tensor("lbR2", [128, 64], F32))
    xngt = es2.enter_context(cx.nc.sbuf_tensor("xngt", [128, 1024], F32))
    pp0 = es2.enter_context(cx.nc.psum_tensor("pp0", [128, 512], F32))
    pp1 = es2.enter_context(cx.nc.psum_tensor("pp1", [128, 512], F32))
    prr = es2.enter_context(cx.nc.psum_tensor("prr", [128, 512], F32))
    pq0 = es2.enter_context(cx.nc.psum_tensor("pq0", [128, 512], F32))
    pq1 = es2.enter_context(cx.nc.psum_tensor("pq1", [128, 512], F32))
    pq2 = es2.enter_context(cx.nc.psum_tensor("pq2", [128, 512], F32))
    pq3 = es2.enter_context(cx.nc.psum_tensor("pq3", [128, 512], F32))
    if True:
        pp = (pp0, pp1)
        pq = (pq0, pq1, pq2, pq3)

        @block.sync
        def _(s):
            cx.inc(s.dma_start(out=xn0sb[:].rearrange("p (i t) -> p i t", t=TOK),
                               in_=xn0fi.rearrange("p (i t) -> p i t", t=TOK)),
                   "lxn", 16)

        @block.vector
        def _(v):
            cx.inc(v.memset(onesrb[:], 1.0), "bse", 1)

        for tt in range(4):
            tb = tt * 512
            @block.vector
            def _(v, tt=tt, tb=tb):
                cx.w(v, "lxn", 16)
                if tt > 0:
                    cx.w(v, "f0p", cx.cnt.get("f0p", 0))
                if tt > 0:
                    cx.w(v, "f0p", cx.cnt.get("f0p", 0))
                cx.inc(v.tensor_scalar_mul(
                    negx[:].rearrange("p (i t) -> p i t", t=512),
                    xn0sb[:].rearrange("p (i t) -> p i t", t=TOK)[:, :, tb:tb + 512],
                    -1.0), "ffd", 1)

            f0p_base = cx.cnt.get("f0p", 0)
            f0a_base = cx.cnt.get("f0a", 0)
            lka_base = cx.cnt.get("lka", 0)
            @block.gpsimd
            def _(g, tt=tt):
                if tt == 0 and "ph2" in cx.cnt:
                    cx.w(g, "ph2", 1)
                for oc in range(32):
                    if oc >= 2:
                        cx.w(g, "f0p", f0p_base + oc - 1)
                    elif tt > 0:
                        cx.w(g, "f0p", f0p_base)
                    cx.inc(g.dma_start(out=kwa[:, (oc % 2) * 1024:(oc % 2) * 1024 + 1024],
                                       in_=k0a_l[oc]), "lka", 16)
                    cx.inc(g.dma_start(out=kwb[:, (oc % 2) * 1024:(oc % 2) * 1024 + 1024],
                                       in_=k0b_l[oc]), "lka", 16)

            @block.tensor
            def _(t, tt=tt, tb=tb):
                if tt == 0 and upto != "ffn":
                    cx.w(t, "aLd", cx.cnt["aLd"])
                    cx.w(t, "aLy", cx.cnt["aLy"])
                cx.w(t, "ffd", cx.cnt["ffd"])
                if cx.cnt.get("f2d", 0) > 0:
                    cx.w(t, "f2d", cx.cnt["f2d"])
                for oc in range(32):
                    cx.w(t, "lka", lka_base + (oc + 1) * 32)
                    if oc >= 2:
                        cx.w(t, "f0a", f0a_base + oc - 1)
                    po = (oc % 2) * 1024
                    pairs = ([(kwa[:, po + ic * 128: po + ic * 128 + 128],
                               xn0sb[:, ic * TOK + tb: ic * TOK + tb + 512])
                              for ic in range(8)] +
                             [(kwb[:, po + ic * 128: po + ic * 128 + 128],
                               negx[:, ic * 512: ic * 512 + 512])
                              for ic in range(8)])
                    mm_acc(cx, t, pp[oc % 2][:, 0:512], pairs, "f0p")

            @block.scalar
            def _(s, tt=tt):
                for oc in range(32):
                    cx.w(s, "f0p", f0p_base + oc + 1)
                    cx.inc(s.activation(out=u1f[:, oc * 512:(oc + 1) * 512],
                                        in_=pp[oc % 2][:, 0:512], func=AF.Relu,
                                        scale=scal[:, 0:1]), "f0a", 1)

            r1p_base = cx.cnt.get("r1p", 0)
            @block.tensor
            def _(t, tt=tt):
                if "r2e" in cx.cnt:
                    cx.w(t, "r2e", cx.cnt["r2e"])
                if cx.cnt.get("f2d", 0) > 0:
                    cx.w(t, "f2d", cx.cnt["f2d"])
                for oc in range(32):
                    cx.w(t, "f0a", f0a_base + oc + 1)
                    ins = t.matmul(prr[0:1, 0:512], ones1b[:, 0:1],
                                   u1f[:, oc * 512:(oc + 1) * 512],
                                   start=(oc == 0), stop=(oc == 31))
                    if oc == 31:
                        cx.inc(ins, "r1p", 1)

            @block.vector
            def _(v, tt=tt):
                cx.w(v, "r1p", r1p_base + 1)
                cx.inc(v.tensor_copy(r1row[0:1, 0:512], prr[0:1, 0:512]), "r1e", 1)

            @block.tensor
            def _(t, tt=tt):
                cx.w(t, "r1e", cx.cnt["r1e"])
                cx.w(t, "bse", cx.cnt["bse"])
                if cx.cnt.get("f2d", 0) > 0:
                    cx.w(t, "f2d", cx.cnt["f2d"])
                cx.inc(t.matmul(pq0[:, 0:512], onesrf[0:1, :], r1row[0:1, :],
                                start=True, stop=True), "r1p", 1)

            @block.vector
            def _(v, tt=tt):
                cx.w(v, "r1p", r1p_base + 2)
                cx.inc(v.tensor_scalar(r1bs[:], pq0[:, 0:512], scal[:, 2:3], None,
                                       ALU.mult), "r1e", 1)

            f1p_base = cx.cnt.get("f1p", 0)
            f1a_base = cx.cnt.get("f1a", 0)
            f1d_base = cx.cnt.get("f1d", 0)
            lk1_base = cx.cnt.get("lk1", 0)
            @block.gpsimd
            def _(g, tt=tt):
                for oc in range(32):
                    if oc >= 2:
                        cx.w(g, "f1p", f1p_base + oc - 1)
                    elif tt > 0:
                        cx.w(g, "f1p", f1p_base)
                    cx.inc(g.dma_start(out=kw1[:, (oc % 2) * 4096:(oc % 2) * 4096 + 4096],
                                       in_=k1_l[oc]), "lk1", 16)

            @block.tensor
            def _(t, tt=tt):
                cx.w(t, "f0a", f0a_base + 32)
                for oc in range(32):
                    cx.w(t, "lk1", lk1_base + (oc + 1) * 16)
                    if oc >= 2:
                        cx.w(t, "f1d", f1d_base + oc - 1)
                    po = (oc % 2) * 4096
                    pairs = [(kw1[:, po + ic * 128: po + ic * 128 + 128],
                              u1f[:, ic * 512: ic * 512 + 512])
                             for ic in range(32)]
                    mm_acc(cx, t, pp[oc % 2][:, 0:512], pairs, "f1p")

            @block.vector
            def _(v, tt=tt):
                for oc in range(32):
                    cx.w(v, "f1p", f1p_base + oc + 1)
                    if oc == 0:
                        cx.w(v, "r1e", cx.cnt["r1e"])
                    if oc >= 2:
                        cx.w(v, "f1a", f1a_base + oc - 1)
                    cx.inc(v.scalar_tensor_tensor(
                        tmp2[:, (oc % 2) * 512:(oc % 2) * 512 + 512],
                        pp[oc % 2][:, 0:512], scal[:, 1:2], r1bs[:],
                        ALU.mult, ALU.subtract), "f1d", 1)

            @block.scalar
            def _(s, tt=tt):
                for oc in range(32):
                    cx.w(s, "f1d", f1d_base + oc + 1)
                    cx.inc(s.activation(out=u2f[:, oc * 512:(oc + 1) * 512],
                                        in_=tmp2[:, (oc % 2) * 512:(oc % 2) * 512 + 512],
                                        func=AF.Relu), "f1a", 1)

            r2p_base = cx.cnt.get("r2p", 0)
            @block.tensor
            def _(t, tt=tt):
                first = True
                for ts in range(4):
                    for ic in range(32):
                        cx.w(t, "f1a", f1a_base + ic + 1)
                        ins = t.matmul(prr[0:128, ts:ts + 1],
                                       u2f[:, ic * 512 + ts * 128: ic * 512 + ts * 128 + 128],
                                       ones1b[:, 0:1],
                                       start=first, stop=(ts == 3 and ic == 31))
                        first = False
                        if ts == 3 and ic == 31:
                            cx.inc(ins, "r2p", 1)

            @block.vector
            def _(v, tt=tt):
                cx.w(v, "r2p", r2p_base + 1)
                cx.inc(v.tensor_scalar(r2s[:], prr[:, 0:4], scal[:, 4:5], None,
                                       ALU.mult), "r2e", 1)

            lk2_base = cx.cnt.get("lk2", 0)
            f2p_base = cx.cnt.get("f2p", 0)
            for R in range(2):
                @block.gpsimd
                def _(g, tt=tt, R=R):
                    for ic in range(32):
                        gi = R * 32 + ic
                        if gi >= 2:
                            cx.w(g, "f2p", f2p_base + (gi - 1) * 4)
                        elif tt > 0 or R > 0:
                            cx.w(g, "f2p", f2p_base)
                        cx.inc(g.dma_start(
                            out=kw2[:, (gi % 2) * 1024:(gi % 2) * 1024 + 1024],
                            in_=k2_l[ic]), "lk2", 16)

                @block.tensor
                def _(t, tt=tt, R=R):
                    if cx.cnt.get("f2d", 0) > 0:
                        cx.w(t, "f2d", cx.cnt["f2d"])
                    cx.w(t, "f1a", f1a_base + 32)
                    for ic in range(32):
                        gi = R * 32 + ic
                        cx.w(t, "lk2", lk2_base + (gi + 1) * 16)
                        for tsl in range(2):
                            ts = R * 2 + tsl
                            for half in range(2):
                                ins = t.matmul(
                                    pq[tsl * 2 + half][:, 0:512],
                                    u2f[:, ic * 512 + ts * 128: ic * 512 + ts * 128 + 128],
                                    kw2[:, (gi % 2) * 1024 + half * 512:
                                        (gi % 2) * 1024 + half * 512 + 512],
                                    start=(ic == 0), stop=(ic == 31))
                                cx.inc(ins, "f2p", 1)

                for tsl in range(2):
                    ts = R * 2 + tsl

                    @block.sync
                    def _(s, tt=tt, ts=ts):
                        if cx.cnt.get("x1d", 0) > 0:
                            cx.w(s, "x1d", cx.cnt["x1d"])
                        r0 = tt * 512 + ts * 128
                        cx.inc(s.dma_start(out=x1t[:], in_=x0[r0:r0 + 128, :]), "lx0", 16)
                        cx.inc(s.dma_start(out=yt[:], in_=y_scr[r0:r0 + 128, :]), "lx0", 16)

                    @block.vector
                    def _(v, tt=tt, R=R, tsl=tsl, ts=ts):
                        cx.w(v, "f2p", f2p_base + R * 128 + 128)
                        cx.w(v, "r2e", cx.cnt["r2e"])
                        cx.w(v, "lx0", cx.cnt["lx0"])
                        for half in range(2):
                            v.tensor_scalar(f0t[:, half * 512:(half + 1) * 512],
                                            pq[tsl * 2 + half][:, 0:512],
                                            scal[:, 3:4], r2s[:, ts:ts + 1],
                                            ALU.mult, ALU.subtract)
                        if upto != "ffn":
                            v.tensor_add(x1t[:], x1t[:], yt[:])
                        cx.inc(v.tensor_add(x1t[:], x1t[:], f0t[:]), "f2d", 1)
                        cx.w(v, "f2d", cx.cnt["f2d"])
                        cx.inc(v.tensor_copy(f0t[:, 0:128], x1t[:, 0:128]), "f2s", 1)

                    @block.sync
                    def _(s, tt=tt, ts=ts):
                        r0 = tt * 512 + ts * 128
                        cx.w(s, "f2s", cx.cnt["f2s"])
                        cx.inc(s.dma_start(out=o_x1[r0:r0 + 128, :], in_=x1t[:]),
                               "x1d", 16)

        for j in range(NSLOT):
            @block.sync
            def _(s, j=j):
                if j < 2:
                    cx.w(s, "x1d", cx.cnt["x1d"])
                else:
                    cx.w(s, "b1d", 2 * (j - 1))
                cx.inc(s.dma_start(
                    out=(x1t if j % 2 == 0 else yt)[:],
                    in_=o_x1[j * 128:(j + 1) * 128, :]), "lb1", 16)

            @block.vector
            def _(v, j=j):
                cx.w(v, "lb1", (j + 1) * 16)
                xv = (x1t if j % 2 == 0 else yt)[:]
                cx.inc(v.reduce_sum(lbS1[:, 4 * j:4 * j + 1], xv, AX.X), "b1d", 1)
                v.tensor_mul(tmp2[:, 0:512], xv[:, 0:512], xv[:, 0:512])
                v.tensor_mul(tmp2[:, 512:1024], xv[:, 512:1024], xv[:, 512:1024])
                cx.w(v, "b1d", cx.cnt["b1d"])
                cx.inc(v.reduce_sum(lbS2[:, 4 * j:4 * j + 1], tmp2[:], AX.X), "b1d", 1)

        @block.vector
        def _(v):
            cx.w(v, "b1d", 2 * NSLOT)
            s1v = lbS1[:].rearrange("p (j f) -> p j f", f=4)[:, :, 0]
            s2v = lbS2[:].rearrange("p (j f) -> p j f", f=4)[:, :, 0]
            mv = lbM[:].rearrange("p (j f) -> p j f", f=4)[:, :, 0]
            vv = lbV[:].rearrange("p (j f) -> p j f", f=4)[:, :, 0]
            sv = lbSh[:].rearrange("p (j f) -> p j f", f=4)[:, :, 0]
            cx.inc(v.tensor_scalar_mul(mv, s1v, 1.0 / 1024), "bcm", 1)
            cx.w(v, "bcm", cx.cnt["bcm"])
            cx.inc(v.tensor_scalar_mul(vv, s2v, 1.0 / 1024), "bcm", 1)
            cx.w(v, "bcm", cx.cnt["bcm"])
            cx.inc(v.tensor_mul(sv, mv, mv), "bcm", 1)
            cx.w(v, "bcm", cx.cnt["bcm"])
            cx.inc(v.tensor_sub(vv, vv, sv), "bcm", 1)
            cx.w(v, "bcm", cx.cnt["bcm"])
            cx.inc(v.tensor_copy(sv, vv), "bcm", 1)

        @block.scalar
        def _(s):
            cx.w(s, "bcm", cx.cnt["bcm"])
            vv = lbV[:].rearrange("p (j f) -> p j f", f=4)[:, :, 0]
            rv_ = lbR[:].rearrange("p (j f) -> p j f", f=4)[:, :, 0]
            r2v = lbR[:].rearrange("p (j f) -> p j f", f=4)[:, :, 1]
            sv = lbSh[:].rearrange("p (j f) -> p j f", f=4)[:, :, 1]
            cx.inc(s.activation(out=r2v, in_=vv, func=AF.Ln, bias=scal[:, 5:6]),
                   "bca", 1)
            cx.w(s, "bca", cx.cnt["bca"])
            cx.inc(s.activation(out=rv_, in_=r2v, func=AF.Exp, scale=-0.5), "bca", 1)
            cx.w(s, "bca", cx.cnt["bca"])
            cx.inc(s.activation(out=sv, in_=rv_, func=AF.Copy), "bca", 1)

        for j in range(NSLOT):
            @block.sync
            def _(s, j=j):
                if j < 2:
                    cx.w(s, "b1d", 2 * NSLOT)
                else:
                    cx.w(s, "b2d", 2 * (j - 1))
                cx.inc(s.dma_start(
                    out=(x1t if j % 2 == 0 else yt)[:],
                    in_=o_x1[j * 128:(j + 1) * 128, :]), "lb2", 16)

            @block.vector
            def _(v, j=j):
                cx.w(v, "lb2", (j + 1) * 16)
                if j == 0:
                    cx.w(v, "bca", cx.cnt["bca"])
                if j >= 1:
                    cx.w(v, "x2d", cx.cnt["x2d"])
                xv = (x1t if j % 2 == 0 else yt)[:]
                cx.inc(v.tensor_scalar(xn1t[:], xv, lbM[:, 4 * j:4 * j + 1],
                                       lbR[:, 4 * j:4 * j + 1],
                                       ALU.subtract, ALU.mult), "b2d", 1)
                v.tensor_mul(tmp2[:, 0:512], xn1t[:, 0:512], xn1t[:, 0:512])
                v.tensor_mul(tmp2[:, 512:1024], xn1t[:, 512:1024], xn1t[:, 512:1024])
                cx.w(v, "b2d", cx.cnt["b2d"])
                cx.inc(v.reduce_sum(lbS2r[:, 4 * j:4 * j + 1], tmp2[:], AX.X),
                       "b2d", 1)

            @block.sync
            def _(s, j=j):
                cx.w(s, "b2d", cx.cnt["b2d"])
                cx.inc(s.dma_start(out=o_xn1[j * 128:(j + 1) * 128, :],
                                   in_=xn1t[:]), "x2d", 16)

            @block.gpsimd
            def _(g, j=j):
                cx.w(g, "b2d", cx.cnt["b2d"])
                cx.inc(g.dma_start(out=o_xn1b[j * 128:(j + 1) * 128, :],
                                   in_=xn1t[:]), "xbd", 16)

        @block.scalar
        def _(s):
            cx.w(s, "b2d", 2 * NSLOT)
            s2v = lbS2r[:].rearrange("p (j f) -> p j f", f=4)[:, :, 0]
            rv_ = lbR2[:].rearrange("p (j f) -> p j f", f=4)[:, :, 0]
            r2v = lbR2[:].rearrange("p (j f) -> p j f", f=4)[:, :, 1]
            sv = lbR2[:].rearrange("p (j f) -> p j f", f=4)[:, :, 2]
            cx.inc(s.activation(out=r2v, in_=s2v, func=AF.Ln, scale=1.0 / 1024,
                                bias=scal[:, 6:7]), "bca", 1)
            cx.w(s, "bca", cx.cnt["bca"])
            cx.inc(s.activation(out=rv_, in_=r2v, func=AF.Exp, scale=-0.5), "bca", 1)
            cx.w(s, "bca", cx.cnt["bca"])
            cx.inc(s.activation(out=sv, in_=rv_, func=AF.Copy), "bca", 1)

        for j in range(NSLOT):
            @block.sync
            def _(s, j=j):
                if j < 2:
                    cx.w(s, "x2d", cx.cnt["x2d"])
                else:
                    cx.w(s, "b3d", j - 1)
                cx.inc(s.dma_start(
                    out=(x1t if j % 2 == 0 else yt)[:],
                    in_=o_xn1[j * 128:(j + 1) * 128, :]), "lb3", 16)

            @block.vector
            def _(v, j=j):
                cx.w(v, "lb3", (j + 1) * 16)
                if j == 0:
                    cx.w(v, "bca", cx.cnt["bca"])
                if j >= 1:
                    cx.w(v, "l1t", 8 * j)
                xv = (x1t if j % 2 == 0 else yt)[:]
                v.tensor_scalar(xngt[:], xv, lbR2[:, 4 * j:4 * j + 1], None,
                                ALU.mult)
                cx.inc(v.tensor_mul(xngt[:], xngt[:], n1wb[:]), "b3d", 1)

            @block.tensor
            def _(t, j=j):
                cx.w(t, "b3d", cx.cnt["b3d"])
                for ci in range(8):
                    gi = j * 8 + ci
                    if gi >= 2:
                        cx.w(t, "l1e", gi - 1)
                    cx.inc(t.transpose(pp[ci % 2][0:128, 0:128],
                                       xngt[:, ci * 128:(ci + 1) * 128],
                                       idf[:]), "l1t", 1)

            @block.vector
            def _(v, j=j):
                for ci in range(8):
                    cx.w(v, "l1t", j * 8 + ci + 1)
                    cx.inc(v.tensor_copy(
                        xng1f[:, ci * TOK + j * 128: ci * TOK + (j + 1) * 128],
                        pp[ci % 2][:, 0:128]), "l1e", 1)

    es2.close()
    if True:
        es3 = ExitStack()
        gwq_sb = es3.enter_context(cx.nc.sbuf_tensor("gwq_sb", [128, 8 * 1024], BF16))
        gkvd_sb = es3.enter_context(cx.nc.sbuf_tensor("gkvd_sb", [128, 8 * 256], BF16))
        gku_sb = es3.enter_context(cx.nc.sbuf_tensor("gku_sb", [128, 2 * 256], BF16))
        gvu_sb = es3.enter_context(cx.nc.sbuf_tensor("gvu_sb", [128, 2 * 256], BF16))
        latf = es3.enter_context(cx.nc.sbuf_tensor("latf", [128, 2 * TOK], BF16))
        ropeb = es3.enter_context(cx.nc.sbuf_tensor("ropeb", [128, 2 * 512], F32))
        ropebk = es3.enter_context(cx.nc.sbuf_tensor("ropebk", [128, 2 * 128], F32))
        qtw = es3.enter_context(cx.nc.sbuf_tensor("qtw", [128, 1024], F32))
        qtn = es3.enter_context(cx.nc.sbuf_tensor("qtn", [128, 1024], F32))
        ktw = es3.enter_context(cx.nc.sbuf_tensor("ktw", [128, 256], F32))
        ktn = es3.enter_context(cx.nc.sbuf_tensor("ktn", [128, 256], F32))
        rA = es3.enter_context(cx.nc.sbuf_tensor("rA", [128, 512], F32))
        rB = es3.enter_context(cx.nc.sbuf_tensor("rB", [128, 512], F32))
        hsqf = es3.enter_context(cx.nc.sbuf_tensor("hsqf", [128, 1024], F32))
        hst = es3.enter_context(cx.nc.sbuf_tensor("hst", [128, 48], F32))
        qrb = es3.enter_context(cx.nc.sbuf_tensor("qrb", [128, 1024], BF16))
        krb = es3.enter_context(cx.nc.sbuf_tensor("krb", [128, 256], BF16))
        vtb = es3.enter_context(cx.nc.sbuf_tensor("vtb", [128, 256], BF16))
        if True:
            @block.gpsimd
            def _(g):
                cx.w(g, "l1e", cx.cnt["l1e"])
                cx.w(g, "x2d", cx.cnt["x2d"])
                cx.w(g, "b3d", cx.cnt["b3d"])
                cx.w(g, "f2p", cx.cnt["f2p"])
                cx.w(g, "f1a", cx.cnt["f1a"])
                cx.inc(g.dma_start(out=gwq_sb[:].rearrange("p (i c) -> p i c", c=1024),
                                   in_=g_wq.rearrange("(i p) c -> p i c", p=128)), "lgw", 16)
                cx.inc(g.dma_start(out=gkvd_sb[:].rearrange("p (i c) -> p i c", c=256),
                                   in_=g_kvd.rearrange("(i p) c -> p i c", p=128)), "lgw", 16)
                cx.inc(g.dma_start(out=gku_sb[:].rearrange("p (i c) -> p i c", c=256),
                                   in_=g_ku.rearrange("(i p) c -> p i c", p=128)), "lgw", 16)
                cx.inc(g.dma_start(out=gvu_sb[:].rearrange("p (i c) -> p i c", c=256),
                                   in_=g_vu.rearrange("(i p) c -> p i c", p=128)), "lgw", 16)

            nL1E = cx.cnt["l1e"]
            jobs = []
            for oc in range(2):
                for tc in range(4):
                    pairs = [(wchunked(gkvd_sb, ic, 256, oc * 128, 128),
                              wchunked(xng1f, ic, TOK, tc * 512, 512))
                             for ic in range(8)]
                    jobs.append((pairs, 512, cp_to_gen(cx, latf, oc * TOK + tc * 512, 512)))
            fproj(cx, block, "lat", jobs, (pq0, pq1),
                  wait_tensor=lambda t: (cx.w(t, "l1e", nL1E), cx.w(t, "lgw", 64)))

            for j in range(NSLOT):
                @block.sync
                def _(s, j=j):
                    if j >= 1:
                        cx.w(s, "rpd", cx.cnt["rpd"])
                    else:
                        cx.w(s, "lgw", 64)
                    cx.inc(s.dma_start(out=ropeb[:, 0:512],
                                       in_=cg512[j * 128:(j + 1) * 128, :]), "lrp", 16)
                    cx.inc(s.dma_start(out=ropeb[:, 512:1024],
                                       in_=sg512[j * 128:(j + 1) * 128, :]), "lrp", 16)
                    cx.inc(s.dma_start(out=ropebk[:, 0:128],
                                       in_=cg128[j * 128:(j + 1) * 128, :]), "lrp", 16)
                    cx.inc(s.dma_start(out=ropebk[:, 128:256],
                                       in_=sg128[j * 128:(j + 1) * 128, :]), "lrp", 16)

                @block.tensor
                def _(t, j=j):
                    if j == 0:
                        cx.w(t, "latv", cx.cnt["latv"])
                        cx.w(t, "lgw", 64)
                    else:
                        cx.w(t, "pre", cx.cnt["pre"])
                    for oc in range(2):
                        pairs = [(xng1f[:, ic * TOK + j * 128: ic * TOK + j * 128 + 128],
                                  wchunked(gwq_sb, ic, 1024, oc * 512, 512))
                                 for ic in range(8)]
                        mm_acc(cx, t, pq[oc][:, 0:512], pairs, "prp")
                    pairs = [(latf[:, lc * TOK + j * 128: lc * TOK + j * 128 + 128],
                              wchunked(gku_sb, lc, 256, 0, 256)) for lc in range(2)]
                    mm_acc(cx, t, pq2[:, 0:256], pairs, "prp")
                    pairs = [(latf[:, lc * TOK + j * 128: lc * TOK + j * 128 + 128],
                              wchunked(gvu_sb, lc, 256, 0, 256)) for lc in range(2)]
                    mm_acc(cx, t, pq3[:, 0:256], pairs, "prp")

                @block.vector
                def _(v, j=j):
                    cx.w(v, "prp", cx.cnt["prp"])
                    cx.w(v, "lrp", cx.cnt["lrp"])
                    if j >= 1:
                        cx.w(v, "stqkv", cx.cnt["stqkv"])
                    v.tensor_copy(qtw[:, 0:512], pq[0][:, 0:512])
                    v.tensor_copy(qtw[:, 512:1024], pq[1][:, 0:512])
                    v.tensor_copy(ktw[:], pq2[:, 0:256])
                    cx.inc(v.tensor_copy(vtb[:], pq3[:, 0:256]), "pre", 1)
                    emit_rope(v, qtn, qtw, ropeb[:, 0:512], ropeb[:, 512:1024],
                              16, rA, rB)
                    emit_rope(v, ktn, ktw, ropebk[:, 0:128], ropebk[:, 128:256],
                              4, rA, rB)
                    v.tensor_mul(hsqf[:], qtn[:], qtn[:])
                    cx.inc(v.tensor_reduce(
                        hst[:, 0:16], hsqf[:].rearrange("p (h d) -> p h d", d=64),
                        AX.X, ALU.add), "rpd", 1)
                    v.tensor_mul(hsqf[:, 0:256], ktn[:], ktn[:])
                    cx.inc(v.tensor_reduce(
                        hst[:, 16:20],
                        hsqf[:, 0:256].rearrange("p (h d) -> p h d", d=64),
                        AX.X, ALU.add), "rpd", 1)

                @block.scalar
                def _(s, j=j):
                    cx.w(s, "rpd", cx.cnt["rpd"])
                    s.activation(out=hst[:, 20:40], in_=hst[:, 0:20], func=AF.Ln,
                                 scale=1.0 / 64, bias=scal[:, 6:7])
                    cx.inc(s.activation(out=hst[:, 20:40], in_=hst[:, 20:40],
                                        func=AF.Exp, scale=-0.5), "rpa", 1)

                @block.vector
                def _(v, j=j):
                    cx.w(v, "rpa", cx.cnt["rpa"])
                    for h in range(16):
                        v.tensor_scalar(qrb[:, h * 64:(h + 1) * 64],
                                        qtn[:, h * 64:(h + 1) * 64],
                                        hst[:, 20 + h: 21 + h], None, ALU.mult)
                    for h in range(4):
                        v.tensor_scalar(krb[:, h * 64:(h + 1) * 64],
                                        ktn[:, h * 64:(h + 1) * 64],
                                        hst[:, 36 + h: 37 + h], None, ALU.mult)
                    cx.inc(v.tensor_copy(krb[:, 0:1], krb[:, 0:1]), "rpd", 1)

                @block.sync
                def _(s, j=j):
                    cx.w(s, "rpd", cx.cnt["rpd"])
                    cx.w(s, "pre", cx.cnt["pre"])
                    r0 = j * 128
                    cx.inc(s.dma_start(out=o_q1r[r0:r0 + 128, :], in_=qrb[:]),
                           "stqkv", 16)
                    cx.inc(s.dma_start(out=o_k1r[r0:r0 + 128, :], in_=krb[:]),
                           "stqkv", 16)
                    cx.inc(s.dma_start(out=o_v1[r0:r0 + 128, :], in_=vtb[:]),
                           "stqkv", 16)

            @block.sync
            def _(s):
                cx.w(s, "stqkv", cx.cnt["stqkv"])
                cx.w(s, "x1d", cx.cnt["x1d"])
                cx.w(s, "x2d", cx.cnt["x2d"])
                cx.w(s, "xbd", cx.cnt["xbd"])
                if "sty" in cx.cnt:
                    cx.w(s, "sty", cx.cnt["sty"])
        es3.close()

    cx.close()
    return cx




def build_C():
    cx = Cx("C")
    qh = [cx.din(f"qh{p}", [64, 4 * TOK], BF16) for p in range(4)]
    Kh = [cx.din(f"Kh{p}", [64, NBLK * 128], BF16) for p in range(4)]
    Vh = [cx.din(f"Vh{p}", [128, NBLK * 65], BF16) for p in range(4)]
    mskt = cx.din("mskt", [128, NSLOT * 4 * 128], BF16)
    xn1i = cx.din("xn1i", [TOK, D], F32)
    g_wo = cx.din("g_wo", [1024, 1024], F32)
    l_wq = cx.din("l_wq", [1024, 1024], F32)
    l_wk = cx.din("l_wk", [1024, 256], F32)
    l_wv = cx.din("l_wv", [1024, 256], F32)
    n2w = cx.din("n2w", [1, 1024], F32)
    cl512 = cx.din("cl512", [TOK, 512], F32)
    sl512 = cx.din("sl512", [TOK, 512], F32)
    cl128 = cx.din("cl128", [TOK, 128], F32)
    sl128 = cx.din("sl128", [TOK, 128], F32)
    identf = cx.din("identf", [128, 128], F32)
    identb = cx.din("identb", [128, 128], BF16)
    epsC = cx.din("epsC", [128, 2], F32)
    o_u = cx.dout("u", [TOK, D], F32)
    o_q2r = cx.dout("q2r", [TOK, 1024], BF16)
    o_k2r = cx.dout("k2r", [TOK, 256], BF16)
    o_v2 = cx.dout("v2", [TOK, 256], BF16)
    y_scr = cx.nc.dram_tensor("y_scr", [TOK, D], F32, kind="Internal").ap()

    idf = cx.sb("idf", [128, 128], F32)
    idb = cx.sb("idbC", [128, 128], BF16)
    eps6 = cx.sb("eps6", [128, 2], F32)
    n2wb = cx.sb("n2wb", [128, 1024], F32)
    n2ws = cx.sb("n2ws", [1, 1024], F32)
    onesrf = cx.sb("onesrf", [1, 128], F32)
    xng2f = cx.sb("xng2f", [128, 8 * TOK], BF16)

    block = cx.block()

    @block.sync
    def _(s):
        cx.inc(s.dma_start(out=idf[:], in_=identf), "ldc", 16)
        cx.inc(s.dma_start(out=idb[:], in_=identb), "ldc", 16)
        cx.inc(s.dma_start(out=eps6[:], in_=epsC), "ldc", 16)
        cx.inc(s.dma_start(out=n2ws[:], in_=n2w), "ldc", 16)

    @block.vector
    def _(v):
        cx.inc(v.memset(onesrf[:], 1.0), "ini", 1)

    with cx.nc.psum_tensor("psetC", [128, 512], F32) as pset:
        for hf in range(2):
            @block.tensor
            def _(t, hf=hf):
                if hf == 0:
                    cx.w(t, "ldc")
                    cx.w(t, "ini", 1)
                else:
                    cx.w(t, "bse", 1)
                cx.inc(t.matmul(pset[:, 0:512], onesrf[0:1, :],
                                n2ws[0:1, hf * 512:(hf + 1) * 512],
                                start=True, stop=True), "bps", 1)

            @block.vector
            def _(v, hf=hf):
                cx.w(v, "bps", 1 + hf)
                cx.inc(v.tensor_copy(n2wb[:, hf * 512:(hf + 1) * 512],
                                     pset[:, 0:512]), "bse", 1)

    with (
        cx.nc.sbuf_tensor("KsbC", [64, NBLK * 128], BF16) as Ksb,
        cx.nc.sbuf_tensor("VsbC", [128, NBLK * 65], BF16) as Vsb,
        cx.nc.sbuf_tensor("qsbC", [64, 4 * TOK], BF16) as qsb,
        cx.nc.sbuf_tensor("mskC", [128, NSLOT * 4 * 128], BF16) as msksb,
        cx.nc.sbuf_tensor("PTC", [128, 2 * 1024], BF16) as PTb,
        cx.nc.sbuf_tensor("ysbC", [128, 1024], F32) as ysb,
        cx.nc.sbuf_tensor("ytokC", [128, 256], F32) as ytok,
        cx.nc.sbuf_tensor("rltC", [128, 2], F32) as rlt,
        cx.nc.psum_tensor("sc0", [128, 512], F32) as sc0,
        cx.nc.psum_tensor("sc1", [128, 512], F32) as sc1,
        cx.nc.psum_tensor("sc2", [128, 512], F32) as sc2,
        cx.nc.psum_tensor("sc3", [128, 512], F32) as sc3,
        cx.nc.psum_tensor("yc0", [128, 512], F32) as yc0,
        cx.nc.psum_tensor("yc1", [128, 512], F32) as yc1,
    ):
        cx.PT = PTb
        sb4 = (sc0, sc1, sc2, sc3)
        yb = (yc0, yc1)

        @block.sync
        def _(s):
            cx.inc(s.dma_start(out=msksb[:], in_=mskt), "ldm", 16)

        for p in range(4):
            ldK0 = cx.cnt.get("ldK", 0)

            @block.sync
            def _(s, p=p):
                if p >= 1:
                    cx.w(s, "aLp", cx.cnt["aLp"])
                    cx.w(s, "aLd", cx.cnt["aLd"])
                cx.inc(s.dma_start(out=qsb[:], in_=qh[p]), "ldK", 16)
                for j in range(4):
                    cx.inc(s.dma_start(out=Ksb[0:64, 2048 * j:2048 * (j + 1)],
                                       in_=Kh[p][0:64, 2048 * j:2048 * (j + 1)]),
                           "ldK", 16)
                    cx.inc(s.dma_start(out=Vsb[:, 1040 * j:1040 * (j + 1)],
                                       in_=Vh[p][:, 1040 * j:1040 * (j + 1)]),
                           "ldK", 16)

            def wp(t, p=p, ldK0=ldK0):
                cx.w(t, "ldK", ldK0 + 48)
                cx.w(t, "ldm", 16)

            def wm(t, m, ldK0=ldK0):
                need = (8 * m + 8 + 15) // 16
                cx.w(t, "ldK", ldK0 + 16 + 32 * need)

            def yst(s_eng, m, qs, yt, p=p):
                return s_eng.dma_start(
                    out=y_scr[256 * m + 128 * qs: 256 * m + 128 * qs + 128,
                              p * 256:(p + 1) * 256],
                    in_=yt[:, 0:256])

            emit_attention(cx, block, "aL", Ksb=Ksb, Vsb=Vsb, qsb=qsb,
                           msksb=msksb, idf32=idf,
                           sb4=sb4, yb=yb, ysb=ysb, ytok=ytok, rlt=rlt,
                           kcol_fn=lambda kb, hl: kb * 128,
                           vcol_fn=lambda kb, hl: kb * 65,
                           y_store_fn=yst, wait_pass_fn=wp, wait_m_fn=wm)

        @block.sync
        def _(s):
            cx.w(s, "sty", cx.cnt["sty"])
            cx.w(s, "aLd", cx.cnt["aLd"])
            cx.w(s, "aLm", cx.cnt["aLm"])
            cx.w(s, "aLe", cx.cnt["aLe"])

        @block.vector
        def _(v):
            cx.w(v, "aLd", cx.cnt["aLd"])
            cx.w(v, "sty", cx.cnt["sty"])
            cx.inc(v.memset(rlt[:, 0:1], 0.0), "ph2", 1)

    es2 = ExitStack()
    gwo_sb = es2.enter_context(cx.nc.sbuf_tensor("gwo_sb", [128, 8 * 1024], BF16))
    yf = es2.enter_context(cx.nc.sbuf_tensor("yf", [128, 8 * TOK], BF16))
    ytl = es2.enter_context(cx.nc.sbuf_tensor("ytl", [128, 2 * 1024], F32))
    ytb = es2.enter_context(cx.nc.sbuf_tensor("ytb", [128, 1024], BF16))
    ut = es2.enter_context(cx.nc.sbuf_tensor("ut", [128, 1024], F32))
    xnt = es2.enter_context(cx.nc.sbuf_tensor("xnt", [128, 1024], F32))
    sq2 = es2.enter_context(cx.nc.sbuf_tensor("sq2", [128, 1024], F32))
    lcS2 = es2.enter_context(cx.nc.sbuf_tensor("lcS2", [128, 64], F32))
    lcR = es2.enter_context(cx.nc.sbuf_tensor("lcR", [128, 64], F32))
    xngt2 = es2.enter_context(cx.nc.sbuf_tensor("xngt2", [128, 1024], F32))
    ppc0 = es2.enter_context(cx.nc.psum_tensor("ppc0", [128, 512], F32))
    ppc1 = es2.enter_context(cx.nc.psum_tensor("ppc1", [128, 512], F32))
    ptc0 = es2.enter_context(cx.nc.psum_tensor("ptc0", [128, 1024], BF16))
    ptc1 = es2.enter_context(cx.nc.psum_tensor("ptc1", [128, 1024], BF16))
    if True:
        ppc = (ppc0, ppc1)
        ptc = (ptc0, ptc1)

        @block.gpsimd
        def _(g):
            cx.w(g, "ph2", 1)
            cx.inc(g.dma_start(out=gwo_sb[:].rearrange("p (i c) -> p i c", c=1024),
                               in_=g_wo.rearrange("(i p) c -> p i c", p=128)),
                   "lgw", 16)

        for j in range(NSLOT):
            @block.gpsimd
            def _(g, j=j):
                if j == 0:
                    cx.w(g, "ph2", 1)
                if j >= 1:
                    cx.w(g, "ctt", 8 * j)
                cx.inc(g.dma_start(out=ytb[:],
                                   in_=y_scr[j * 128:(j + 1) * 128, :]), "lyt", 16)

            @block.tensor
            def _(t, j=j):
                cx.w(t, "lyt", (j + 1) * 16)
                for ci in range(8):
                    gi = j * 8 + ci
                    if gi >= 2:
                        cx.w(t, "cte", gi - 1)
                    cx.inc(t.transpose(ptc[ci % 2][:, 0:128],
                                       ytb[:, ci * 128:(ci + 1) * 128],
                                       idb[:]), "ctt", 1)

            @block.vector
            def _(v, j=j):
                for ci in range(8):
                    cx.w(v, "ctt", j * 8 + ci + 1)
                    cx.inc(v.tensor_copy(
                        yf[:, ci * TOK + j * 128: ci * TOK + (j + 1) * 128],
                        ptc[ci % 2][:, 0:128]), "cte", 1)

        for j in range(NSLOT):
            @block.sync
            def _(s, j=j):
                if j >= 1:
                    cx.w(s, "cud", cx.cnt["cud"])
                cx.inc(s.dma_start(out=xnt[:],
                                   in_=xn1i[j * 128:(j + 1) * 128, :]), "lxn", 16)

            @block.tensor
            def _(t, j=j):
                if j == 0:
                    cx.w(t, "cte", NSLOT * 8)
                    cx.w(t, "lgw", 16)
                else:
                    cx.w(t, "cue", cx.cnt["cue"])
                for oc in range(2):
                    pairs = [(yf[:, ic * TOK + j * 128: ic * TOK + j * 128 + 128],
                              wchunked(gwo_sb, ic, 1024, oc * 512, 512))
                             for ic in range(8)]
                    mm_acc(cx, t, ppc[oc][:, 0:512], pairs, "cup")

            @block.vector
            def _(v, j=j):
                cx.w(v, "cup", cx.cnt["cup"])
                cx.w(v, "lxn", cx.cnt["lxn"])
                v.tensor_add(ut[:, 0:512], xnt[:, 0:512], ppc[0][:, 0:512])
                cx.inc(v.tensor_add(ut[:, 512:1024], xnt[:, 512:1024],
                                    ppc[1][:, 0:512]), "cue", 1)
                v.tensor_mul(sq2[:, 0:512], ut[:, 0:512], ut[:, 0:512])
                v.tensor_mul(sq2[:, 512:1024], ut[:, 512:1024], ut[:, 512:1024])
                cx.w(v, "cue", cx.cnt["cue"])
                cx.inc(v.reduce_sum(lcS2[:, 4 * j:4 * j + 1], sq2[:], AX.X),
                       "cud", 1)

            @block.sync
            def _(s, j=j):
                cx.w(s, "cud", cx.cnt["cud"])
                cx.inc(s.dma_start(out=o_u[j * 128:(j + 1) * 128, :], in_=ut[:]),
                       "ust", 16)

        @block.scalar
        def _(s):
            cx.w(s, "cud", NSLOT)
            s2v = lcS2[:].rearrange("p (j f) -> p j f", f=4)[:, :, 0]
            rv_ = lcR[:].rearrange("p (j f) -> p j f", f=4)[:, :, 0]
            r2v = lcR[:].rearrange("p (j f) -> p j f", f=4)[:, :, 1]
            sv = lcR[:].rearrange("p (j f) -> p j f", f=4)[:, :, 2]
            cx.inc(s.activation(out=r2v, in_=s2v, func=AF.Ln, scale=1.0 / 1024,
                                bias=eps6[:, 1:2]), "cla", 1)
            cx.w(s, "cla", cx.cnt["cla"])
            cx.inc(s.activation(out=rv_, in_=r2v, func=AF.Exp, scale=-0.5), "cla", 1)
            cx.w(s, "cla", cx.cnt["cla"])
            cx.inc(s.activation(out=sv, in_=rv_, func=AF.Copy), "cla", 1)

        for j in range(NSLOT):
            @block.sync
            def _(s, j=j):
                if j < 2:
                    cx.w(s, "ust", cx.cnt["ust"])
                else:
                    cx.w(s, "cgd", j - 1)
                cx.inc(s.dma_start(out=(ut if j % 2 == 0 else xnt)[:],
                                   in_=o_u[j * 128:(j + 1) * 128, :]), "lu2", 16)

            @block.vector
            def _(v, j=j):
                cx.w(v, "lu2", (j + 1) * 16)
                if j == 0:
                    cx.w(v, "cla", cx.cnt["cla"])
                if j >= 1:
                    cx.w(v, "cgt", 8 * j)
                uv = (ut if j % 2 == 0 else xnt)[:]
                v.tensor_scalar(xngt2[:], uv, lcR[:, 4 * j:4 * j + 1], None,
                                ALU.mult)
                cx.inc(v.tensor_mul(xngt2[:], xngt2[:], n2wb[:]), "cgd", 1)

            @block.tensor
            def _(t, j=j):
                cx.w(t, "cgd", cx.cnt["cgd"])
                for ci in range(8):
                    gi = j * 8 + ci
                    if gi >= 2:
                        cx.w(t, "cge", gi - 1)
                    cx.inc(t.transpose(ppc[ci % 2][0:128, 0:128],
                                       xngt2[:, ci * 128:(ci + 1) * 128],
                                       idf[:]), "cgt", 1)

            @block.vector
            def _(v, j=j):
                for ci in range(8):
                    cx.w(v, "cgt", j * 8 + ci + 1)
                    cx.inc(v.tensor_copy(
                        xng2f[:, ci * TOK + j * 128: ci * TOK + (j + 1) * 128],
                        ppc[ci % 2][:, 0:128]), "cge", 1)
    es2.close()

    es3 = ExitStack()
    lwq_sb = es3.enter_context(cx.nc.sbuf_tensor("lwq_sb", [128, 8 * 1024], BF16))
    lwk_sb = es3.enter_context(cx.nc.sbuf_tensor("lwk_sb", [128, 8 * 256], BF16))
    lwv_sb = es3.enter_context(cx.nc.sbuf_tensor("lwv_sb", [128, 8 * 256], BF16))
    ropeb = es3.enter_context(cx.nc.sbuf_tensor("ropebC", [128, 2 * 512], F32))
    ropebk = es3.enter_context(cx.nc.sbuf_tensor("ropebkC", [128, 2 * 128], F32))
    qtw = es3.enter_context(cx.nc.sbuf_tensor("qtwC", [128, 1024], F32))
    qtn = es3.enter_context(cx.nc.sbuf_tensor("qtnC", [128, 1024], F32))
    ktw = es3.enter_context(cx.nc.sbuf_tensor("ktwC", [128, 256], F32))
    ktn = es3.enter_context(cx.nc.sbuf_tensor("ktnC", [128, 256], F32))
    rA = es3.enter_context(cx.nc.sbuf_tensor("rAC", [128, 512], F32))
    rB = es3.enter_context(cx.nc.sbuf_tensor("rBC", [128, 512], F32))
    hsqf = es3.enter_context(cx.nc.sbuf_tensor("hsqfC", [128, 1024], F32))
    hst = es3.enter_context(cx.nc.sbuf_tensor("hstC", [128, 48], F32))
    qrb = es3.enter_context(cx.nc.sbuf_tensor("qrbC", [128, 1024], BF16))
    krb = es3.enter_context(cx.nc.sbuf_tensor("krbC", [128, 256], BF16))
    vtb = es3.enter_context(cx.nc.sbuf_tensor("vtbC", [128, 256], BF16))
    pq0 = es3.enter_context(cx.nc.psum_tensor("pqc0", [128, 512], F32))
    pq1 = es3.enter_context(cx.nc.psum_tensor("pqc1", [128, 512], F32))
    pq2 = es3.enter_context(cx.nc.psum_tensor("pqc2", [128, 512], F32))
    pq3 = es3.enter_context(cx.nc.psum_tensor("pqc3", [128, 512], F32))
    if True:
        pqq = (pq0, pq1)

        @block.gpsimd
        def _(g):
            cx.w(g, "cge", cx.cnt["cge"])
            cx.w(g, "ust", cx.cnt["ust"])
            cx.inc(g.dma_start(out=lwq_sb[:].rearrange("p (i c) -> p i c", c=1024),
                               in_=l_wq.rearrange("(i p) c -> p i c", p=128)), "llw", 16)
            cx.inc(g.dma_start(out=lwk_sb[:].rearrange("p (i c) -> p i c", c=256),
                               in_=l_wk.rearrange("(i p) c -> p i c", p=128)), "llw", 16)
            cx.inc(g.dma_start(out=lwv_sb[:].rearrange("p (i c) -> p i c", c=256),
                               in_=l_wv.rearrange("(i p) c -> p i c", p=128)), "llw", 16)

        for j in range(NSLOT):
            @block.sync
            def _(s, j=j):
                if j >= 1:
                    cx.w(s, "rpd", cx.cnt["rpd"])
                cx.inc(s.dma_start(out=ropeb[:, 0:512],
                                   in_=cl512[j * 128:(j + 1) * 128, :]), "lrp", 16)
                cx.inc(s.dma_start(out=ropeb[:, 512:1024],
                                   in_=sl512[j * 128:(j + 1) * 128, :]), "lrp", 16)
                cx.inc(s.dma_start(out=ropebk[:, 0:128],
                                   in_=cl128[j * 128:(j + 1) * 128, :]), "lrp", 16)
                cx.inc(s.dma_start(out=ropebk[:, 128:256],
                                   in_=sl128[j * 128:(j + 1) * 128, :]), "lrp", 16)

            @block.tensor
            def _(t, j=j):
                if j == 0:
                    cx.w(t, "llw", 48)
                else:
                    cx.w(t, "pre", cx.cnt["pre"])
                for oc in range(2):
                    pairs = [(xng2f[:, ic * TOK + j * 128: ic * TOK + j * 128 + 128],
                              wchunked(lwq_sb, ic, 1024, oc * 512, 512))
                             for ic in range(8)]
                    mm_acc(cx, t, pqq[oc][:, 0:512], pairs, "prp")
                pairs = [(xng2f[:, ic * TOK + j * 128: ic * TOK + j * 128 + 128],
                          wchunked(lwk_sb, ic, 256, 0, 256)) for ic in range(8)]
                mm_acc(cx, t, pq2[:, 0:256], pairs, "prp")
                pairs = [(xng2f[:, ic * TOK + j * 128: ic * TOK + j * 128 + 128],
                          wchunked(lwv_sb, ic, 256, 0, 256)) for ic in range(8)]
                mm_acc(cx, t, pq3[:, 0:256], pairs, "prp")

            @block.vector
            def _(v, j=j):
                cx.w(v, "prp", cx.cnt["prp"])
                cx.w(v, "lrp", cx.cnt["lrp"])
                if j >= 1:
                    cx.w(v, "stqkv", cx.cnt["stqkv"])
                v.tensor_copy(qtw[:, 0:512], pqq[0][:, 0:512])
                v.tensor_copy(qtw[:, 512:1024], pqq[1][:, 0:512])
                v.tensor_copy(ktw[:], pq2[:, 0:256])
                cx.inc(v.tensor_copy(vtb[:], pq3[:, 0:256]), "pre", 1)
                emit_rope(v, qtn, qtw, ropeb[:, 0:512], ropeb[:, 512:1024],
                          16, rA, rB)
                emit_rope(v, ktn, ktw, ropebk[:, 0:128], ropebk[:, 128:256],
                          4, rA, rB)
                v.tensor_mul(hsqf[:], qtn[:], qtn[:])
                cx.inc(v.tensor_reduce(
                    hst[:, 0:16], hsqf[:].rearrange("p (h d) -> p h d", d=64),
                    AX.X, ALU.add), "rpd", 1)
                v.tensor_mul(hsqf[:, 0:256], ktn[:], ktn[:])
                cx.inc(v.tensor_reduce(
                    hst[:, 16:20],
                    hsqf[:, 0:256].rearrange("p (h d) -> p h d", d=64),
                    AX.X, ALU.add), "rpd", 1)

            @block.scalar
            def _(s, j=j):
                cx.w(s, "rpd", cx.cnt["rpd"])
                s.activation(out=hst[:, 20:40], in_=hst[:, 0:20], func=AF.Ln,
                             scale=1.0 / 64, bias=eps6[:, 1:2])
                cx.inc(s.activation(out=hst[:, 20:40], in_=hst[:, 20:40],
                                    func=AF.Exp, scale=-0.5), "rpa", 1)

            @block.vector
            def _(v, j=j):
                cx.w(v, "rpa", cx.cnt["rpa"])
                for h in range(16):
                    v.tensor_scalar(qrb[:, h * 64:(h + 1) * 64],
                                    qtn[:, h * 64:(h + 1) * 64],
                                    hst[:, 20 + h: 21 + h], None, ALU.mult)
                for h in range(4):
                    v.tensor_scalar(krb[:, h * 64:(h + 1) * 64],
                                    ktn[:, h * 64:(h + 1) * 64],
                                    hst[:, 36 + h: 37 + h], None, ALU.mult)
                cx.inc(v.tensor_copy(krb[:, 0:1], krb[:, 0:1]), "rpd", 1)

            @block.sync
            def _(s, j=j):
                cx.w(s, "rpd", cx.cnt["rpd"])
                cx.w(s, "pre", cx.cnt["pre"])
                r0 = j * 128
                cx.inc(s.dma_start(out=o_q2r[r0:r0 + 128, :], in_=qrb[:]), "stqkv", 16)
                cx.inc(s.dma_start(out=o_k2r[r0:r0 + 128, :], in_=krb[:]), "stqkv", 16)
                cx.inc(s.dma_start(out=o_v2[r0:r0 + 128, :], in_=vtb[:]), "stqkv", 16)

        @block.sync
        def _(s):
            cx.w(s, "stqkv", cx.cnt["stqkv"])
            cx.w(s, "ust", cx.cnt["ust"])
    es3.close()

    cx.close()
    return cx




def build_D(upto="full"):
    nslA = NSLOT
    phase = "all"
    if upto.startswith("attn:"):
        ps = upto.split(":")
        nslA = int(ps[1])
        phase = ps[2] if len(ps) > 2 else "all"
        upto = "attn"
    do_exp = phase == "all" or "e" in phase
    do_msk = phase == "all" or "m" in phase
    do_pv = phase == "all" or "p" in phase
    do_fin = phase == "all"
    cx = Cx("D")
    x1i = cx.din("x1i", [TOK, D], F32)
    ui = cx.din("ui", [TOK, D], F32)
    xn1fi = cx.din("xn1fi", [128, 8 * TOK], BF16)
    qh = [cx.din(f"qh{p}", [64, 4 * TOK], BF16) for p in range(4)]
    Kw = [cx.din(f"Kw{p}", [64, NSLOT * 3 * 128], BF16) for p in range(4)]
    Vw = [cx.din(f"Vw{p}", [128, NSLOT * 3 * 65], BF16) for p in range(4)]
    msklt = cx.din("msklt", [128, NSLOT * 3 * 128], BF16)
    l_wo = cx.din("l_wo", [1024, 1024], F32)
    gate_wT = cx.din("gate_wT", [128, 8 * 8], BF16)
    gate_b = cx.din("gate_b", [1, 8], F32)
    w1 = cx.din("w1", [8, 1024, 1024], F32)
    w2 = cx.din("w2", [8, 1024, 1024], F32)
    w3 = cx.din("w3", [8, 1024, 1024], F32)
    b1i = cx.din("b1i", [8, 128, 8], F32)
    b2i = cx.din("b2i", [8, 128, 8], F32)
    b3i = cx.din("b3i", [8, 1, 1024], F32)
    iota8 = cx.din("iota8", [128, 8], F32)
    cst = cx.din("cst", [128, 4], F32)
    identf = cx.din("identf", [128, 128], F32)
    identb = cx.din("identb", [128, 128], BF16)
    o_x2 = cx.dout("x2", [TOK, D], F32)
    yl_scr = cx.nc.dram_tensor("yl_scr", [TOK, D], F32, kind="Internal").ap()

    idf = cx.sb("idfD", [128, 128], F32)
    idb = cx.sb("idbD", [128, 128], BF16)
    io8 = cx.sb("io8", [128, 8], F32)
    cstb = cx.sb("cstb", [128, 4], F32)
    gwT = cx.sb("gwT", [128, 8 * 8], BF16)
    gbb = cx.sb("gbb", [128, 8], F32)
    gbr = cx.sb("gbr", [1, 8], F32)
    onesrf = cx.sb("onesrfD", [1, 128], F32)
    cwt = cx.sb("cwt", [128, 2 * 4 * 8], F32)
    block = cx.block()

    @block.sync
    def _(s):
        cx.inc(s.dma_start(out=idf[:], in_=identf), "ldc", 16)
        cx.inc(s.dma_start(out=idb[:], in_=identb), "ldc", 16)
        cx.inc(s.dma_start(out=io8[:], in_=iota8), "ldc", 16)
        cx.inc(s.dma_start(out=cstb[:], in_=cst), "ldc", 16)
        cx.inc(s.dma_start(out=gwT[:], in_=gate_wT), "ldc", 16)
        cx.inc(s.dma_start(out=gbr[:], in_=gate_b), "ldc", 16)

    @block.vector
    def _(v):
        cx.inc(v.memset(onesrf[:], 1.0), "ini", 1)

    with cx.nc.psum_tensor("psetD", [128, 512], F32) as pset:
        @block.tensor
        def _(t):
            cx.w(t, "ldc")
            cx.w(t, "ini", 1)
            cx.inc(t.matmul(pset[:, 0:8], onesrf[0:1, :], gbr[0:1, :],
                            start=True, stop=True), "bps", 1)

        @block.vector
        def _(v):
            cx.w(v, "bps", 1)
            cx.inc(v.tensor_copy(gbb[:], pset[:, 0:8]), "bse", 1)

    with (
        cx.nc.sbuf_tensor("KsbD", [64, 4 * NSLOT * 3 * 128], BF16) as Ksb,
        cx.nc.sbuf_tensor("VsbD", [128, 4 * NSLOT * 3 * 65], BF16) as Vsb,
        cx.nc.sbuf_tensor("qsbD", [64, 4 * 4 * TOK], BF16) as qsb,
        cx.nc.sbuf_tensor("mskDD", [128, NSLOT * 3 * 128], BF16) as msksb,
        cx.nc.sbuf_tensor("PTD", [128, 2 * 512], BF16) as PTb,
        cx.nc.sbuf_tensor("ysbDD", [128, 2048], F32) as ysb,
        cx.nc.sbuf_tensor("ytokDD", [128, 1024], F32) as ytok,
        cx.nc.sbuf_tensor("rltDD", [128, 2], F32) as rlt,
        cx.nc.psum_tensor("sdd0", [128, 512], F32) as sd0,
        cx.nc.psum_tensor("sdd1", [128, 512], F32) as sd1,
        cx.nc.psum_tensor("sdd2", [128, 512], F32) as sd2,
        cx.nc.psum_tensor("sdd3", [128, 512], F32) as sd3,
        cx.nc.psum_tensor("ydd0", [128, 512], F32) as yd0,
        cx.nc.psum_tensor("ydd1", [128, 512], F32) as yd1,
        cx.nc.psum_tensor("ydd2", [128, 512], F32) as yd2,
        cx.nc.psum_tensor("ydd3", [128, 512], F32) as yd3,
    ):
        sb4 = (sd0, sd1, sd2, sd3)
        ybk = (yd0, yd1, yd2, yd3)

        @block.sync
        def _(s):
            cx.inc(s.dma_start(out=msksb[:], in_=msklt), "ldm", 16)
            for p in range(4):
                cx.inc(s.dma_start(
                    out=Ksb[0:64, p * NSLOT * 384: (p + 1) * NSLOT * 384],
                    in_=Kw[p]), "ldK", 16)
                cx.inc(s.dma_start(
                    out=Vsb[:, p * NSLOT * 195: (p + 1) * NSLOT * 195],
                    in_=Vw[p]), "ldK", 16)
                cx.inc(s.dma_start(out=qsb[0:64, p * 4 * TOK: (p + 1) * 4 * TOK],
                                   in_=qh[p]), "ldK", 16)

        for j in range(nslA):
            sS_base = cx.cnt.get("dS", 0)
            dE_base = cx.cnt.get("dE", 0)
            dM_base = cx.cnt.get("dM", 0)
            dP_base = cx.cnt.get("dP", 0)
            dT_base = cx.cnt.get("dT", 0)
            dD_base = cx.cnt.get("dD", 0)
            dY_base = cx.cnt.get("dY", 0)
            q0 = j * 128

            @block.tensor
            def _(t, j=j, q0=q0, sS_base=sS_base, dE_base=dE_base,
                  dM_base=dM_base, dD_base=dD_base, dY_base=dY_base):
                if j == 0:
                    cx.w(t, "ldK", cx.cnt["ldK"])
                    cx.w(t, "ldm", 16)
                    cx.w(t, "bse", 1)
                elif do_fin:
                    cx.w(t, "dY", dY_base)
                    cx.w(t, "dD", dD_base)
                for p in range(4):
                    for kb in range(3):
                        for hl in range(4):
                            i = (p * 3 + kb) * 4 + hl
                            if sS_base + i >= 4 and do_exp:
                                cx.w(t, "dE", sS_base + i - 4 + 1)
                            ins = t.matmul(
                                sb4[(sS_base + i) % 4][:, 0:128],
                                Ksb[0:64, p * NSLOT * 384 + (j * 3 + kb) * 128:
                                    p * NSLOT * 384 + (j * 3 + kb) * 128 + 128],
                                qsb[0:64, p * 4 * TOK + hl * TOK + q0:
                                    p * 4 * TOK + hl * TOK + q0 + 128],
                                start=True, stop=True)
                            cx.inc(ins, "dS", 1)
                        for hl in (range(4) if do_pv else []):
                            i = (p * 3 + kb) * 4 + hl
                            cx.w(t, "dM", dM_base + i + 1)
                            first = (kb == 0 and hl == 0)
                            last = (kb == 2 and hl == 3)
                            ins = t.matmul(
                                ybk[p][0:65, hl * 128: hl * 128 + 128],
                                Vsb[:, p * NSLOT * 195 + (j * 3 + kb) * 65:
                                    p * NSLOT * 195 + (j * 3 + kb) * 65 + 65],
                                PTb[:, (i % 2) * 512 + hl * 128:
                                    (i % 2) * 512 + hl * 128 + 128],
                                start=first, stop=last)
                            cx.inc(ins, "dP", 1)

            @block.scalar
            def _(s, j=j, sS_base=sS_base, dP_base=dP_base):
                for p in (range(4) if do_exp else []):
                    for kb in range(3):
                        for hl in range(4):
                            i = (p * 3 + kb) * 4 + hl
                            if (dP_base > 0 or i >= 4) and do_pv:
                                cx.w(s, "dP", dP_base + max(i - 3, 0))
                            cx.w(s, "dS", sS_base + i + 1)
                            ins = s.activation(
                                out=PTb[:, (i % 2) * 512 + hl * 128:
                                        (i % 2) * 512 + hl * 128 + 128],
                                in_=sb4[(sS_base + i) % 4][:, 0:128],
                                func=AF.Exp, scale=0.125)
                            cx.inc(ins, "dE", 1)

            @block.vector
            def _(v, j=j, dE_base=dE_base):
                for p in (range(4) if do_msk else []):
                    for kb in range(3):
                        for hl in range(4):
                            i = (p * 3 + kb) * 4 + hl
                            cx.w(v, "dE", dE_base + i + 1)
                            off = (i % 2) * 512 + hl * 128
                            cx.inc(v.tensor_mul(
                                PTb[:, off: off + 128], PTb[:, off: off + 128],
                                msksb[:, (j * 3 + kb) * 128: (j * 3 + kb) * 128 + 128]),
                                "dM", 1)

            @block.vector
            def _(v, j=j, dP_base=dP_base):
                if not do_fin:
                    return
                cx.w(v, "dP", dP_base + 48)
                for p in range(4):
                    cx.inc(v.tensor_copy(ysb[0:65, p * 512:(p + 1) * 512],
                                         ybk[p][0:65, 0:512]), "dY", 1)

            @block.tensor
            def _(t, j=j, dY_base=dY_base, dT_base=dT_base, dD_base=dD_base):
                if not do_fin:
                    return
                cx.w(t, "dY", dY_base + 4)
                for h in range(16):
                    if dT_base + h >= 2:
                        cx.w(t, "dD", dT_base + h - 2 + 1)
                    cx.inc(t.transpose(sb4[h % 2][0:128, 0:65],
                                       ysb[0:65, h * 128: h * 128 + 128],
                                       idf[0:65, 0:65]), "dT", 1)

            @block.vector
            def _(v, j=j, dT_base=dT_base):
                if not do_fin:
                    return
                if j >= 1:
                    cx.w(v, "dst", 16 * j)
                for h in range(16):
                    cx.w(v, "dT", dT_base + h + 1)
                    cx.inc(v.reciprocal(rlt[:, 0:1], sb4[h % 2][:, 64:65]), "dre", 1)
                    cx.w(v, "dre", cx.cnt["dre"])
                    cx.inc(v.tensor_scalar(ytok[:, h * 64: h * 64 + 64],
                                           sb4[h % 2][:, 0:64], rlt[:, 0:1], None,
                                           ALU.mult), "dD", 1)

            @block.sync
            def _(s, j=j, dD_base=dD_base):
                if not do_fin:
                    return
                cx.w(s, "dD", dD_base + 16)
                cx.inc(s.dma_start(out=yl_scr[j * 128:(j + 1) * 128, :],
                                   in_=ytok[:]), "dst", 16)
                if upto == "attn":
                    cx.inc(s.dma_start(out=o_x2[j * 128:(j + 1) * 128, :],
                                       in_=ytok[:]), "dst", 16)

        @block.sync
        def _(s):
            cx.w(s, "dst", cx.cnt.get("dst", 0))

        @block.vector
        def _(v):
            cx.w(v, "dD", cx.cnt.get("dD", 0))
            cx.w(v, "dst", cx.cnt.get("dst", 0))
            cx.inc(v.memset(rlt[:, 0:1], 0.0), "ph2", 1)

    if upto == "attn":
        @block.sync
        def _(s):
            cx.w(s, "dst", cx.cnt.get("dst", 0))
            cx.w(s, "ldK", cx.cnt.get("ldK", 0))
            cx.w(s, "ldm", cx.cnt.get("ldm", 0))
            cx.w(s, "ldc", cx.cnt.get("ldc", 0))
        cx.close()
        return cx

    es2 = ExitStack()
    xn1sb = es2.enter_context(cx.nc.sbuf_tensor("xn1sb", [128, 8 * TOK], BF16))
    lwo_sb = es2.enter_context(cx.nc.sbuf_tensor("lwo_sb", [128, 8 * 1024], BF16))
    ylf = es2.enter_context(cx.nc.sbuf_tensor("ylf", [128, 8 * TOK], BF16))
    ytb = es2.enter_context(cx.nc.sbuf_tensor("ytbD", [128, 1024], BF16))
    mw0 = es2.enter_context(cx.nc.sbuf_tensor("mwD0", [128, 8 * 1024], BF16))
    mw1 = es2.enter_context(cx.nc.sbuf_tensor("mwD1", [128, 8 * 1024], BF16))
    h1f = es2.enter_context(cx.nc.sbuf_tensor("h1fD", [128, 2 * 8 * 512], BF16))
    h2f = es2.enter_context(cx.nc.sbuf_tensor("h2fD", [128, 2 * 8 * 512], BF16))
    facc = es2.enter_context(cx.nc.sbuf_tensor("faccD", [128, 2 * 4 * 1024], F32))
    b1b = es2.enter_context(cx.nc.sbuf_tensor("b1bD", [128, 8], F32))
    b2b = es2.enter_context(cx.nc.sbuf_tensor("b2bD", [128, 8], F32))
    b3r = es2.enter_context(cx.nc.sbuf_tensor("b3rD", [1, 1024], F32))
    b3b = es2.enter_context(cx.nc.sbuf_tensor("b3bD", [128, 1024], F32))
    gpro = es2.enter_context(cx.nc.sbuf_tensor("gproD", [128, 8], F32))
    gtmp = es2.enter_context(cx.nc.sbuf_tensor("gtmpD", [128, 8], F32))
    gst = es2.enter_context(cx.nc.sbuf_tensor("gstD", [128, 16], F32))
    t8 = es2.enter_context(cx.nc.sbuf_tensor("t8D", [128, 8], F32))
    I1 = es2.enter_context(cx.nc.sbuf_tensor("I1D", [128, 8], F32))
    I2 = es2.enter_context(cx.nc.sbuf_tensor("I2D", [128, 8], F32))
    pex = es2.enter_context(cx.nc.sbuf_tensor("pexD", [128, 8], F32))
    xt1 = es2.enter_context(cx.nc.sbuf_tensor("xt1D", [128, 1024], F32))
    xt2 = es2.enter_context(cx.nc.sbuf_tensor("xt2D", [128, 1024], F32))
    ppd0 = es2.enter_context(cx.nc.psum_tensor("ppd0", [128, 512], F32))
    ppd1 = es2.enter_context(cx.nc.psum_tensor("ppd1", [128, 512], F32))
    ptd0 = es2.enter_context(cx.nc.psum_tensor("ptd0", [128, 1024], BF16))
    ptd1 = es2.enter_context(cx.nc.psum_tensor("ptd1", [128, 1024], BF16))
    pg = es2.enter_context(cx.nc.psum_tensor("pgD", [128, 512], F32))
    if True:
        ppd = (ppd0, ppd1)
        ptd = (ptd0, ptd1)
        mws = (mw0, mw1)
        stage_done = []
        stage_no = [0]

        @block.gpsimd
        def _(g):
            cx.w(g, "ph2", 1)
            cx.inc(g.dma_start(out=lwo_sb[:].rearrange("p (i c) -> p i c", c=1024),
                               in_=l_wo.rearrange("(i p) c -> p i c", p=128)),
                   "lwo", 16)

        @block.sync
        def _(s):
            cx.w(s, "dst", cx.cnt["dst"])
            cx.inc(s.dma_start(out=xn1sb[:].rearrange("p (i t) -> p i t", t=TOK),
                               in_=xn1fi.rearrange("p (i t) -> p i t", t=TOK)),
                   "lxn", 16)

        for j in range(NSLOT):
            @block.gpsimd
            def _(g, j=j):
                if j >= 1:
                    cx.w(g, "dtt", 8 * j)
                cx.inc(g.dma_start(out=ytb[:],
                                   in_=yl_scr[j * 128:(j + 1) * 128, :]), "lyt", 16)

            @block.tensor
            def _(t, j=j):
                cx.w(t, "lyt", (j + 1) * 16)
                for ci in range(8):
                    gi = j * 8 + ci
                    if gi >= 2:
                        cx.w(t, "dte", gi - 1)
                    cx.inc(t.transpose(ptd[ci % 2][:, 0:128],
                                       ytb[:, ci * 128:(ci + 1) * 128],
                                       idb[:]), "dtt", 1)

            @block.vector
            def _(v, j=j):
                for ci in range(8):
                    cx.w(v, "dtt", j * 8 + ci + 1)
                    cx.inc(v.tensor_copy(
                        ylf[:, ci * TOK + j * 128: ci * TOK + (j + 1) * 128],
                        ptd[ci % 2][:, 0:128]), "dte", 1)

        for tp in range(2):
            for tl in range(2):
                tb = (tp * 2 + tl) * 512

                @block.tensor
                def _(t, tb=tb):
                    cx.w(t, "lxn", 16)
                    cx.w(t, "bse", 1)
                    if cx.cnt.get("gte", 0) > 0:
                        cx.w(t, "gte", cx.cnt["gte"])
                    if cx.cnt.get("b3e", 0) > 0:
                        cx.w(t, "b3e", cx.cnt["b3e"])
                    for ts in range(4):
                        for ic in range(8):
                            ins = t.matmul(
                                pg[:, ts * 8: ts * 8 + 8],
                                xn1sb[:, ic * TOK + tb + ts * 128:
                                      ic * TOK + tb + ts * 128 + 128],
                                gwT[:, ic * 8:(ic + 1) * 8],
                                start=(ts == 0 and ic == 0),
                                stop=(ts == 3 and ic == 7))
                            if ts == 3 and ic == 7:
                                cx.inc(ins, "gtp", 1)

                for ts in range(4):
                    @block.vector
                    def _(v, ts=ts):
                        cx.w(v, "gtp", cx.cnt["gtp"])
                        cx.w(v, "bse", 1)
                        cx.inc(v.tensor_add(gtmp[:], pg[:, ts * 8: ts * 8 + 8],
                                            gbb[:]), "gtd", 1)

                    @block.scalar
                    def _(s, ts=ts):
                        cx.w(s, "gtd", cx.cnt["gtd"])
                        cx.inc(s.activation(out=gpro[:], in_=gtmp[:], func=AF.Exp,
                                            accum_out=gst[:, 0:1]), "gta", 1)
                        cx.inc(s.activation(out=gst[:, 4:5], in_=gst[:, 0:1],
                                            func=AF.Copy), "gta", 1)

                    @block.vector
                    def _(v, tl=tl, ts=ts):
                        cx.w(v, "gta", cx.cnt["gta"])
                        cx.inc(v.reciprocal(gst[:, 8:9], gst[:, 4:5]), "gtd", 1)
                        cx.w(v, "gtd", cx.cnt["gtd"])
                        cx.inc(v.tensor_scalar(gpro[:], gpro[:], gst[:, 8:9], None,
                                               ALU.mult), "gtd", 1)
                        cx.w(v, "gtd", cx.cnt["gtd"])
                        cx.inc(v.reduce_max(gst[:, 12:13], gpro[:], AX.X), "gtd", 1)
                        cx.w(v, "gtd", cx.cnt["gtd"])
                        v.tensor_scalar(t8[:], gpro[:], gst[:, 12:13], None,
                                        ALU.is_ge)
                        v.scalar_tensor_tensor(t8[:], t8[:], cstb[:, 0:1], io8[:],
                                               ALU.mult, ALU.add)
                        cx.inc(v.tensor_scalar(t8[:], t8[:], cstb[:, 1:2], None,
                                               ALU.add), "gtd", 1)
                        cx.w(v, "gtd", cx.cnt["gtd"])
                        cx.inc(v.tensor_reduce(gst[:, 13:14], t8[:], AX.X, ALU.min),
                               "gtd", 1)
                        cx.w(v, "gtd", cx.cnt["gtd"])
                        v.tensor_scalar(I1[:], io8[:], gst[:, 13:14], None,
                                        ALU.is_equal)
                        cx.inc(v.scalar_tensor_tensor(pex[:], I1[:], cstb[:, 2:3],
                                                      gpro[:], ALU.mult, ALU.add),
                               "gtd", 1)
                        cx.w(v, "gtd", cx.cnt["gtd"])
                        cx.inc(v.reduce_max(gst[:, 14:15], pex[:], AX.X), "gtd", 1)
                        cx.w(v, "gtd", cx.cnt["gtd"])
                        v.tensor_scalar(t8[:], pex[:], gst[:, 14:15], None,
                                        ALU.is_ge)
                        v.scalar_tensor_tensor(t8[:], t8[:], cstb[:, 0:1], io8[:],
                                               ALU.mult, ALU.add)
                        cx.inc(v.tensor_scalar(t8[:], t8[:], cstb[:, 1:2], None,
                                               ALU.add), "gtd", 1)
                        cx.w(v, "gtd", cx.cnt["gtd"])
                        cx.inc(v.tensor_reduce(gst[:, 15:16], t8[:], AX.X, ALU.min),
                               "gtd", 1)
                        cx.w(v, "gtd", cx.cnt["gtd"])
                        v.tensor_scalar(I2[:], io8[:], gst[:, 15:16], None,
                                        ALU.is_equal)
                        v.tensor_scalar(gtmp[:], I1[:], gpro[:, 0:1], None,
                                        ALU.mult)
                        cx.inc(v.scalar_tensor_tensor(
                            cwt[:, tl * 32 + ts * 8: tl * 32 + (ts + 1) * 8],
                            I2[:], gpro[:, 1:2], gtmp[:], ALU.mult, ALU.add),
                            "gte", 1)

            for e in (range(8) if upto == "full" else []):
                st1 = stage_no[0]
                stage_no[0] += 1

                @block.gpsimd
                def _(g, e=e, st=st1):
                    if st >= 2:
                        sn, sv = stage_done[st - 2]
                        cx.w(g, sn, sv)
                    cx.inc(g.dma_start(
                        out=mws[st % 2][:].rearrange("p (i c) -> p i c", c=1024),
                        in_=w1[e].rearrange("(i p) c -> p i c", p=128)), "lmw", 16)
                    cx.inc(g.dma_start(out=b1b[:], in_=b1i[e]), "lmw", 16)

                e1a_base = cx.cnt.get("e1a", 0)

                @block.tensor
                def _(t, tp=tp, e=e):
                    cx.w(t, "lmw", cx.cnt["lmw"])
                    if cx.cnt.get("f3d", 0) > 0:
                        cx.w(t, "f3d", cx.cnt["f3d"])
                    if cx.cnt.get("x2d", 0) > 0:
                        cx.w(t, "x2d", cx.cnt["x2d"])
                    for ji in range(16):
                        tl, oc = divmod(ji, 8)
                        tb = (tp * 2 + tl) * 512
                        if ji >= 2:
                            cx.w(t, "e1a", e1a_base + ji - 1)
                        elif cx.cnt.get("e1a", 0) > 0:
                            cx.w(t, "e1a", e1a_base)
                        pairs = [(wchunked(mws[st1 % 2], ic, 1024, oc * 128, 128),
                                  xn1sb[:, ic * TOK + tb: ic * TOK + tb + 512])
                                 for ic in range(8)]
                        mm_acc(cx, t, ppd[ji % 2][:, 0:512], pairs, "e1p")
                stage_done.append(("e1p", cx.cnt["e1p"]))

                @block.scalar
                def _(s, e=e):
                    for ji in range(16):
                        tl, oc = divmod(ji, 8)
                        cx.w(s, "e1p", cx.cnt["e1p"] - 15 + ji)
                        cx.inc(s.activation(
                            out=h1f[:, tl * 4096 + oc * 512:
                                    tl * 4096 + oc * 512 + 512],
                            in_=ppd[ji % 2][:, 0:512],
                            func=AF.Relu, bias=b1b[:, oc:oc + 1]), "e1a", 1)

                st2 = stage_no[0]
                stage_no[0] += 1

                @block.gpsimd
                def _(g, e=e, st=st2):
                    if st >= 2:
                        sn, sv = stage_done[st - 2]
                        cx.w(g, sn, sv)
                    cx.inc(g.dma_start(
                        out=mws[st % 2][:].rearrange("p (i c) -> p i c", c=1024),
                        in_=w2[e].rearrange("(i p) c -> p i c", p=128)), "lm2", 16)
                    cx.inc(g.dma_start(out=b2b[:], in_=b2i[e]), "lm2", 16)

                e2a_base = cx.cnt.get("e2a", 0)

                @block.tensor
                def _(t, e=e):
                    cx.w(t, "lm2", cx.cnt["lm2"])
                    cx.w(t, "e1a", cx.cnt["e1a"])
                    for ji in range(16):
                        tl, oc = divmod(ji, 8)
                        if ji >= 2:
                            cx.w(t, "e2a", e2a_base + ji - 1)
                        elif cx.cnt.get("e2a", 0) > 0:
                            cx.w(t, "e2a", e2a_base)
                        pairs = [(wchunked(mws[st2 % 2], ic, 1024, oc * 128, 128),
                                  h1f[:, tl * 4096 + ic * 512:
                                      tl * 4096 + ic * 512 + 512])
                                 for ic in range(8)]
                        mm_acc(cx, t, ppd[ji % 2][:, 0:512], pairs, "e2p")
                stage_done.append(("e2p", cx.cnt["e2p"]))

                @block.scalar
                def _(s, e=e):
                    for ji in range(16):
                        tl, oc = divmod(ji, 8)
                        cx.w(s, "e2p", cx.cnt["e2p"] - 15 + ji)
                        cx.inc(s.activation(
                            out=h2f[:, tl * 4096 + oc * 512:
                                    tl * 4096 + oc * 512 + 512],
                            in_=ppd[ji % 2][:, 0:512],
                            func=AF.Relu, bias=b2b[:, oc:oc + 1]), "e2a", 1)

                st3 = stage_no[0]
                stage_no[0] += 1

                @block.gpsimd
                def _(g, e=e, st=st3):
                    if st >= 2:
                        sn, sv = stage_done[st - 2]
                        cx.w(g, sn, sv)
                    cx.inc(g.dma_start(
                        out=mws[st % 2][:].rearrange("p (i c) -> p i c", c=1024),
                        in_=w3[e].rearrange("(i p) c -> p i c", p=128)), "lm3", 16)
                    cx.inc(g.dma_start(out=b3r[:], in_=b3i[e]), "lm3", 16)

                b3e_base = cx.cnt.get("b3e", 0)
                for hf in range(2):
                    @block.tensor
                    def _(t, e=e, hf=hf):
                        if hf == 0:
                            cx.w(t, "lm3", cx.cnt["lm3"])
                            cx.w(t, "gte", cx.cnt["gte"])
                            if b3e_base > 0:
                                cx.w(t, "b3e", b3e_base)
                        else:
                            cx.w(t, "b3e", cx.cnt["b3e"])
                        cx.inc(t.matmul(pg[:, 0:512], onesrf[0:1, :],
                                        b3r[0:1, hf * 512:(hf + 1) * 512],
                                        start=True, stop=True), "b3p", 1)

                    @block.vector
                    def _(v, e=e, hf=hf):
                        cx.w(v, "b3p", cx.cnt["b3p"])
                        cx.inc(v.tensor_copy(b3b[:, hf * 512:(hf + 1) * 512],
                                             pg[:, 0:512]), "b3e", 1)

                e3p_base = cx.cnt.get("e3p", 0)
                f3d_base = cx.cnt.get("f3d", 0)
                jobs3 = [(tl, ts, half) for tl in range(2)
                         for ts in range(4) for half in range(2)]

                @block.tensor
                def _(t, e=e):
                    cx.w(t, "e2a", cx.cnt["e2a"])
                    for ji, (tl, ts, half) in enumerate(jobs3):
                        if ji >= 2:
                            cx.w(t, "f3d", f3d_base + ji - 1)
                        pairs = [(h2f[:, tl * 4096 + ic * 512 + ts * 128:
                                      tl * 4096 + ic * 512 + ts * 128 + 128],
                                  wchunked(mws[st3 % 2], ic, 1024, half * 512, 512))
                                 for ic in range(8)]
                        mm_acc(cx, t, ppd[ji % 2][:, 0:512], pairs, "e3p")
                stage_done.append(("e3p", cx.cnt["e3p"]))

                @block.vector
                def _(v, e=e):
                    if e == 0 and cx.cnt.get("x2d", 0) > 0:
                        cx.w(v, "x2d", cx.cnt["x2d"])
                    cx.w(v, "b3e", cx.cnt["b3e"])
                    cx.w(v, "gte", cx.cnt["gte"])
                    for ji, (tl, ts, half) in enumerate(jobs3):
                        cx.w(v, "e3p", e3p_base + ji + 1)
                        fs = facc[:, tl * 4096 + ts * 1024 + half * 512:
                                  tl * 4096 + ts * 1024 + half * 512 + 512]
                        cwap = cwt[:, tl * 32 + ts * 8 + e:
                                   tl * 32 + ts * 8 + e + 1]
                        if e == 0:
                            v.tensor_scalar(fs, ppd[ji % 2][:, 0:512], cwap, None,
                                            ALU.mult)
                        else:
                            v.scalar_tensor_tensor(fs, ppd[ji % 2][:, 0:512], cwap,
                                                   fs, ALU.mult, ALU.add)
                        cx.inc(v.scalar_tensor_tensor(
                            fs, b3b[:, half * 512:(half + 1) * 512], cwap,
                            fs, ALU.mult, ALU.add), "f3d", 1)

            for fi in (range(8) if upto == "full" else []):
                tl, ts = divmod(fi, 4)
                tt = tp * 2 + tl
                tb = tt * 512

                @block.sync
                def _(s, tt=tt, ts=ts):
                    if cx.cnt.get("x2st", 0) > 0:
                        cx.w(s, "x2st", cx.cnt["x2st"])
                    r0 = tt * 512 + ts * 128
                    cx.inc(s.dma_start(out=xt1[:], in_=x1i[r0:r0 + 128, :]),
                           "lxx", 16)
                    cx.inc(s.dma_start(out=xt2[:], in_=ui[r0:r0 + 128, :]),
                           "lxx", 16)

                @block.tensor
                def _(t, fi=fi, tb=tb, ts=ts):
                    if fi == 0:
                        cx.w(t, "dte", NSLOT * 8)
                        cx.w(t, "lwo", 16)
                        cx.w(t, "f3d", cx.cnt["f3d"])
                    else:
                        cx.w(t, "x2d", cx.cnt["x2d"])
                    for half in range(2):
                        pairs = [(ylf[:, ic * TOK + tb + ts * 128:
                                      ic * TOK + tb + ts * 128 + 128],
                                  wchunked(lwo_sb, ic, 1024, half * 512, 512))
                                 for ic in range(8)]
                        mm_acc(cx, t, ppd[half][:, 0:512], pairs, "lop")

                @block.vector
                def _(v, tl=tl, ts=ts):
                    cx.w(v, "lop", cx.cnt["lop"])
                    cx.w(v, "lxx", cx.cnt["lxx"])
                    v.tensor_add(xt1[:], xt1[:], xt2[:])
                    v.tensor_add(xt1[:], xt1[:],
                                 facc[:, tl * 4096 + ts * 1024:
                                      tl * 4096 + ts * 1024 + 1024])
                    v.tensor_add(xt1[:, 0:512], xt1[:, 0:512], ppd[0][:, 0:512])
                    cx.inc(v.tensor_add(xt1[:, 512:1024], xt1[:, 512:1024],
                                        ppd[1][:, 0:512]), "x2d", 1)

                @block.sync
                def _(s, tt=tt, ts=ts):
                    cx.w(s, "x2d", cx.cnt["x2d"])
                    r0 = tt * 512 + ts * 128
                    cx.inc(s.dma_start(out=o_x2[r0:r0 + 128, :], in_=xt1[:]),
                           "x2st", 16)

        if upto == "gate":
            @block.sync
            def _(s):
                cx.w(s, "gte", cx.cnt["gte"])
                cx.inc(s.dma_start(out=o_x2[0:128, 0:64], in_=cwt[:]),
                       "x2st", 16)

        @block.sync
        def _(s):
            cx.w(s, "x2st", cx.cnt["x2st"])
    es2.close()

    cx.close()
    return cx


_NCA = None


def _get_A(upto="full"):
    global _NCA
    if _NCA is None or getattr(_NCA, "_upto", None) != upto:
        _NCA = build_A(upto)
        _NCA._upto = upto
    return _NCA


def _ident_bf16():
    return np.eye(128, dtype=BF)


def maps_A(inputs):
    x = np.asarray(inputs["L0_attn_x"])
    wdq = np.asarray(inputs["L0_attn_W_dq"])
    wuq = np.asarray(inputs["L0_attn_W_uq"])
    wdkv = np.asarray(inputs["L0_attn_W_dkv"])
    wuk = np.asarray(inputs["L0_attn_W_uk"])
    wuv = np.asarray(inputs["L0_attn_W_uv"])
    wo = np.asarray(inputs["L0_attn_W_o"])
    ident = _ident_bf16()
    in_maps = []
    for g in range(NCORE):
        b, c = g // GRP, g % GRP
        idx = tok_idx_of(c)
        in_maps.append({
            "x0": np.ascontiguousarray(x[b][idx]),
            "wdqT": np.ascontiguousarray(wdq.T),
            "wuqT": np.ascontiguousarray(wuq.T),
            "wdkvT": np.ascontiguousarray(wdkv.T),
            "wdq": wdq, "wuk": wuk, "wuv": wuv,
            "woT": np.ascontiguousarray(wo.T),
            "ident": ident,
            "epsA": np.full((128, 1), 1e-5, np.float32),
        })
    return in_maps


def run_A(inputs, upto="full"):
    cxA = _get_A(upto)
    in_maps = maps_A(inputs)
    res = run_bass_kernel_spmd(cxA.nc, in_maps, core_ids=list(range(NCORE)))
    return res.results




_NCB = None


def _get_B():
    global _NCB
    if _NCB is None:
        _NCB = build_B()
    return _NCB


def _rope_tables(pos, base, nh_rep):
    hd = 64
    inv_freq = (1.0 / (np.float32(base) **
                       (np.arange(0, hd, 2, dtype=np.float32) / np.float32(hd))))
    freqs = pos.astype(np.float32)[:, None] * inv_freq[None, :].astype(np.float32)
    cos = np.cos(freqs).astype(np.float32)
    sin = np.sin(freqs).astype(np.float32)
    return (np.ascontiguousarray(np.tile(cos, (1, nh_rep))),
            np.ascontiguousarray(np.tile(sin, (1, nh_rep))))


def _causal_masks(c):
    tl = tiles_of(c)
    out = np.zeros((NSLOT, 4, 128, 128), dtype=BF)
    kk = np.arange(128)[:, None]
    qq = np.arange(128)[None, :]
    for j in range(NSLOT):
        t = tl[j]
        m = j // 2
        kb0 = 8 * m + (0 if j % 2 == 0 else 4)
        for r in range(4):
            kb = kb0 + r
            out[j, r] = ((kb * 128 + kk) <= (t * 128 + qq)).astype(BF)
    return out.reshape(NSLOT * 4 * 128 // 128, -1) if False else \
        np.ascontiguousarray(out.transpose(2, 0, 1, 3).reshape(128, NSLOT * 4 * 128))


def _quarters_K(K_all):
    Kr = np.asarray(K_all).reshape(16, 64, 64, 128)
    out = []
    for p in range(4):
        a = Kr[4 * p:4 * p + 4]
        out.append(np.ascontiguousarray(
            a.transpose(1, 2, 0, 3).reshape(64, 64 * 4 * 128)))
    return out


def _quarters_V(V_all):
    Vr = np.asarray(V_all).reshape(64, 128, 16, 64)
    out = []
    for p in range(4):
        a = Vr[:, :, 4 * p:4 * p + 4, :]
        pad = np.zeros((64, 128, 4, 65), dtype=a.dtype)
        pad[:, :, :, 0:64] = a
        pad[:, :, :, 64] = 1.0
        out.append(np.ascontiguousarray(
            pad.transpose(1, 0, 2, 3).reshape(128, 64 * 260)))
    return out


def _quarters_q(q0f):
    qr = np.asarray(q0f).reshape(16, 64, TOK)
    return [np.ascontiguousarray(
        qr[4 * p:4 * p + 4].transpose(1, 0, 2).reshape(64, 4 * TOK))
        for p in range(4)]


def _fmajor_relayout(a):
    return np.ascontiguousarray(
        np.asarray(a).reshape(8, 128, TOK).transpose(1, 0, 2).reshape(128, 8 * TOK))


def maps_B(inputs, resA):
    x = np.asarray(inputs["L0_attn_x"])
    k0 = np.asarray(inputs["L0_ffn_k0"]).view(np.uint8)
    k1 = np.asarray(inputs["L0_ffn_k1"]).view(np.uint8)
    k2 = np.asarray(inputs["L0_ffn_k2"]).view(np.uint8)
    k0a = np.ascontiguousarray(
        k0[:1024].reshape(8, 128, 32, 128).transpose(2, 1, 0, 3).reshape(32, 128, 1024))
    k0b = np.ascontiguousarray(
        k0[1024:].reshape(8, 128, 32, 128).transpose(2, 1, 0, 3).reshape(32, 128, 1024))
    k1l = np.ascontiguousarray(
        k1.reshape(32, 128, 32, 128).transpose(2, 1, 0, 3).reshape(32, 128, 4096))
    k2l = np.ascontiguousarray(k2.reshape(32, 128, 1024))
    s012 = np.array([[inputs["L0_ffn_s0"], inputs["L0_ffn_s1"],
                      inputs["L0_ffn_s2"]]], dtype=np.float32)
    n1w = np.asarray(inputs["L1_attn_norm1_w"]).reshape(1, 1024).astype(np.float32)
    KV = {}
    for b in range(B):
        K_all = np.zeros((1024, T), dtype=BF)
        V_all = np.zeros((T, 1024), dtype=BF)
        for c in range(GRP):
            idx = tok_idx_of(c)
            g = b * GRP + c
            K_all[:, idx] = np.asarray(resA[g]["K0f"])
            V_all[idx, :] = np.asarray(resA[g]["V0t"])
        KV[b] = (_quarters_K(K_all), _quarters_V(V_all))
    ident = np.eye(128, dtype=BF)
    identf = np.eye(128, dtype=np.float32)
    onesb = np.ones((128, 1), dtype=BF)
    onesf = np.ones((1, 128), dtype=np.float32)
    in_maps = []
    for g in range(NCORE):
        b, c = g // GRP, g % GRP
        idx = tok_idx_of(c)
        qq = _quarters_q(resA[g]["q0f"])
        Kq, Vq = KV[b]
        cg512, sg512 = _rope_tables(idx, 1e6, 16)
        cg128, sg128 = _rope_tables(idx, 1e6, 4)
        m = {
            "x0": np.ascontiguousarray(x[b][idx]),
            "mskt": _causal_masks(c),
            "xn0fi": _fmajor_relayout(resA[g]["xn0f"]),
            "k0a_l": k0a, "k0b_l": k0b, "k1_l": k1l, "k2_l": k2l,
            "s012": s012, "n1w": n1w,
            "g_wq": np.asarray(inputs["L1_attn_g_wq"]),
            "g_kvd": np.asarray(inputs["L1_attn_g_kv_down"]),
            "g_ku": np.asarray(inputs["L1_attn_g_k_up"]),
            "g_vu": np.asarray(inputs["L1_attn_g_v_up"]),
            "cg512": cg512, "sg512": sg512, "cg128": cg128, "sg128": sg128,
            "identb": ident, "identf": identf, "onesb": onesb, "onesf": onesf,
            "epsB": np.concatenate([np.full((128, 1), 1e-5, np.float32),
                                    np.full((128, 1), 1e-6, np.float32)], 1),
        }
        for p in range(4):
            m[f"qh{p}"] = qq[p]
            m[f"Kh{p}"] = Kq[p]
            m[f"Vh{p}"] = Vq[p]
        in_maps.append(m)
    return in_maps


def run_B(inputs, resA):
    cxB = _get_B()
    in_maps = maps_B(inputs, resA)
    res = run_bass_kernel_spmd(cxB.nc, in_maps, core_ids=list(range(NCORE)))
    return res.results




_NCC = None


def _get_C():
    global _NCC
    if _NCC is None:
        _NCC = build_C()
    return _NCC


def _quarters_K_256(k_all):
    out = []
    for p in range(4):
        sub = np.asarray(k_all[:, 64 * p:64 * p + 64])
        a = sub.reshape(64, 128, 64).transpose(2, 0, 1)
        out.append(np.ascontiguousarray(a.reshape(64, NBLK * 128)))
    return out


def _quarters_V_256(v_all):
    out = []
    for p in range(4):
        sub = np.asarray(v_all[:, 64 * p:64 * p + 64]).reshape(64, 128, 64)
        pad = np.zeros((64, 128, 65), dtype=sub.dtype)
        pad[:, :, 0:64] = sub
        pad[:, :, 64] = 1.0
        out.append(np.ascontiguousarray(
            pad.transpose(1, 0, 2).reshape(128, NBLK * 65)))
    return out


def maps_C(inputs, resB):
    KV = {}
    for b in range(B):
        k_all = np.zeros((T, 256), dtype=BF)
        v_all = np.zeros((T, 256), dtype=BF)
        for c in range(GRP):
            idx = tok_idx_of(c)
            g = b * GRP + c
            k_all[idx] = np.asarray(resB[g]["k1r"]).astype(BF)
            v_all[idx] = np.asarray(resB[g]["v1"]).astype(BF)
        KV[b] = (_quarters_K_256(k_all), _quarters_V_256(v_all))
    identf = np.eye(128, dtype=np.float32)
    identb = np.eye(128, dtype=BF)
    in_maps = []
    for g in range(NCORE):
        b, c = g // GRP, g % GRP
        idx = tok_idx_of(c)
        qq = _quarters_q(np.asarray(resB[g]["q1r"]).astype(BF).T)
        Kq, Vq = KV[b]
        cl512, sl512 = _rope_tables(idx, 1e4, 16)
        cl128, sl128 = _rope_tables(idx, 1e4, 4)
        m = {
            "mskt": _causal_masks(c),
            "xn1i": np.asarray(resB[g]["xn1"]),
            "g_wo": np.asarray(inputs["L1_attn_g_wo"]),
            "l_wq": np.asarray(inputs["L1_attn_l_wq"]),
            "l_wk": np.asarray(inputs["L1_attn_l_wk"]),
            "l_wv": np.asarray(inputs["L1_attn_l_wv"]),
            "n2w": np.asarray(inputs["L1_attn_norm2_w"]).reshape(1, 1024).astype(np.float32),
            "cl512": cl512, "sl512": sl512, "cl128": cl128, "sl128": sl128,
            "identf": identf, "identb": identb,
            "epsC": np.concatenate([np.full((128, 1), 1e-5, np.float32),
                                    np.full((128, 1), 1e-6, np.float32)], 1),
        }
        for p in range(4):
            m[f"qh{p}"] = qq[p]
            m[f"Kh{p}"] = Kq[p]
            m[f"Vh{p}"] = Vq[p]
        in_maps.append(m)
    return in_maps


def run_C(inputs, resB):
    cxC = _get_C()
    res = run_bass_kernel_spmd(cxC.nc, maps_C(inputs, resB),
                               core_ids=list(range(NCORE)))
    return res.results


_NCD = None


def _get_D():
    global _NCD
    if _NCD is None:
        _NCD = build_D()
    return _NCD


def _window_KV(k_all, v_all, c):
    tl = tiles_of(c)
    Kw = [np.zeros((64, NSLOT * 3 * 128), dtype=BF) for _ in range(4)]
    Vw = [np.zeros((128, NSLOT * 3 * 65), dtype=BF) for _ in range(4)]
    for p in range(4):
        for j in range(NSLOT):
            t = tl[j]
            for kb in range(3):
                tk = t - 2 + kb
                col = j * 3 + kb
                Vw[p][:, col * 65 + 64] = 1.0
                if tk < 0:
                    continue
                blk = np.asarray(k_all[tk * 128:(tk + 1) * 128,
                                       64 * p:64 * p + 64])
                Kw[p][:, col * 128:(col + 1) * 128] = blk.T
                Vw[p][:, col * 65: col * 65 + 64] = \
                    v_all[tk * 128:(tk + 1) * 128, 64 * p:64 * p + 64]
    return Kw, Vw


def _local_masks(c):
    tl = tiles_of(c)
    out = np.zeros((128, NSLOT * 3 * 128), dtype=BF)
    kk = np.arange(128)[:, None]
    qq = np.arange(128)[None, :]
    for j in range(NSLOT):
        t = tl[j]
        for kb in range(3):
            tk = t - 2 + kb
            if tk < 0:
                continue
            diff = (t * 128 + qq) - (tk * 128 + kk)
            out[:, (j * 3 + kb) * 128:(j * 3 + kb + 1) * 128] = \
                ((diff >= 0) & (diff < 256)).astype(BF)
    return out


def maps_D(inputs, resB, resC):
    KV = {}
    for b in range(B):
        k_all = np.zeros((T, 256), dtype=BF)
        v_all = np.zeros((T, 256), dtype=BF)
        for c in range(GRP):
            idx = tok_idx_of(c)
            g = b * GRP + c
            k_all[idx] = np.asarray(resC[g]["k2r"]).astype(BF)
            v_all[idx] = np.asarray(resC[g]["v2"]).astype(BF)
        KV[b] = (k_all, v_all)
    gw = np.asarray(inputs["L1_ffn_gate_w"], np.float32)
    gate_wT = np.ascontiguousarray(
        gw.T.reshape(8, 128, 8).transpose(1, 0, 2).reshape(128, 64)).astype(BF)
    b1i = np.ascontiguousarray(np.asarray(inputs["L1_ffn_b1"], np.float32)
                               .reshape(8, 8, 128).transpose(0, 2, 1))
    b2i = np.ascontiguousarray(np.asarray(inputs["L1_ffn_b2"], np.float32)
                               .reshape(8, 8, 128).transpose(0, 2, 1))
    b3i = np.ascontiguousarray(
        np.asarray(inputs["L1_ffn_b3"], np.float32).reshape(8, 1, 1024))
    iota8 = np.tile(np.arange(8, dtype=np.float32)[None, :], (128, 1))
    cst = np.tile(np.array([[-65536.0, 65536.0, -2.0, 0.0]], np.float32),
                  (128, 1))
    identf = np.eye(128, dtype=np.float32)
    identb = np.eye(128, dtype=BF)
    w1 = np.asarray(inputs["L1_ffn_w1"], np.float32)
    w2 = np.asarray(inputs["L1_ffn_w2"], np.float32)
    w3 = np.asarray(inputs["L1_ffn_w3"], np.float32)
    in_maps = []
    for g in range(NCORE):
        b, c = g // GRP, g % GRP
        k_all, v_all = KV[b]
        Kw, Vw = _window_KV(k_all, v_all, c)
        qq4 = _quarters_q(np.asarray(resC[g]["q2r"]).astype(BF).T)
        m = {
            "x1i": np.asarray(resB[g]["x1"], np.float32),
            "ui": np.asarray(resC[g]["u"], np.float32),
            "xn1fi": _fmajor_relayout(np.asarray(resB[g]["xn1b"]).astype(BF).T),
            "msklt": _local_masks(c),
            "l_wo": np.asarray(inputs["L1_attn_l_wo"], np.float32),
            "gate_wT": gate_wT,
            "gate_b": np.asarray(inputs["L1_ffn_gate_b"],
                                 np.float32).reshape(1, 8),
            "w1": w1, "w2": w2, "w3": w3,
            "b1i": b1i, "b2i": b2i, "b3i": b3i,
            "iota8": iota8, "cst": cst,
            "identf": identf, "identb": identb,
        }
        for p in range(4):
            m[f"qh{p}"] = qq4[p]
            m[f"Kw{p}"] = Kw[p]
            m[f"Vw{p}"] = Vw[p]
        in_maps.append(m)
    return in_maps


def run_D(inputs, resB, resC):
    cxD = _get_D()
    res = run_bass_kernel_spmd(cxD.nc, maps_D(inputs, resB, resC),
                               core_ids=list(range(NCORE)))
    return res.results


def kernel(**inputs):
    resA = run_A(inputs)
    resB = run_B(inputs, resA)
    resC = run_C(inputs, resB)
    resD = run_D(inputs, resB, resC)
    out = np.zeros((B, T, D), np.float32)
    for g in range(NCORE):
        b, c = g // GRP, g % GRP
        out[b][tok_idx_of(c)] = np.asarray(resD[g]["x2"], np.float32)
    return out
